# version 08
# baseline (speedup 1.0000x reference)
import contextlib
import numpy as np
import ml_dtypes
import concourse.bass as bass
import concourse.mybir as mybir
from concourse.bass_utils import run_bass_kernel_spmd

F32 = mybir.dt.float32
BF16 = mybir.dt.bfloat16
AF = mybir.ActivationFunctionType
ALU = mybir.AluOpType
AX = mybir.AxisListType.X
BF = ml_dtypes.bfloat16

NC = 8
S = 8192
D = 4096
NBLK = 16
BLK = 512
TPC = 2 * BLK
NT = D // 128
NJ = BLK // 128

LN_EPS = 1e-5
DEN_EPS = 1e-6
PRIOR_EPS = 1e-12

DEINT = list(range(0, 64, 2)) + list(range(1, 64, 2))


def _bf(a):
    return np.ascontiguousarray(np.asarray(a, dtype=np.float32)).astype(BF)


def _f32(a):
    return np.ascontiguousarray(np.asarray(a, dtype=np.float32))


def _ogmajor(wt, og_w=512):
    K, N = wt.shape
    nog = N // og_w
    return np.ascontiguousarray(
        wt.reshape(K, nog, og_w).transpose(1, 0, 2).reshape(nog * K, og_w))


def _blocks_of_core(c):
    return (c, 15 - c)



def _prep(inp):
    shared = {}
    percore = [dict() for _ in range(NC)]

    x0 = _f32(inp["L0_attn_hidden_states"][0])

    sel_heads = list(range(0, 32, 4))
    Wq = _f32(inp["L0_attn_Wq"])
    rows = [r for h in sel_heads for r in range(128 * h, 128 * h + 128)]
    wqsel_t = Wq[rows].T
    shared["WQSELT"] = _bf(
        wqsel_t.reshape(D, 8, 128).transpose(1, 0, 2).reshape(8 * D, 128))
    bq_sel = _f32(inp["L0_attn_bq"])[rows]

    WoT = _f32(inp["L0_attn_Wo"]).T
    shared["WOT"] = _bf(_ogmajor(WoT))
    shared["M0"] = _f32(inp["L0_attn_M"])
    shared["Z0"] = _f32(inp["L0_attn_z"])

    act = np.asarray(inp["L0_ffn_active"]).astype(np.int64)
    act = np.where(act < 0, act + 16, act)
    act = np.clip(act, 0, 15)
    A = _f32(inp["L0_ffn_lora_A"])[act]
    Bm = _f32(inp["L0_ffn_lora_B"])[act]
    Rw = _f32(inp["L0_ffn_router_W"])[act]
    w_moe = np.concatenate([A.reshape(32, D), Rw], 0)
    shared["WMT"] = _bf(w_moe.T)
    shared["BMT"] = _bf(Bm.transpose(0, 2, 1).reshape(32, D))
    shared["RBSEL"] = _f32(inp["L0_ffn_router_b"])[act].reshape(1, 4)
    shared["PRSEL"] = _f32(inp["L0_ffn_domain_prior"])[act].reshape(1, 4)
    shared["WBT"] = _bf(_ogmajor(_f32(inp["L0_ffn_W_base"]).T))

    qperm = [64 * h + d for h in range(16) for d in DEINT]
    kperm = [1024 + 64 * h + d for h in range(4) for d in DEINT]
    vrows = list(range(1280, 1536))
    w_qkv_p = _f32(inp["L1_attn_w_qkv"])[qperm + kperm + vrows]
    shared["WQKVT"] = _bf(
        w_qkv_p.T.reshape(D, 12, 128).transpose(1, 0, 2).reshape(12 * D, 128))

    qc1 = _f32(inp["L1_attn_qc1_w"])
    qc1p = qc1[[64 * h + d for h in range(16) for d in DEINT]][:, DEINT, :]
    shared["QC1T"] = _bf(np.ascontiguousarray(
        qc1p.reshape(16, 64, 64, 3).transpose(3, 0, 2, 1)).reshape(3 * 16 * 64, 64))
    qc2 = _f32(inp["L1_attn_qc2_w"])
    qperm2 = [64 * h + d for h in range(16) for d in DEINT]
    qc2p = qc2[qperm2][:, qperm2, :]
    shared["QC2T"] = _bf(np.ascontiguousarray(
        qc2p.transpose(2, 1, 0).reshape(3, 1024, 8, 128).transpose(2, 0, 1, 3)
    ).reshape(8 * 3 * 1024, 128))
    kc1 = _f32(inp["L1_attn_kc1_w"])
    kperm1 = [64 * h + d for h in range(4) for d in DEINT]
    kc1p = kc1[kperm1][:, DEINT, :]
    shared["KC1T"] = _bf(np.ascontiguousarray(
        kc1p.reshape(4, 64, 64, 3).transpose(3, 0, 2, 1)).reshape(3 * 4 * 64, 64))
    kc2 = _f32(inp["L1_attn_kc2_w"])
    kc2p = kc2[kperm1][:, kperm1, :]
    shared["KC2T"] = _bf(np.ascontiguousarray(
        kc2p.transpose(2, 1, 0).reshape(3, 256, 2, 128).transpose(2, 0, 1, 3)
    ).reshape(2 * 3 * 256, 128))

    bias_vecs = [bq_sel,
                 _f32(inp["L1_attn_qc1_b"])[qperm2],
                 _f32(inp["L1_attn_qc2_b"])[qperm2],
                 _f32(inp["L1_attn_kc1_b"])[kperm1],
                 _f32(inp["L1_attn_kc2_b"])[kperm1]]
    cb = np.concatenate([v.reshape(-1, 128).T for v in bias_vecs], axis=1)
    shared["CBIAS"] = _f32(cb)
    shared["KEYT"] = _f32(inp["L1_attn_key_temp"]).reshape(1, 1)

    shared["WO2T"] = _bf(_ogmajor(_f32(inp["L1_attn_w_o"]).T))

    shared["KERT"] = _bf(_ogmajor(_f32(inp["L1_ffn_kernel"])))
    rowp = np.zeros((1, 128 + D), np.float32)
    rowp[0, 0:128] = 1.0
    rowp[0, 128:] = _f32(inp["L0_ffn_b_base"])
    shared["ROWP"] = _bf(rowp)
    b1r = np.zeros((1, 128 + D), np.float32)
    b1r[0, 0:128] = 1.0
    b1r[0, 128:] = _f32(inp["L1_ffn_bias"])
    shared["BIAS1R"] = _bf(b1r)
    shared["L1A"] = _bf(_f32(inp["L1_ffn_lora_A"]).reshape(512, D))
    shared["L1BT"] = _bf(
        _f32(inp["L1_ffn_lora_B"]).transpose(0, 2, 1).reshape(512, D))
    ast = _f32(inp["L1_ffn_lora_A_shared"]).T
    shared["L1ASTP"] = _bf(np.ascontiguousarray(
        ast.reshape(32, 128, 32).transpose(1, 0, 2)).reshape(128, 1024))
    shared["L1BST"] = _bf(_f32(inp["L1_ffn_lora_B_shared"]).T)
    gate = _f32(inp["L1_ffn_gate"])

    shared["IDENT"] = _bf(np.eye(128))
    shared["IDENT32"] = _f32(np.eye(128))
    mb = np.zeros((128, 896), np.float32)
    for p in range(128):
        mb[p, p + 384:] = 1.0
    shared["MBIG"] = _bf(mb)
    qms = np.zeros((128, 8 * 64), np.float32)
    kms = np.zeros((128, 2 * 64), np.float32)
    for p in range(128):
        for c in range(8):
            qms[p, 64 * c + (p % 64)] = 1.0 / 16.0
        for c in range(2):
            kms[p, 64 * c + (p % 64)] = 0.25
    shared["QMS"] = _bf(qms)
    shared["KMS"] = _bf(kms)
    hs = np.zeros((128, 8 * 16), np.float32)
    khs = np.zeros((128, 2 * 4), np.float32)
    for p in range(128):
        for c in range(8):
            hs[p, 16 * c + 2 * c + p // 64] = 1.0
        for c in range(2):
            khs[p, 4 * c + 2 * c + p // 64] = 1.0
    shared["HSQ"] = _bf(hs)
    shared["HSK"] = _bf(khs)
    exq = np.zeros((16, 8 * 128), np.float32)
    exk = np.zeros((4, 2 * 128), np.float32)
    for c in range(8):
        for p in range(128):
            exq[2 * c + p // 64, 128 * c + p] = 1.0
    for c in range(2):
        for p in range(128):
            exk[2 * c + p // 64, 128 * c + p] = 1.0
    shared["EXQ"] = _f32(exq)
    shared["EXK"] = _f32(exk)
    perm = np.zeros((128, 128), np.float32)
    for p in range(128):
        perm[64 * (p // 64) + (p % 64 + 32) % 64, p] = 1.0
    shared["PERMSW"] = _bf(perm)
    i32 = np.zeros((128, 32), np.float32)
    for p in range(128):
        i32[p, p % 32] = 1.0
    shared["I32S"] = _bf(i32)
    e4 = np.zeros((4, 128), np.float32)
    for p in range(128):
        e4[p // 32, p] = 1.0
    shared["E4128"] = _bf(e4)
    e432 = np.zeros((4, 32), np.float32)
    for m in range(32):
        e432[m // 8, m] = 1.0
    shared["E432"] = _f32(e432)
    shared["IOTA"] = _bf(np.broadcast_to(
        np.arange(512, dtype=np.float32), (128, 512)).copy())

    inv = (1.0 / 10000.0 ** (np.arange(0, 64, 2, dtype=np.float32) / 64.0))

    for c in range(NC):
        ga, gb = _blocks_of_core(c)
        toks = np.concatenate([np.arange(BLK * ga, BLK * ga + BLK),
                               np.arange(BLK * gb, BLK * gb + BLK)])
        percore[c]["X0"] = np.ascontiguousarray(x0[toks])
        ang = toks[None, :].astype(np.float32) * inv[:, None]
        ct = np.cos(ang)
        st = np.sin(ang)
        CT = np.concatenate([ct, ct, ct, ct], axis=0)
        ST = np.concatenate([-st, st, -st, st], axis=0)
        percore[c]["CT"] = _f32(CT)
        percore[c]["ST"] = _f32(ST)
        percore[c]["GATES"] = _f32(gate[[ga // 2, gb // 2]])
        r2 = np.zeros((64, 8), np.float32)
        for m, g in ((0, ga), (4, gb)):
            if g == 0:
                continue
            gp = g - 1
            o = gp if gp <= 7 else 15 - gp
            hf = 0 if gp <= 7 else 1
            for i in range(4):
                r2[8 * o + 4 * hf + i, m + i] = 1.0
        percore[c]["R2"] = _bf(r2)
        percore[c]["HMASK"] = _f32(np.array(
            [[0.0 if ga == 0 else 1.0, 0.0 if gb == 0 else 1.0]]))
        thr = np.zeros((128, 128), np.float32)
        for bslot, g in ((0, ga), (1, gb)):
            for kc in range(64):
                thr[:, 64 * bslot + kc] = (128 * kc +
                                           np.arange(128) - 512 * g)
        percore[c]["THR"] = thr

    return shared, percore



PARANOID = True


class Ctx:
    ENGS = ("sync", "tensor", "vector", "scalar", "gpsimd")

    def __init__(self, nc):
        self.nc = nc
        self.selfser = {"vector": "dve", "scalar": "act", "gpsimd": "pool"}
        self.ops = {e: [] for e in self.ENGS}
        self.es = contextlib.ExitStack()
        self.semh = {}
        self.cnt = {}
        for name in ("sy", "gp", "pe", "act", "dve", "pool"):
            self._addsem(name)
        self.hw = {e: {} for e in self.ENGS}
        self.dram = {}
        self.sb = {}
        self.scope = "init"
        self.scope_by_name = {}

    def _addsem(self, name):
        self.semh[name] = self.es.enter_context(self.nc.semaphore(name))
        self.cnt[name] = 0

    def sem(self, name):
        if name not in self.semh:
            self._addsem(name)
        return name

    def sbuf(self, name, shape, dtype):
        t = self.es.enter_context(self.nc.sbuf_tensor(name, shape, dtype))
        self.sb[name] = t
        return t

    def psum(self, name, shape, dtype):
        return self.es.enter_context(self.nc.psum_tensor(name, shape, dtype))

    def emit(self, eng, fn, waits=(), inc=None, inc_by=1):
        hw = self.hw[eng]
        if PARANOID and eng in self.selfser and inc is None:
            inc = self.selfser[eng]
        if PARANOID and eng in self.selfser:
            sname = self.selfser[eng]
            waits = list(waits) + [(sname, self.cnt[sname])]
        wl = []
        for s, v in waits:
            if v > hw.get(s, 0):
                wl.append((s, v))
                hw[s] = v
        ticket = None
        if inc is not None:
            self.cnt[inc] += inc_by
            ticket = self.cnt[inc]
        semh = self.semh

        nc = self.nc
        scope = self.scope

        sbn = self.scope_by_name

        def replay(q, wl=tuple(wl), fn=fn, inc=inc, inc_by=inc_by,
                   scope=scope):
            with nc.named_scope(scope):
                for s, v in wl:
                    q.wait_ge(semh[s], v)
                ins = fn(q)
                try:
                    sbn[ins.ins.name] = scope
                except Exception:
                    pass
                if inc is not None:
                    ins.then_inc(semh[inc], inc_by)
        self.ops[eng].append(replay)
        return ticket

    def raw(self, eng, fn):
        self.ops[eng].append(fn)

    def dma(self, eng, out, in_, waits=(), sem=None, **kw):
        semn = sem or {"sync": "sy", "gpsimd": "gp", "scalar": "act_dma"}[eng]
        self.sem(semn)
        return self.emit(
            eng, lambda q: q.dma_start(out=out, in_=in_, **kw),
            waits=waits, inc=semn, inc_by=16)

    def finalize(self, final_waits=None):
        nc = self.nc
        if final_waits is None:
            final_waits = [(s, v) for s, v in self.cnt.items() if v > 0]
        with nc.Block() as block:
            for eng in self.ENGS:
                ops = self.ops[eng]
                if not ops and eng != "sync":
                    continue
                deco = getattr(block, eng)
                if eng == "sync":
                    fw = list(final_waits)
                    semh = self.semh

                    def body_sync(q, ops=ops, fw=fw):
                        for f in ops:
                            f(q)
                        for s, v in fw:
                            q.wait_ge(semh[s], v)
                    deco(body_sync)
                else:
                    def body(q, ops=ops):
                        for f in ops:
                            f(q)
                    deco(body)


class Banks:

    def __init__(self, ctx):
        self.ctx = ctx
        self.p01 = ctx.psum("ps01", [128, 1024], F32)
        self.p23 = ctx.psum("ps23", [128, 1024], F32)
        self.p45 = ctx.psum("ps45", [128, 1024], F32)
        self.p67 = ctx.psum("pst67", [128, 2048], BF16)
        self.rd = {i: [] for i in range(8)}

    def ap(self, i):
        if i >= 6:
            return self.p67[:, 1024 * (i - 6):1024 * (i - 6) + 1024]
        pair = (self.p01, self.p23, self.p45)[i // 2]
        return pair[:, 512 * (i % 2):512 * (i % 2) + 512]

    def pair_ap(self, lo):
        return (self.p01, self.p23, self.p45)[lo // 2][:, :]

    def write_waits(self, i):
        w = self.rd[i]
        self.rd[i] = []
        return w

    def read_done(self, i, ticket):
        self.rd[i].append(ticket)


class Stream:

    def __init__(self, ctx, name, shape, dtype, nbuf=2):
        self.ctx = ctx
        self.bufs = [ctx.sbuf(f"{name}{i}", shape, dtype) for i in range(nbuf)]
        self.nbuf = nbuf
        self.i = 0
        self.cons = [[] for _ in range(nbuf)]
        self.load_t = [None] * nbuf

    def load(self, src_ap):
        b = self.i % self.nbuf
        self.i += 1
        t = self.ctx.dma("sync", self.bufs[b][:, :], src_ap,
                         waits=self.cons[b])
        self.cons[b] = []
        self.load_t[b] = t
        return b

    def buf(self, b):
        return self.bufs[b]

    def wait(self, b):
        return ("sy", self.load_t[b])

    def done(self, b, ticket):
        self.cons[b].append(ticket)





CONST_KEYS = [
    "IDENT", "IDENT32", "I32S", "E4128", "E432", "CBIAS", "ROWP",
]
PERCORE_SB = ["R2"]


class Kern:
    def __init__(self, nc, shared, percore0, fused=True):
        self.nc = nc
        self.fused = fused
        self.c = Ctx(nc)
        self.bk = Banks(self.c)
        self.shapes = {}
        for k, v in shared.items():
            self.shapes[k] = v
        for k, v in percore0.items():
            self.shapes[k] = v
        self.in_keys = set()
        self.dram_in = {}
        c = self.c
        self.XT = [c.sbuf("xt0", [128, D], F32), c.sbuf("xt1", [128, D], F32)]
        self.XLN = c.sbuf("xlnb", [128, D], BF16)
        self.XLT = c.sbuf("xltb", [128, NT * BLK], BF16)
        self.O = c.sbuf("obuf", [128, NJ * D], BF16)
        self.SCR = c.sbuf("scrA", [128, 512], F32)
        self.SCR2 = c.sbuf("scrB", [128, 512], F32)
        self.SS = c.sbuf("sscol", [128, 160], F32)
        self.EPSC = c.sbuf("epsc", [128, 4], F32)
        self.ONESW = c.sbuf("onesw", [1, 512], BF16)
        self.ONESF = c.sbuf("onesf", [1, 128], F32)
        self.RW = [c.sbuf("rwA", [128, 8 * 512], BF16),
                   c.sbuf("rwB", [128, 8 * 512], BF16)]
        self.MWB = [c.sbuf("mwA", [128, NT * 128], BF16),
                    c.sbuf("mwB", [128, NT * 128], BF16)]
        self.rwi, self.rw_load, self.rw_cons = 0, [None, None], [[], []]
        self.mwi, self.mw_load, self.mw_cons = 0, [None, None], [[], []]
        self.SB = {}
        self.FCOL = c.sbuf("fcol", [1, 4], F32)
        c.emit("vector", lambda q: q.memset(self.FCOL[:, :], 0.0), inc="dve")
        self.xt_free = {0: [], 1: []}
        self.xo_free = []
        self.xln_readers = []
        self.qkv_area_free = []
        self.xlt_readers = []
        self.o_readers = []
        self.ep_parity = 0

    def act_fence(self):
        return self.c.emit("scalar", lambda q: q.activation(
            out=self.FCOL[0:1, 0:1], in_=self.FCOL[0:1, 3:4], func=AF.Copy),
            inc="act")

    def dve_fence(self):
        return self.c.emit("vector", lambda q: q.tensor_copy(
            out=self.FCOL[0:1, 1:2], in_=self.FCOL[0:1, 2:3]), inc="dve")

    def pe_fence(self, bank=7):
        ident, ident_t = self.SB["IDENT"]
        fw = self.bk.write_waits(bank)
        t = self.c.emit("tensor", lambda q, ident=ident, bank=bank:
                        q.transpose(
            self.bk.ap(bank)[:, 0:128], ident[:, :], ident[:, :]),
            waits=fw + [ident_t], inc="pe")
        self.bk.read_done(bank, ("pe", t))
        return t

    def din(self, key, shape=None, dtype=None):
        if key in self.dram_in:
            return self.dram_in[key]
        if shape is None:
            arr = self.shapes[key]
            shape = list(arr.shape)
            dtype = BF16 if arr.dtype == BF else F32
        t = self.nc.dram_tensor("i_" + key.lower(), shape, dtype,
                                kind="ExternalInput")
        self.in_keys.add(key)
        self.dram_in[key] = t.ap()
        return self.dram_in[key]

    def dout(self, key, shape, dtype):
        return self.nc.dram_tensor("o_" + key.lower(), shape, dtype,
                                   kind="ExternalOutput").ap()

    def dint(self, key, shape, dtype, **kw):
        return self.nc.dram_tensor("t_" + key.lower(), shape, dtype, **kw).ap()

    def dbg(self, name, sb_ap):
        c = self.c
        d = self.dout("dbg_" + name, list(sb_ap.shape), sb_ap.dtype)
        w = [(k, v) for k, v in c.cnt.items() if v > 0 and k in
             ("pe", "act", "dve", "pool")]
        c.dma("sync", d, sb_ap, waits=w, sem="sdbg")

    def rw_next(self, src_ap, nch):
        b = self.rwi % 2
        self.rwi += 1
        half = nch // 2
        src3 = src_ap.rearrange("(c p) n -> p c n", p=128)
        w = self.rw_cons[b]
        self.c.dma("sync", self.RW[b][:, :half * 512], src3[:, :half, :],
                   waits=w, sem=f"srw{b}")
        t = self.c.dma("scalar", self.RW[b][:, half * 512:nch * 512],
                       src3[:, half:nch, :], waits=w, sem=f"srw{b}")
        self.rw_cons[b] = []
        self.rw_load[b] = (f"srw{b}", t)
        return b

    def mw_next(self, src_ap, nch):
        b = self.mwi % 2
        self.mwi += 1
        t = self.c.dma("sync", self.MWB[b][:, :nch * 128],
                       src_ap.rearrange("(c p) n -> p c n", p=128),
                       waits=self.mw_cons[b], sem=f"smw{b}")
        self.mw_cons[b] = []
        self.mw_load[b] = (f"smw{b}", t)
        return b

    def preamble(self, keys=None):
        c = self.c
        if keys is None:
            keys = CONST_KEYS + PERCORE_SB
        for k in keys:
            arr = self.shapes[k]
            dt = BF16 if arr.dtype == BF else F32
            sb = c.sbuf("k_" + k.lower(), list(arr.shape), dt)
            c.dma("sync", sb[:, :], self.din(k), sem="cst")
            self.SB[k] = sb
        cst_total = c.cnt["cst"]
        cw = ("cst", cst_total)
        self.SB = {k: (v, cw) for k, v in self.SB.items()}
        self.cst_w = cw

        c.emit("vector", lambda q: q.memset(self.EPSC[:, 0:1], LN_EPS), inc="dve")
        c.emit("vector", lambda q: q.memset(self.EPSC[:, 1:2], DEN_EPS), inc="dve")
        c.emit("vector", lambda q: q.memset(self.EPSC[:, 2:3], PRIOR_EPS), inc="dve")
        c.emit("vector", lambda q: q.memset(self.ONESW[:, :], 1.0), inc="dve")
        tof = c.emit("vector", lambda q: q.memset(self.ONESF[:, :], 1.0), inc="dve")
        self.t_onesf = tof

        stg = c.sbuf("stg", [128, 128], F32)
        self.STG = stg
        self.MAUG = c.sbuf("maug", [128, 8 * 128], BF16)
        self.ZCOLS = c.sbuf("zcols", [128, 8], BF16)
        m0 = self.din("M0")
        z0 = self.din("Z0")
        prev = []
        for hk in range(8):
            tl = c.dma("sync", stg[:, 0:128], m0[hk, :, :], waits=prev,
                       sem="stg")
            t = c.emit("vector", lambda q, hk=hk: q.tensor_copy(
                out=self.MAUG[:, 128 * hk:128 * hk + 128], in_=stg[:, 0:128]),
                waits=[("stg", tl)], inc="dve")
            prev = [("dve", t)]
        for hk in range(8):
            tl = c.dma("sync", stg[:, 0:1],
                       z0[hk:hk + 1, :].rearrange("a b -> b a"), waits=prev,
                       sem="stg")
            t = c.emit("vector", lambda q, hk=hk: q.tensor_copy(
                out=self.ZCOLS[:, hk:hk + 1], in_=stg[:, 0:1]),
                waits=[("stg", tl)], inc="dve")
            prev = [("dve", t)]

        self.RB36 = c.sbuf("rb36", [1, 36], BF16)
        c.emit("vector", lambda q: q.memset(self.RB36[:, :], 0.0), inc="dve")
        trb = c.dma("sync", stg[0:1, 0:4], self.din("RBSEL"), waits=prev,
                    sem="stg")
        c.emit("vector", lambda q: q.tensor_copy(
            out=self.RB36[0:1, 32:36], in_=stg[0:1, 0:4]),
            waits=[("stg", trb)], inc="dve")
        self.t_rb36 = self.dve_fence()

        self.PRB = c.sbuf("prb", [128, 4], F32)
        tpr = c.dma("sync", stg[0:1, 4:8], self.din("PRSEL"),
                    waits=[("dve", self.t_rb36)], sem="stg")
        c.emit("scalar", lambda q: q.activation(
            out=stg[0:1, 4:8], in_=stg[0:1, 4:8], func=AF.Ln,
            bias=self.EPSC[0:1, 2:3]), waits=[("stg", tpr), ("dve", 3)],
            inc="act")
        tln = self.act_fence()
        bw = self.bk.write_waits(5)
        tmm = c.emit("tensor", lambda q: q.matmul(
            self.bk.ap(5)[:, 0:4], self.ONESF[0:1, :], stg[0:1, 4:8],
            start=True, stop=True),
            waits=[("act", tln), ("dve", tof)] + bw, inc="pe")
        tprb = c.emit("vector", lambda q: q.tensor_copy(
            out=self.PRB[:, :], in_=self.bk.ap(5)[:, 0:4]),
            waits=[("pe", tmm)], inc="dve")
        self.bk.read_done(5, ("dve", tprb))

        self.KTC = c.sbuf("ktc", [128, 1], F32)
        self.EXPB = c.sbuf("expb", [128, 1], F32)
        tk = c.dma("sync", stg[0:1, 8:9], self.din("KEYT"),
                   waits=[("dve", tprb)], sem="stg")
        c.emit("scalar", lambda q: q.activation(
            out=stg[0:1, 9:10], in_=stg[0:1, 8:9], func=AF.Abs),
            waits=[("stg", tk)], inc="act")
        ta1 = self.act_fence()
        tab = c.emit("vector", lambda q: q.tensor_scalar_mul(
            out=stg[0:1, 9:10], in0=stg[0:1, 9:10], scalar1=-0.125),
            waits=[("act", ta1)], inc="dve")
        bw = self.bk.write_waits(5)
        tmm2 = c.emit("tensor", lambda q: q.matmul(
            self.bk.ap(5)[:, 4:6], self.ONESF[0:1, :], stg[0:1, 8:10],
            start=True, stop=True),
            waits=[("dve", tab), ("dve", tof)] + bw, inc="pe")
        tc1 = c.emit("vector", lambda q: q.tensor_copy(
            out=self.KTC[:, :], in_=self.bk.ap(5)[:, 4:5]),
            waits=[("pe", tmm2)], inc="dve")
        tc2 = c.emit("vector", lambda q: q.tensor_copy(
            out=self.EXPB[:, :], in_=self.bk.ap(5)[:, 5:6]), inc="dve")
        self.bk.read_done(5, ("dve", tc2))

        self.HMB = c.sbuf("hmb", [128, 2], F32)
        th = c.dma("sync", stg[0:1, 10:12], self.din("HMASK"),
                   waits=[("dve", tc2)], sem="stg")
        bw = self.bk.write_waits(5)
        tmm = c.emit("tensor", lambda q: q.matmul(
            self.bk.ap(5)[:, 6:8], self.ONESF[0:1, :], stg[0:1, 10:12],
            start=True, stop=True), waits=[("stg", th), ("dve", tof)] + bw,
            inc="pe")
        thm = c.emit("vector", lambda q: q.tensor_copy(
            out=self.HMB[:, :], in_=self.bk.ap(5)[:, 6:8]),
            waits=[("pe", tmm)], inc="dve")
        self.bk.read_done(5, ("dve", thm))
        self.t_pre_dve = self.dve_fence()

    def build_woeff(self):
        c = self.c
        wot = self.din("WOT")
        park = self.dint("woeffp", [8 * 1024, 512], BF16)
        src = [self.RW[0][:, 0:2048], self.RW[1][:, 0:2048]]
        OF32 = self.O[:, :].bitcast(F32)
        acc = [OF32[:, 0:512], OF32[:, 512:1024]]
        out = [self.O[:, 2048:2560], self.O[:, 2560:3072]]
        st_t = [None, None]
        pool_t = [None, None]
        i = 0
        for og in range(8):
            for hk in range(8):
                bi = i % 2
                w = list(self.rw_cons[bi])
                self.rw_cons[bi] = []
                if pool_t[bi] is not None:
                    w.append(("pool", pool_t[bi]))
                tl = c.dma("sync", src[bi][:, :],
                           wot[4096 * og + 512 * hk:4096 * og + 512 * hk + 512, :]
                           .rearrange("(j p) n -> p j n", p=128), waits=w,
                           sem=f"swo{bi}")
                sl = src[bi]
                w2 = [(f"swo{bi}", tl)]
                wst = [(f"spk{bi}", st_t[bi])] if st_t[bi] else []
                c.emit("gpsimd", lambda q, sl=sl, bi=bi: q.tensor_tensor(
                    out=acc[bi][:, :], in0=sl[:, 0:512], in1=sl[:, 512:1024],
                    op=ALU.add), waits=w2, inc="pool")
                c.emit("gpsimd", lambda q, sl=sl, bi=bi: q.tensor_tensor(
                    out=out[bi][:, :], in0=sl[:, 1024:1536], in1=sl[:, 1536:2048],
                    op=ALU.add), waits=wst, inc="pool")
                pool_t[bi] = c.emit("gpsimd", lambda q, bi=bi: q.tensor_tensor(
                    out=out[bi][:, :], in0=acc[bi][:, :], in1=out[bi][:, :],
                    op=ALU.add), inc="pool")
                st_t[bi] = c.dma(
                    "scalar",
                    park[1024 * og + 128 * hk:1024 * og + 128 * hk + 128, :],
                    out[bi][:, :], waits=[("pool", pool_t[bi])],
                    sem=f"spk{bi}")
                i += 1
        self.woeff_park = park
        self.woeff_done = [("spk0", st_t[0]), ("spk1", st_t[1])]
        for bi in (0, 1):
            self.rw_cons[bi].append(("pool", pool_t[bi]))

    def preln_block(self, xsrc, b):
        c, bk = self.c, self.bk
        c.scope = c.scope.split("/")[0] + "/preln"
        ident, ident_t = self.SB["IDENT"]
        ready = []
        for j in range(NJ):
            xt = self.XT[j % 2]
            tld = ("sx%d" % (j % 2), c.dma(
                "sync", xt[:, :],
                xsrc[BLK * b + 128 * j:BLK * b + 128 * j + 128, :],
                waits=self.xt_free[j % 2], sem="sx%d" % (j % 2)))
            self.xt_free[j % 2] = []
            sc = 16 + 4 * (j % 2)
            xw = list(self.xln_readers)
            c.emit("vector", lambda q, xt=xt, sc=sc: q.tensor_scalar(
                out=self.XLN[:, :], in0=xt[:, :],
                scalar1=1.0, scalar2=0.0, op0=ALU.mult, op1=ALU.add,
                accum_out=self.SS[:, sc:sc + 1]),
                waits=[tld] + xw, inc="dve")
            c.emit("vector", lambda q, xt=xt, sc=sc:
                   q.scalar_tensor_tensor(
                out=self.XLN[:, :], in0=xt[:, :],
                scalar=1.0, in1=xt[:, :],
                op0=ALU.mult, op1=ALU.mult,
                accum_out=self.SS[:, sc + 1:sc + 2]), inc="dve")
            c.emit("vector", lambda q, sc=sc: q.tensor_scalar_mul(
                out=self.SS[:, sc:sc + 1], in0=self.SS[:, sc:sc + 1],
                scalar1=1.0 / D), inc="dve")
            c.emit("vector", lambda q, sc=sc: q.tensor_scalar_mul(
                out=self.SS[:, sc + 1:sc + 2], in0=self.SS[:, sc + 1:sc + 2],
                scalar1=1.0 / D), inc="dve")
            c.emit("vector", lambda q, sc=sc: q.tensor_tensor(
                out=self.SS[:, sc + 2:sc + 3], in0=self.SS[:, sc:sc + 1],
                in1=self.SS[:, sc:sc + 1], op=ALU.mult), inc="dve")
            tv = c.emit("vector", lambda q, sc=sc: q.tensor_tensor(
                out=self.SS[:, sc + 1:sc + 2], in0=self.SS[:, sc + 1:sc + 2],
                in1=self.SS[:, sc + 2:sc + 3], op=ALU.subtract), inc="dve")
            c.emit("scalar", lambda q, sc=sc: q.activation(
                out=self.SS[:, sc + 3:sc + 4], in_=self.SS[:, sc + 1:sc + 2],
                func=AF.Ln, bias=self.EPSC[:, 0:1]),
                waits=[("dve", tv)], inc="act")
            c.emit("scalar", lambda q, sc=sc: q.activation(
                out=self.SS[:, sc + 3:sc + 4], in_=self.SS[:, sc + 3:sc + 4],
                func=AF.Exp, scale=-0.5), inc="act")
            tr = self.act_fence()
            tn = c.emit("vector", lambda q, xt=xt, sc=sc: q.tensor_scalar(
                out=self.XLN[:, :], in0=xt[:, :], scalar1=self.SS[:, sc:sc + 1],
                scalar2=self.SS[:, sc + 3:sc + 4], op0=ALU.subtract,
                op1=ALU.mult), waits=[("act", tr)] + self.xln_readers,
                inc="dve")
            self.xln_readers = []
            self.xt_free[j % 2].append(("dve", tn))
            tn = self.dve_fence()
            evs = []
            lasts = []
            pend = []
            for hb in range(9):
                if hb < 8:
                    bank = 6 + (hb % 2)
                    w = [("dve", tn), ident_t] + bk.write_waits(bank)
                    last = None
                    for k in range(4):
                        cc = 4 * hb + k
                        last = c.emit("tensor", lambda q, cc=cc, bank=bank,
                                      k=k: q.transpose(
                                          self.bk.ap(bank)[:, 128 * k:
                                                           128 * k + 128],
                                          self.XLN[:, 128 * cc:128 * cc + 128],
                                          ident[:, :]),
                                      waits=w if k == 0 else [],
                                      inc="pe" if k == 3 else None)
                    lasts.append(last)
                else:
                    bank = 6 + (hb % 2)
                    fw = bk.write_waits(bank)
                    last = c.emit("tensor", lambda q, bank=bank: q.transpose(
                        self.bk.ap(bank)[:, 0:128],
                        self.XLN[:, 0:128], ident[:, :]), waits=fw, inc="pe")
                    lasts.append(last)
                    bk.read_done(bank, ("pe", last))
                if hb >= 1:
                    phb = hb - 1
                    pbank = 6 + (phb % 2)
                    ew = list(self.xlt_readers)
                    self.xlt_readers = []
                    xv = self.XLT[:, :].rearrange("p (c t) -> p c t", t=BLK)
                    tev = c.emit("vector", lambda q, phb=phb, j=j, pbank=pbank,
                                 xv=xv: q.tensor_copy(
                                     out=xv[:, 4 * phb:4 * phb + 4,
                                            128 * j:128 * j + 128],
                                     in_=self.bk.ap(pbank)[:, 0:512]
                                     .rearrange("p (c t) -> p c t", t=128)),
                                 waits=[("pe", lasts[hb])] + ew, inc="dve")
                    bk.read_done(pbank, ("dve", tev))
                    evs.append(tev)
            self.xln_readers = [("pe", lasts[-1])]
            tfd = self.dve_fence()
            evs.append(tfd)
            ready.append([("dve", evs[-2]), ("dve", evs[-1])])
        return ready

    def proj_tm(self, src_sb, nk, w_dram, ready, extras=None, bias_row=None,
                src_fn=None):
        c, bk = self.c, self.bk
        c.scope = c.scope.split("/")[0] + "/proj"
        if src_fn is None:
            def src_fn(cc, j):
                return src_sb[:, 512 * cc + 128 * j:512 * cc + 128 * j + 128]
        K = nk * 128
        cpl = min(nk, 8)
        nloads = nk // cpl
        evac = {}
        for og in range(8):
            grp_w = {j: None for j in range(NJ)}
            close_t = {}
            for lg in range(nloads):
                src = w_dram[K * og + cpl * 128 * lg:
                             K * og + cpl * 128 * (lg + 1), :]
                bi = self.rw_next(src, cpl)
                t_lg = None
                for j in range(NJ):
                    bank = (4 * og + j) % 6
                    if grp_w[j] is None:
                        grp_w[j] = bk.write_waits(bank) + list(ready[j])
                    for k in range(cpl):
                        cc = cpl * lg + k
                        w = []
                        if lg == 0 and k == 0:
                            w += grp_w[j]
                        if k == 0:
                            w.append(self.rw_load[bi])
                        st = (lg == 0 and k == 0)
                        fin = (extras is None and bias_row is None
                               and lg == nloads - 1 and k == cpl - 1)
                        is_last = (j == NJ - 1 and k == cpl - 1)
                        t = c.emit("tensor", lambda q, bank=bank, j=j,
                                   bi=bi, st=st, fin=fin, k=k,
                                   lhs=src_fn(cc, j): q.matmul(
                                       bk.ap(bank)[:, :], lhs,
                                       self.RW[bi][:, 512 * k:512 * k + 512],
                                       start=st, stop=fin), waits=w,
                                   inc="pe" if (is_last or fin) else None)
                        if is_last:
                            t_lg = t
                        if fin:
                            close_t[j] = t
                self.rw_cons[bi].append(("pe", t_lg + 1))
            reader_emits = []
            for j in range(NJ):
                bank = (4 * og + j) % 6
                exl = []
                if bias_row is not None:
                    ones_ap, row_ap, row_t = bias_row
                    exl.append((lambda q, ps, stop, og=og, row_ap=row_ap,
                                ones_ap=ones_ap: q.matmul(
                                    ps, ones_ap,
                                    row_ap[:, 512 * og:512 * og + 512],
                                    start=False, stop=stop),
                                [row_t]))
                if extras is not None:
                    exl += extras(og, j)
                for idx, (efn, ew) in enumerate(exl):
                    fin = (idx == len(exl) - 1)
                    t = c.emit("tensor", lambda q, efn=efn, bank=bank,
                               fin=fin: efn(q, bk.ap(bank)[:, :], fin),
                               waits=ew, inc="pe" if fin else None)
                    if fin:
                        close_t[j] = t
                orw = list(self.o_readers)
                self.o_readers = []
                reader_emits.append((j, bank, orw))
            ident, ident_t = self.SB["IDENT"]
            fw = bk.write_waits(6)
            c.emit("tensor", lambda q: q.transpose(
                bk.ap(6)[:, 0:128], ident[:, :], ident[:, :]),
                waits=fw + [ident_t], inc="pe")
            tfence = c.emit("tensor", lambda q: q.transpose(
                bk.ap(6)[:, 128:256], ident[:, :], ident[:, :]),
                inc="pe")
            for bi_ in set(range(2)):
                self.rw_cons[bi_].append(("pe", tfence))
            for (j, bank, orw) in reader_emits:
                tpe = close_t[j + 1] if (j + 1) in close_t else tfence
                tev0 = c.emit("vector", lambda q, bank=bank, og=og, j=j:
                              q.tensor_copy(
                                  out=self.O[:, D * j + 512 * og:
                                             D * j + 512 * og + 512],
                                  in_=bk.ap(bank)[:, :]),
                              waits=orw + [("pe", max(tpe, close_t[j]))],
                              inc="dve")
                bk.read_done(bank, ("dve", tev0))
                bk.read_done(6, ("dve", tev0))
                evac[(og, j)] = ("dve", tev0)
        self.xlt_readers.append(("pe", c.cnt["pe"]))
        return evac

    def epilogue_block(self, b, xsrc, xdst, evac, final=False):
        c = self.c
        c.scope = c.scope.split("/")[0] + "/epi"
        XO = [self.MWB[0][:, :].bitcast(F32), self.MWB[1][:, :].bitcast(F32)]
        XS = [self.RW[0][:, :].bitcast(F32), self.RW[1][:, :].bitcast(F32)]
        out_t = []
        for j in range(NJ):
            sc = 16 + 4 * (j % 2)
            wj = [evac[(og, j)] for og in range(8)]
            xw = list(self.xln_readers) + [("pe", c.cnt["pe"])]
            c.emit("vector", lambda q, sc=sc, j=j: q.tensor_scalar(
                out=self.XLN[:, :], in0=self.O[:, D * j:D * j + D],
                scalar1=1.0, scalar2=0.0, op0=ALU.mult, op1=ALU.add,
                accum_out=self.SS[:, sc:sc + 1]),
                waits=wj + xw, inc="dve")
            self.xln_readers = []
            c.emit("vector", lambda q, sc=sc, j=j: q.scalar_tensor_tensor(
                out=self.XLN[:, :], in0=self.O[:, D * j:D * j + D],
                scalar=1.0, in1=self.O[:, D * j:D * j + D],
                op0=ALU.mult, op1=ALU.mult,
                accum_out=self.SS[:, sc + 1:sc + 2]), inc="dve")
            c.emit("vector", lambda q, sc=sc: q.tensor_scalar_mul(
                out=self.SS[:, sc:sc + 1], in0=self.SS[:, sc:sc + 1],
                scalar1=1.0 / D), inc="dve")
            c.emit("vector", lambda q, sc=sc: q.tensor_scalar_mul(
                out=self.SS[:, sc + 1:sc + 2], in0=self.SS[:, sc + 1:sc + 2],
                scalar1=1.0 / D), inc="dve")
            c.emit("vector", lambda q, sc=sc: q.tensor_tensor(
                out=self.SS[:, sc + 2:sc + 3], in0=self.SS[:, sc:sc + 1],
                in1=self.SS[:, sc:sc + 1], op=ALU.mult), inc="dve")
            tv = c.emit("vector", lambda q, sc=sc: q.tensor_tensor(
                out=self.SS[:, sc + 1:sc + 2], in0=self.SS[:, sc + 1:sc + 2],
                in1=self.SS[:, sc + 2:sc + 3], op=ALU.subtract), inc="dve")
            c.emit("scalar", lambda q, sc=sc: q.activation(
                out=self.SS[:, sc + 3:sc + 4], in_=self.SS[:, sc + 1:sc + 2],
                func=AF.Ln, bias=self.EPSC[:, 0:1]),
                waits=[("dve", tv)], inc="act")
            c.emit("scalar", lambda q, sc=sc: q.activation(
                out=self.SS[:, sc + 3:sc + 4], in_=self.SS[:, sc + 3:sc + 4],
                func=AF.Exp, scale=-0.5), inc="act")
            tr = self.act_fence()
            for h in range(2):
                xo = XO[h]
                xs = XS[h]
                tld = (f"srw{h}", c.dma(
                    "sync", xs[:, :2048],
                    xsrc[BLK * b + 128 * j:BLK * b + 128 * j + 128,
                         2048 * h:2048 * h + 2048],
                    waits=self.rw_cons[h], sem=f"srw{h}"))
                self.rw_cons[h] = []
                self.rw_load[h] = tld
                tn = c.emit("vector", lambda q, j=j, sc=sc, h=h, xo=xo:
                            q.tensor_scalar(
                    out=xo[:, :], in0=self.O[:, D * j + 2048 * h:
                                             D * j + 2048 * h + 2048],
                    scalar1=self.SS[:, sc:sc + 1],
                    scalar2=self.SS[:, sc + 3:sc + 4],
                    op0=ALU.subtract, op1=ALU.mult),
                    waits=[("act", tr)] + self.xo_free + self.mw_cons_pending(h),
                    inc="dve")
                ta = c.emit("vector", lambda q, xs=xs, xo=xo:
                            q.tensor_tensor(
                    out=xo[:, :], in0=xo[:, :],
                    in1=xs[:, :2048], op=ALU.add),
                    waits=[tld], inc="dve")
                self.rw_cons[h].append(("dve", ta))
                if h == 1:
                    self.o_readers = [("dve", ta)]
                tf = self.dve_fence()
                ts = c.dma("scalar",
                           xdst[BLK * b + 128 * j:BLK * b + 128 * j + 128,
                                2048 * h:2048 * h + 2048],
                           xo[:, :], waits=[("dve", tf)], sem="sgp%d" % h)
                self.xo_free = [("sgp%d" % h, ts)]
                self.mw_cons[h].append(("sgp%d" % h, ts))
                out_t.append(("sgp%d" % h, ts))
        return out_t

    def mw_cons_pending(self, h):
        if self.mw_load[h] is None:
            return []
        return [self.mw_load[h]]

    def moe_small(self, ready):
        c, bk = self.c, self.bk
        if not hasattr(self, "ZL"):
            self.WMTS = c.sbuf("wmts", [128, 32 * 36], BF16)
            self.t_wmts = ("swm", c.dma(
                "sync", self.WMTS[:, :],
                self.din("WMT").rearrange("(c p) n -> p c n", p=128),
                sem="swm"))
            self.ZL = c.sbuf("zl", [36, 512], F32)
            self.ALW = c.sbuf("alw", [128, 16], F32)
            self.ALT = c.sbuf("alt", [4, 512], F32)
            self.ZW = c.sbuf("zw", [32, 512], BF16)
        rall = [w for r in ready for w in r]
        pebar = [("pe", c.cnt["pe"])]
        ident32, t_id32 = self.SB["IDENT32"]
        w0 = bk.write_waits(4) + rall + [self.t_wmts]
        for cc in range(NT):
            c.emit("tensor", lambda q, cc=cc: q.matmul(
                bk.ap(4)[0:36, :], self.WMTS[:, 36 * cc:36 * cc + 36],
                self.XLT[:, 512 * cc:512 * cc + 512],
                start=(cc == 0), stop=False), waits=w0 if cc == 0 else [])
        tz = c.emit("tensor", lambda q: q.matmul(
            bk.ap(4)[0:36, :], self.RB36[0:1, :], self.ONESW[0:1, :],
            start=False, stop=True),
            waits=[("dve", self.t_rb36)], inc="pe")
        tzl = c.emit("vector", lambda q: q.tensor_copy(
            out=self.ZL[:, :], in_=bk.ap(4)[0:36, :]),
            waits=[("pe", tz)], inc="dve")
        bk.read_done(4, ("dve", tzl))
        tcp = c.emit("vector", lambda q: q.tensor_copy(
            out=self.SCR[0:4, :], in_=self.ZL[32:36, :]),
            waits=[("dve", tzl)] + pebar, inc="dve")
        tcp_f = self.dve_fence()
        w5 = bk.write_waits(5)
        tt = None
        for j in range(NJ):
            tt = c.emit("tensor", lambda q, j=j: q.transpose(
                bk.ap(5)[:, 4 * j:4 * j + 4],
                self.SCR[0:4, 128 * j:128 * j + 128], ident32[0:4, 0:4]),
                waits=(w5 + [("dve", tcp_f), t_id32]) if j == 0 else [],
                inc="pe" if j == 3 else None)
        tlg = c.emit("vector", lambda q: q.tensor_copy(
            out=self.ALW[:, :], in_=bk.ap(5)[:, 0:16]),
            waits=[("pe", tt)] + pebar, inc="dve")
        bk.read_done(5, ("dve", tlg))
        for j in range(NJ):
            c.emit("vector", lambda q, j=j: q.tensor_tensor(
                out=self.ALW[:, 4 * j:4 * j + 4],
                in0=self.ALW[:, 4 * j:4 * j + 4], in1=self.PRB[:, :],
                op=ALU.add), inc="dve")
            c.emit("vector", lambda q, j=j: q.reduce_max(
                out=self.SS[:, 24:25], in_=self.ALW[:, 4 * j:4 * j + 4],
                axis=AX), inc="dve")
            tmx = c.emit("vector", lambda q: q.tensor_scalar_mul(
                out=self.SS[:, 25:26], in0=self.SS[:, 24:25], scalar1=-1.0),
                inc="dve")
            c.emit("scalar", lambda q, j=j: q.activation(
                out=self.ALW[:, 4 * j:4 * j + 4],
                in_=self.ALW[:, 4 * j:4 * j + 4], func=AF.Exp,
                bias=self.SS[:, 25:26]), waits=[("dve", tmx)], inc="act")
            te = self.act_fence()
            c.emit("vector", lambda q, j=j: q.reduce_sum(
                out=self.SS[:, 26:27], in_=self.ALW[:, 4 * j:4 * j + 4],
                axis=AX), waits=[("act", te)], inc="dve")
            c.emit("vector", lambda q: q.reciprocal(
                out=self.SS[:, 27:28], in_=self.SS[:, 26:27]), inc="dve")
            c.emit("vector", lambda q, j=j: q.tensor_scalar(
                out=self.ALW[:, 4 * j:4 * j + 4],
                in0=self.ALW[:, 4 * j:4 * j + 4],
                scalar1=self.SS[:, 27:28], scalar2=2.0,
                op0=ALU.mult, op1=ALU.mult), inc="dve")
        tal_f = self.dve_fence()
        w5 = bk.write_waits(5)
        tt = None
        for j in range(NJ):
            tt = c.emit("tensor", lambda q, j=j: q.transpose(
                bk.ap(5)[0:4, 128 * j:128 * j + 128],
                self.ALW[:, 4 * j:4 * j + 4], ident32[:, :]),
                waits=(w5 + [("dve", tal_f)]) if j == 0 else [],
                inc="pe" if j == 3 else None)
        talt = c.emit("vector", lambda q: q.tensor_copy(
            out=self.ALT[:, :], in_=bk.ap(5)[0:4, 0:512]),
            waits=[("pe", tt)] + pebar, inc="dve")
        bk.read_done(5, ("dve", talt))
        talt_f = self.dve_fence()
        e432, t_e432 = self.SB["E432"]
        w4 = bk.write_waits(4)
        tex = c.emit("tensor", lambda q: q.matmul(
            bk.ap(4)[0:32, :], e432[:, :], self.ALT[:, :],
            start=True, stop=True),
            waits=w4 + [("dve", talt_f), t_e432], inc="pe")
        tzw = c.emit("vector", lambda q: q.tensor_tensor(
            out=self.ZW[:, :], in0=self.ZL[0:32, :], in1=bk.ap(4)[0:32, :],
            op=ALU.mult), waits=[("pe", tex)] + pebar, inc="dve")
        bk.read_done(4, ("dve", tzw))
        tzw_f = self.dve_fence()
        self.xlt_readers.append(("pe", tz))
        return tzw_f

    def arm_l0_ffn(self, xsrc, xdst, final=False):
        c = self.c
        wbt = self.din("WBT")
        bmt = self.din("BMT")
        c.sem("sbm")
        rowp, rowp_t = self.SB["ROWP"]
        for b in (0, 1):
            ready = self.preln_block(xsrc, b)
            bmt_sb = self.XLN[0:32, :]
            tbm = ("sbm", c.dma("sync", bmt_sb, bmt,
                                waits=[("pe", c.cnt["pe"])] +
                                self.xln_readers, sem="sbm"))
            tzw = self.moe_small(ready)

            def extras(og, j, tzw=tzw, bmt_sb=bmt_sb, tbm=tbm):
                return [(lambda q, ps, stop, og=og, j=j: q.matmul(
                    ps, self.ZW[:, 128 * j:128 * j + 128],
                    bmt_sb[:, 512 * og:512 * og + 512],
                    start=False, stop=stop),
                    [("dve", tzw), tbm])]
            evac = self.proj_tm(self.XLT, NT, wbt, ready, extras=extras,
                                bias_row=(rowp[0:1, 0:128], rowp[0:1, 128:],
                                          rowp_t))
            self.xln_readers.append(("pe", c.cnt["pe"]))
            self.epilogue_block(b, xsrc, xdst, evac, final=final)



    def qkv_block(self, xsrc, b, qkv_out):
        c, bk = self.c, self.bk
        ready = self.preln_block(xsrc, b)
        rall = [w for r in ready for w in r]
        wqkvt = self.din("WQKVT")
        QKVE = self.O[:, :].bitcast(BF16)
        close = {}
        st = []
        for mt in range(13):
            if mt < 12:
                bank = 4 + (mt % 2)
                bi = self.mw_next(wqkvt[4096 * mt:4096 * (mt + 1), :], NT)
                w0 = bk.write_waits(bank) + rall + [self.mw_load[bi]]
                for cc in range(NT):
                    t = c.emit("tensor", lambda q, bank=bank, cc=cc, bi=bi:
                               q.matmul(
                                   bk.ap(bank)[:, :],
                                   self.MWB[bi][:, 128 * cc:128 * cc + 128],
                                   self.XLT[:, 512 * cc:512 * cc + 512],
                                   start=(cc == 0), stop=(cc == NT - 1)),
                               waits=w0 if cc == 0 else [],
                               inc="pe" if cc == NT - 1 else None)
                close[mt] = t
                self.mw_cons[bi].append(("pe", t + 1))
            else:
                ident, ident_t = self.SB["IDENT"]
                fw = bk.write_waits(6)
                c.emit("tensor", lambda q: q.transpose(
                    bk.ap(6)[:, 0:128], self.XLN[:, 0:128], ident[:, :]),
                    waits=fw + [ident_t], inc="pe")
                close[mt] = c.emit("tensor", lambda q: q.transpose(
                    bk.ap(6)[:, 128:256], self.XLN[:, 0:128], ident[:, :]),
                    inc="pe")
                bk.read_done(6, ("pe", close[mt]))
            if mt >= 1:
                pm = mt - 1
                bank = 4 + (pm % 2)
                tev = c.emit("vector", lambda q, bank=bank, pm=pm, QKVE=QKVE:
                             q.tensor_copy(
                                 out=QKVE[:, 516 * pm + 4:516 * pm + 516],
                                 in_=bk.ap(bank)[:, :]),
                             waits=[("pe", close[pm + 1])] +
                             self.qkv_area_free, inc="dve")
                bk.read_done(bank, ("dve", tev))
                taf = self.dve_fence()
                ts_ = c.dma("sync",
                            qkv_out[128 * pm:128 * pm + 128,
                                    512 * b:512 * b + 512],
                            QKVE[:, 516 * pm + 4:516 * pm + 516],
                            waits=[("dve", taf)], sem="sqo")
                st.append(ts_)
        self.qkv_area_free = [("sqo", c.cnt["sqo"])]
        return st

    def load_conv_consts(self):
        c = self.c
        c.sem("sccv")
        XT0 = self.XT[0]
        XT0B = XT0[:, :].bitcast(BF16)
        w = [("pe", c.cnt["pe"]), ("dve", c.cnt["dve"]),
             ("act", c.cnt["act"])] + self.xt_free[0]
        self.xt_free[0] = []
        plan = [("CT", XT0[:, 0:1024]), ("ST", XT0[:, 1024:2048]),
                ("EXQ", XT0[0:16, 2048:3072]), ("EXK", XT0[0:4, 3072:3328]),
                ("QMS", XT0B[:, 6656:7168]), ("KMS", XT0B[:, 7168:7296]),
                ("PERMSW", XT0B[:, 7296:7424]), ("HSQ", XT0B[:, 7424:7552]),
                ("HSK", XT0B[:, 7552:7560])]
        self.CCV = {}
        for k, ap in plan:
            c.dma("sync", ap, self.din(k), waits=w, sem="sccv")
            self.CCV[k] = ap
        self.ccv_w = ("sccv", c.cnt["sccv"])

    def conv_block(self, b, qkv_in, halo_in, a2a_out, qloc=None, kvin=None):
        c, bk = self.c, self.bk
        for sn in ("sa2", "sqi", "scv", "sck"):
            c.sem(sn)
        QKVE = self.O[:, :].bitcast(BF16)
        C1E = self.XLT[:, 0:5160]
        SQ = [self.XLT[:, 5160 + 512 * i:5672 + 512 * i] for i in range(2)]
        KR = self.XLT[:, 6184:6696]
        CV1W = self.XLT[:, 6696:9768]
        KV1W = self.XLT[:, 9768:10536]
        QMN = self.SCR[:, :]
        KMN = self.XLT[:, 10536:11560].bitcast(F32)
        R16 = self.SCR[0:16, :]
        cbias, cbias_w = self.SB["CBIAS"]
        cw = self.ccv_w
        ct, ct_w = self.CCV["CT"], cw
        st_, st_w = self.CCV["ST"], cw
        permsw, permsw_w = self.CCV["PERMSW"], cw
        hsq, hsq_w = self.CCV["HSQ"], cw
        hsk, hsk_w = self.CCV["HSK"], cw
        exq, exq_w = self.CCV["EXQ"], cw
        exk, exk_w = self.CCV["EXK"], cw
        qms, qms_w = self.CCV["QMS"], cw
        kms, kms_w = self.CCV["KMS"], cw
        if qkv_in is not None:
            for mt in range(12):
                c.dma("sync", QKVE[:, 516 * mt + 4:516 * mt + 516],
                      qkv_in[128 * mt:128 * mt + 128, 512 * b:512 * b + 512],
                      waits=self.qkv_area_free, sem="sqi")
        if halo_in is not None:
            for mt in range(12):
                c.dma("sync", QKVE[:, 516 * mt:516 * mt + 4],
                      halo_in[128 * mt:128 * mt + 128, 4 * b:4 * b + 4],
                      waits=self.qkv_area_free, sem="sqi")
        self.qkv_area_free = []
        qw = [("sqi", c.cnt["sqi"])]
        for hb2 in range(2):
            tc1 = c.dma("sync",
                        CV1W[64 * hb2:64 * hb2 + 64, :]
                        .rearrange("p (a o) -> p a o", o=64),
                        self.din("QC1T").rearrange("(a i) o -> i a o", i=64),
                        sem="scv")
            tk1 = c.dma("sync",
                        KV1W[64 * hb2:64 * hb2 + 64, :]
                        .rearrange("p (a o) -> p a o", o=64),
                        self.din("KC1T").rearrange("(a i) o -> i a o", i=64),
                        sem="sck")
        c1ev = {}
        wb2 = bk.write_waits(2)
        for gi in range(21):
            if gi < 20:
                isk = gi >= 16
                g = gi - 16 if isk else gi
                bank = 4 + (gi % 2)
                ccd = (8 + g // 2) if isk else (g // 2)
                half = (g % 2) * 64
                W = (KV1W if isk else CV1W)[half:half + 64, :]
                ng = 4 if isk else 16
                ww = (bk.write_waits(bank) + qw +
                      [("sck" if isk else "scv", tk1 if isk else tc1)])
                last = None
                for tau in range(3):
                    lh = W[:, (tau * ng + g) * 64:(tau * ng + g) * 64 + 64]
                    last = c.emit("tensor", lambda q, bank=bank, half=half,
                                  lh=lh, ccd=ccd, tau=tau, QKVE=QKVE:
                                  q.matmul(
                                      bk.ap(bank)[half:half + 64, :],
                                      lh,
                                      QKVE[half:half + 64,
                                           516 * ccd + 2 + tau:
                                           516 * ccd + 2 + tau + 512],
                                      start=(tau == 0), stop=(tau == 2)),
                                  waits=ww if tau == 0 else [],
                                  inc="pe" if tau == 2 else None)
                for tau in range(3):
                    lh = W[:, (tau * ng + g) * 64:(tau * ng + g) * 64 + 64]
                    c.emit("tensor", lambda q, half=half, lh=lh, ccd=ccd,
                           tau=tau, gi=gi, QKVE=QKVE: q.matmul(
                               bk.ap(2)[half:half + 64, 2 * gi:2 * gi + 2],
                               lh,
                               QKVE[half:half + 64,
                                    516 * ccd + tau:516 * ccd + tau + 2],
                               start=(tau == 0), stop=(tau == 2)),
                           waits=wb2 if (gi == 0 and tau == 0) else [])
                wb2 = []
                c1ev[gi] = (last, bank, half, ccd, isk, g)
            if gi >= 1:
                (tcl, bank, half, ccd, isk, g) = c1ev[gi - 1]
                guard = ("pe", c.cnt["pe"])
                bcol = (24 + g // 2) if isk else (8 + g // 2)
                t1 = c.emit("scalar", lambda q, bank=bank, half=half,
                            ccd=ccd, bcol=bcol, C1E=C1E: q.activation(
                    out=C1E[half:half + 64, 516 * ccd + 4:516 * ccd + 516],
                    in_=bk.ap(bank)[half:half + 64, :], func=AF.Identity,
                    bias=cbias[half:half + 64, bcol:bcol + 1]),
                    waits=[guard, cbias_w], inc="act")
                bk.read_done(bank, ("act", t1))
        tfin = self.pe_fence()
        for gi in range(20):
            isk = gi >= 16
            g = gi - 16 if isk else gi
            ccd = (8 + g // 2) if isk else (g // 2)
            half = (g % 2) * 64
            bcol = (24 + g // 2) if isk else (8 + g // 2)
            th = c.emit("scalar", lambda q, half=half, ccd=ccd, gi=gi,
                        bcol=bcol, C1E=C1E: q.activation(
                out=C1E[half:half + 64, 516 * ccd + 2:516 * ccd + 4],
                in_=bk.ap(2)[half:half + 64, 2 * gi:2 * gi + 2], func=AF.Identity,
                bias=cbias[half:half + 64, bcol:bcol + 1]),
                waits=[("pe", tfin)], inc="act")
        bk.read_done(2, ("act", c.cnt["act"]))
        taf = self.act_fence()
        for ccd in range(10):
            c.emit("vector", lambda q, ccd=ccd, b=b, C1E=C1E:
                   q.tensor_scalar_mul(
                out=C1E[:, 516 * ccd + 2:516 * ccd + 4],
                in0=C1E[:, 516 * ccd + 2:516 * ccd + 4],
                scalar1=self.HMB[:, b:b + 1]),
                waits=[("act", taf), ("dve", self.t_pre_dve)], inc="dve")
        thm = self.dve_fence()
        c1w = [("dve", thm)]
        wb = bk.write_waits(2)
        for cc in range(8):
            tqm = c.emit("tensor", lambda q, cc=cc, QKVE=QKVE: q.matmul(
                bk.ap(2)[0:64, :], qms[:, 64 * cc:64 * cc + 64],
                QKVE[:, 516 * cc + 4:516 * cc + 516],
                start=(cc == 0), stop=(cc == 7)),
                waits=(wb + qw + [qms_w]) if cc == 0 else [],
                inc="pe" if cc == 7 else None)
        wb = bk.write_waits(3)
        for cc in range(2):
            tkm = c.emit("tensor", lambda q, cc=cc, QKVE=QKVE: q.matmul(
                bk.ap(3)[0:64, :], kms[:, 64 * cc:64 * cc + 64],
                QKVE[:, 516 * (8 + cc) + 4:516 * (8 + cc) + 516],
                start=(cc == 0), stop=(cc == 1)),
                waits=(wb + [kms_w]) if cc == 0 else [],
                inc="pe" if cc == 1 else None)
        for hb2 in range(2):
            tqm2 = c.emit("vector", lambda q, hb2=hb2: q.tensor_copy(
                out=QMN[64 * hb2:64 * hb2 + 64, :], in_=bk.ap(2)[0:64, :]),
                waits=[("pe", tkm)], inc="dve")
            tkm2 = c.emit("vector", lambda q, hb2=hb2: q.tensor_copy(
                out=KMN[64 * hb2:64 * hb2 + 64, :], in_=bk.ap(3)[0:64, :]),
                inc="dve")
        bk.read_done(2, ("dve", tqm2))
        bk.read_done(3, ("dve", tkm2))
        tmn = self.dve_fence()
        qc2t = self.din("QC2T")
        kc2t = self.din("KC2T")
        cv2 = {}
        qq_t = {}
        prev_qq = None
        for mi in range(11):
            if mi < 10:
                isk = mi >= 8
                mt = mi - 8 if isk else mi
                bank = 4 + (mi % 2)
                nch = 6 if isk else 24
                src = (kc2t[768 * mt:768 * (mt + 1), :] if isk
                       else qc2t[3072 * mt:3072 * (mt + 1), :])
                bi = self.mw_next(src, nch)
                w0 = bk.write_waits(bank) + c1w + [self.mw_load[bi]]
                ncc = 2 if isk else 8
                first = True
                for tau in range(3):
                    for ccl in range(ncc):
                        scc = (8 + ccl) if isk else ccl
                        t = c.emit("tensor", lambda q, bank=bank, bi=bi,
                                   tau=tau, ccl=ccl, scc=scc, ncc=ncc,
                                   first=first, C1E=C1E: q.matmul(
                                       bk.ap(bank)[:, :],
                                       self.MWB[bi][:, 128 * (tau * ncc + ccl):
                                                    128 * (tau * ncc + ccl) + 128],
                                       C1E[:, 516 * scc + 2 + tau:
                                           516 * scc + 2 + tau + 512],
                                       start=first,
                                       stop=(tau == 2 and ccl == ncc - 1)),
                                   waits=w0 if first else [],
                                   inc="pe" if (tau == 2 and ccl == ncc - 1)
                                   else None)
                        first = False
                cv2[mi] = (t, bank, isk, mt)
                self.mw_cons[bi].append(("pe", t + 1))
            if mi >= 1:
                (tcl, bank, isk, mt) = cv2[mi - 1]
                guard = ("pe", cv2[mi][0] if mi < 10 else c.cnt["pe"])
                bcol = (26 + mt) if isk else (16 + mt)
                tb = c.emit("scalar", lambda q, bank=bank, bcol=bcol:
                            q.activation(
                    out=self.SCR2[:, :], in_=bk.ap(bank)[:, :], func=AF.Identity,
                    bias=cbias[:, bcol:bcol + 1]),
                    waits=[guard, ("dve", tmn)] +
                    ([("dve", prev_qq)] if prev_qq else []), inc="act")
                bk.read_done(bank, ("act", tb))
                taf2 = self.act_fence()
                dst = (8 + mt) if isk else mt
                mean = QMN if isk else KMN
                for hh in range(2):
                    c.emit("vector", lambda q, dst=dst, hh=hh, mean=mean,
                           QKVE=QKVE: q.scalar_tensor_tensor(
                        out=QKVE[64 * hh:64 * hh + 64,
                                 516 * dst + 4:516 * dst + 516],
                        in0=mean[64 * hh:64 * hh + 64, :], scalar=0.5,
                        in1=self.SCR2[64 * hh:64 * hh + 64, :],
                        op0=ALU.mult, op1=ALU.add),
                        waits=[("act", taf2)], inc="dve")
                qq_t[dst] = self.dve_fence()
                prev_qq = qq_t[dst]
        for isk in (False, True):
            nh = 4 if isk else 16
            ncc = 2 if isk else 8
            hs, hs_w = (hsk, hsk_w) if isk else (hsq, hsq_w)
            wb = bk.write_waits(3)
            tsqp = None
            for ccl in range(ncc):
                scc = (8 + ccl) if isk else ccl
                par = ccl % 2
                tsq = c.emit("scalar", lambda q, scc=scc, par=par, QKVE=QKVE,
                             SQ=SQ: q.activation(
                    out=SQ[par][:, :],
                    in_=QKVE[:, 516 * scc + 4:516 * scc + 516],
                    func=AF.Square),
                    waits=[("dve", qq_t[scc])] +
                          ([("pe", tsqp)] if tsqp else []), inc="act")
                tafq = self.act_fence()
                tsqp = c.emit("tensor", lambda q, ccl=ccl, par=par, nh=nh,
                              hs=hs, ncc=ncc, SQ=SQ: q.matmul(
                    bk.ap(3)[0:nh, :], hs[:, nh * ccl:nh * ccl + nh],
                    SQ[par][:, :], start=(ccl == 0), stop=(ccl == ncc - 1)),
                    waits=(wb + [hs_w, ("act", tafq)]) if ccl == 0
                    else [("act", tafq)], inc="pe")
            tsqf = self.pe_fence()
            trs = c.emit("vector", lambda q, nh=nh: q.tensor_copy(
                out=self.SCR[0:nh, :],
                in_=bk.ap(3)[0:nh, :]), waits=[("pe", tsqf)], inc="dve")
            bk.read_done(3, ("dve", trs))
            R = self.SCR[0:nh, :]
            tlnf = self.dve_fence()
            c.emit("scalar", lambda q, R=R: q.activation(
                out=R, in_=R, func=AF.Ln), waits=[("dve", tlnf)],
                inc="act")
            c.emit("scalar", lambda q, R=R: q.activation(
                out=R, in_=R, func=AF.Exp, scale=-0.5), inc="act")
            tafr = self.act_fence()
            tcl = c.emit("vector", lambda q, R=R: q.tensor_scalar_min(
                out=R, in0=R, scalar1=1e12), waits=[("act", tafr)], inc="dve")
            if isk:
                tcl = c.emit("vector", lambda q, R=R, nh=nh: q.tensor_scalar(
                    out=R, in0=R, scalar1=self.KTC[0:nh, 0:1], scalar2=0.0,
                    op0=ALU.mult, op1=ALU.add), inc="dve")
            tclf = self.dve_fence()
            ex, ex_w = (exk, exk_w) if isk else (exq, exq_w)
            for ccl in range(ncc):
                scc = (8 + ccl) if isk else ccl
                bb = ccl % 2
                wbx = bk.write_waits(bb)
                tex = c.emit("tensor", lambda q, ccl=ccl, bb=bb, ex=ex, R=R,
                             nh=nh: q.matmul(
                    bk.ap(bb)[:, :], ex[:, 128 * ccl:128 * ccl + 128], R,
                    start=True, stop=True),
                    waits=wbx + [("dve", tclf), ex_w], inc="pe")
                tqn = c.emit("vector", lambda q, scc=scc, bb=bb, QKVE=QKVE:
                             q.tensor_tensor(
                    out=QKVE[:, 516 * scc + 4:516 * scc + 516],
                    in0=QKVE[:, 516 * scc + 4:516 * scc + 516],
                    in1=bk.ap(bb)[:, :], op=ALU.mult),
                    waits=[("pe", tex)], inc="dve")
                bk.read_done(bb, ("dve", tqn))
                qq_t[scc] = self.dve_fence()
        na2 = 0
        for cc in range(10):
            isk = cc >= 8
            ccl = cc - 8 if isk else cc
            bank = 4 + (cc % 2)
            wbx = bk.write_waits(bank)
            tpm = c.emit("tensor", lambda q, cc=cc, bank=bank, QKVE=QKVE:
                         q.matmul(
                bk.ap(bank)[:, :], permsw[:, :],
                QKVE[:, 516 * cc + 4:516 * cc + 516],
                start=True, stop=True),
                waits=wbx + [("dve", qq_t[cc]), permsw_w], inc="pe")
            tpmf = self.pe_fence()
            t1 = c.emit("vector", lambda q, bank=bank, b=b, st_=st_:
                        q.tensor_tensor(
                out=self.SCR2[:, :], in0=bk.ap(bank)[:, :],
                in1=st_[:, 512 * b:512 * b + 512], op=ALU.mult),
                waits=[("pe", tpmf), st_w], inc="dve")
            bk.read_done(bank, ("dve", t1))
            t2 = c.emit("vector", lambda q, cc=cc, b=b, ct=ct, QKVE=QKVE:
                        q.scalar_tensor_tensor(
                out=self.SCR[:, :], in0=QKVE[:, 516 * cc + 4:516 * cc + 516],
                scalar=1.0, in1=ct[:, 512 * b:512 * b + 512],
                op0=ALU.mult, op1=ALU.mult), waits=[ct_w], inc="dve")
            t3 = c.emit("vector", lambda q, KR=KR: q.tensor_tensor(
                out=KR[:, :], in0=self.SCR[:, :], in1=self.SCR2[:, :],
                op=ALU.add), waits=[("sa2", c.cnt["sa2"])], inc="dve")
            tf3 = self.dve_fence()
            if not isk:
                na2 += 1
                if qloc is not None:
                    c.dma("scalar", qloc[128 * cc:128 * cc + 128,
                                         512 * b:512 * b + 512],
                          KR[:, :], waits=[("dve", tf3)], sem="sa2")
                else:
                    c.dma("gpsimd", a2a_out[512 * cc + 256 * b:
                                            512 * cc + 256 * b + 128, :],
                          KR[:, :], waits=[("dve", tf3)], sem="sa2")
            else:
                for hh in range(2):
                    h = 2 * ccl + hh
                    if kvin is not None:
                        na2 += 1
                        c.dma("scalar",
                              kvin[64 * h:64 * h + 64,
                                   512 * b:512 * b + 512],
                              KR[64 * hh:64 * hh + 64, :],
                              waits=[("dve", tf3)], sem="sa2")
                        continue
                    for dd in (2 * h, 2 * h + 1):
                        na2 += 1
                        c.dma("gpsimd",
                              a2a_out[512 * dd + 256 * b + 128:
                                      512 * dd + 256 * b + 192, :],
                              KR[64 * hh:64 * hh + 64, :],
                              waits=[("dve", tf3)], sem="sa2")
        tvf = ("sqi", c.cnt["sqi"])
        for h in range(4):
            src_cc = 10 if h < 2 else 11
            roff = 64 * (h % 2)
            c0 = 516 * src_cc + (4 if h < 2 else 3)
            if kvin is not None:
                na2 += 1
                c.dma("scalar",
                      kvin[256 + 64 * h:256 + 64 * h + 64,
                           512 * b:512 * b + 512],
                      QKVE[roff:roff + 64, c0:c0 + 512],
                      waits=[tvf, ("dve", c.cnt["dve"])], sem="sa2")
                continue
            for dd in (2 * h, 2 * h + 1):
                na2 += 1
                c.dma("gpsimd",
                      a2a_out[512 * dd + 256 * b + 192:
                              512 * dd + 256 * b + 256, :],
                      QKVE[roff:roff + 64, c0:c0 + 512],
                      waits=[tvf, ("dve", c.cnt["dve"])], sem="sa2")
        self.qkv_area_free = [("sa2", c.cnt["sa2"]), ("pe", c.cnt["pe"]),
                              ("act", c.cnt["act"]), ("dve", c.cnt["dve"])]
        self.xt_free[0].append(("pe", c.cnt["pe"]))
        return ("sa2", c.cnt["sa2"])

    def arm_l0_attn(self, xsrc, xdst, final=False):
        c, bk = self.c, self.bk
        wqselt = self.din("WQSELT")
        cbias, cbias_w = self.SB["CBIAS"]
        for b in (0, 1):
            ready = self.preln_block(xsrc, b)
            if b == 0:
                self.build_woeff()
            woeff_done = self.woeff_done
            rall = [w for r in ready for w in r]
            SIGQ = self.XLN[:, :]
            RETR = self.XT[0][:, :].bitcast(BF16)[:, 0:4096]
            NUMB = self.XT[1][:, :].bitcast(BF16)[:, 0:512]
            close = {}
            sig_t = {}
            for mt in range(9):
                if mt < 8:
                    bank = 4 + (mt % 2)
                    bi = self.mw_next(wqselt[4096 * mt:4096 * (mt + 1), :], NT)
                    w0 = bk.write_waits(bank) + rall + [self.mw_load[bi]]
                    for cc in range(NT):
                        t = c.emit("tensor", lambda q, bank=bank, cc=cc, bi=bi:
                                   q.matmul(
                                       bk.ap(bank)[:, :],
                                       self.MWB[bi][:, 128 * cc:128 * cc + 128],
                                       self.XLT[:, 512 * cc:512 * cc + 512],
                                       start=(cc == 0), stop=(cc == NT - 1)),
                                   waits=w0 if cc == 0 else [],
                                   inc="pe" if cc == NT - 1 else None)
                    close[mt] = t
                    self.mw_cons[bi].append(("pe", t + 1))
                else:
                    ident, ident_t = self.SB["IDENT"]
                    fw = bk.write_waits(6)
                    c.emit("tensor", lambda q: q.transpose(
                        bk.ap(6)[:, 0:128], self.XLN[:, 0:128], ident[:, :]),
                        waits=fw + [ident_t], inc="pe")
                    close[mt] = c.emit("tensor", lambda q: q.transpose(
                        bk.ap(6)[:, 128:256], self.XLN[:, 0:128], ident[:, :]),
                        inc="pe")
                    bk.read_done(6, ("pe", close[mt]))
                if mt >= 1:
                    pm = mt - 1
                    bank = 4 + (pm % 2)
                    guard = [("pe", close[mt])]
                    t1 = c.emit("vector", lambda q, bank=bank, pm=pm:
                                q.tensor_scalar(
                        out=self.SCR[:, :], in0=bk.ap(bank)[:, :],
                        scalar1=cbias[:, pm:pm + 1], scalar2=0.0,
                        op0=ALU.add, op1=ALU.min),
                        waits=guard + [cbias_w], inc="dve")
                    t2 = c.emit("vector", lambda q, bank=bank, pm=pm:
                                q.tensor_scalar(
                        out=self.SCR2[:, :], in0=bk.ap(bank)[:, :],
                        scalar1=cbias[:, pm:pm + 1], scalar2=0.0,
                        op0=ALU.add, op1=ALU.max), inc="dve")
                    bk.read_done(bank, ("dve", t2))
                    tf = self.dve_fence()
                    c.emit("scalar", lambda q: q.activation(
                        out=self.SCR[:, :], in_=self.SCR[:, :], func=AF.Exp),
                        waits=[("dve", tf)], inc="act")
                    ta = self.act_fence()
                    ts_ = c.emit("vector", lambda q, pm=pm, SIGQ=SIGQ:
                                 q.tensor_tensor(
                        out=SIGQ[:, 512 * pm:512 * pm + 512],
                        in0=self.SCR[:, :], in1=self.SCR2[:, :], op=ALU.add),
                        waits=[("act", ta)], inc="dve")
                    sig_t[pm] = self.dve_fence()
            retr_t = {}
            denb = {}
            for hk in range(9):
                if hk < 8:
                    nb = 4 + (hk % 2)
                    db = 2 + (hk % 2)
                    w0 = bk.write_waits(nb) + [("dve", sig_t[hk]),
                                               ("dve", self.t_pre_dve)]
                    c.emit("tensor", lambda q, nb=nb, hk=hk, SIGQ=SIGQ:
                           q.matmul(
                               bk.ap(nb)[:, :],
                               self.MAUG[:, 128 * hk:128 * hk + 128],
                               SIGQ[:, 512 * hk:512 * hk + 512],
                               start=True, stop=True), waits=w0, inc="pe")
                    wd = bk.write_waits(db)
                    t = c.emit("tensor", lambda q, db=db, hk=hk, SIGQ=SIGQ:
                               q.matmul(
                                   bk.ap(db)[0:1, :],
                                   self.ZCOLS[:, hk:hk + 1],
                                   SIGQ[:, 512 * hk:512 * hk + 512],
                                   start=True, stop=True), waits=wd, inc="pe")
                    denb[hk] = t
                if hk >= 1:
                    ph = hk - 1
                    nb = 4 + (ph % 2)
                    db = 2 + (ph % 2)
                    guard = [("pe", denb[hk] if hk < 8 else c.cnt["pe"])]
                    c.emit("vector", lambda q, db=db: q.tensor_scalar_add(
                        out=self.SCR[0:1, :], in0=bk.ap(db)[0:1, :],
                        scalar1=DEN_EPS), waits=guard, inc="dve")
                    trc = c.emit("vector", lambda q: q.reciprocal(
                        out=self.SCR[0:1, :], in_=self.SCR[0:1, :]), inc="dve")
                    bk.read_done(db, ("dve", trc))
                    tfd = self.dve_fence()
                    bb = 0 if (ph % 2) == 0 else 1
                    wb = bk.write_waits(bb)
                    tbc = c.emit("tensor", lambda q, bb=bb: q.matmul(
                        bk.ap(bb)[:, :], self.ONESF[0:1, :], self.SCR[0:1, :],
                        start=True, stop=True),
                        waits=wb + [("dve", tfd)], inc="pe")
                    tnm = c.emit("scalar", lambda q, nb=nb, NUMB=NUMB:
                                 q.activation(out=NUMB[:, :],
                                              in_=bk.ap(nb)[:, :],
                                              func=AF.Copy),
                                 waits=[("pe", tbc)], inc="act")
                    bk.read_done(nb, ("act", tnm))
                    taf = self.act_fence()
                    trr = c.emit("vector", lambda q, ph=ph, bb=bb, NUMB=NUMB,
                                 RETR=RETR: q.tensor_tensor(
                        out=RETR[:, 512 * ph:512 * ph + 512],
                        in0=NUMB[:, :], in1=bk.ap(bb)[:, :], op=ALU.mult),
                        waits=[("act", taf)], inc="dve")
                    bk.read_done(bb, ("dve", trr))
                    retr_t[ph] = trr
            tretr = self.dve_fence()
            self.rw_cons[0] += woeff_done
            self.rw_cons[1] += woeff_done
            rd = [[("dve", tretr)]] * NJ
            evac = self.proj_tm(RETR, 8, self.woeff_park, rd)
            pw = ("pe", c.cnt["pe"])
            self.xt_free[0].append(pw)
            self.xt_free[1].append(pw)
            self.epilogue_block(b, xsrc, xdst, evac, final=final)



    def attention(self, att_in, a2a3_out):
        c, bk = self.c, self.bk
        for sn in ("sat", "sa3"):
            c.sem(sn)
        QF = [self.XLT[0:64, 0:8192], self.XLT[0:64, 8192:16384]]
        KF = self.O[0:64, 0:8192]
        VF = self.O[0:64, 8192:16384]
        XT0B = self.XT[0][:, :].bitcast(BF16)
        XT1B = self.XT[1][:, :].bitcast(BF16)
        VAUG = XT0B[:, 0:4160]
        PBUF = [XT0B[:, 4160 + 512 * i:4672 + 512 * i] for i in range(2)]
        NUMB = XT0B[0:64, 5184:5696]
        AOUT = [XT1B[0:64, 0:8192], self.O[0:64, 8192:16384]]
        mbig, mbig_w = self.SB["MBIG"]
        ident, ident_t = self.SB["IDENT"]
        pw = [("pe", c.cnt["pe"]), ("dve", c.cnt["dve"]),
              ("act", c.cnt["act"])]
        for g in range(16):
            s_ = g if g < 8 else 15 - g
            hf = 0 if g < 8 else 1
            base = 512 * s_ + 256 * hf
            for hl in range(2):
                c.dma("sync", QF[hl][:, 512 * g:512 * g + 512],
                      att_in[base + 64 * hl:base + 64 * hl + 64, :],
                      waits=pw, sem="sat")
            c.dma("sync", KF[:, 512 * g:512 * g + 512],
                  att_in[base + 128:base + 192, :], waits=pw, sem="sat")
            c.dma("sync", VF[:, 512 * g:512 * g + 512],
                  att_in[base + 192:base + 256, :], waits=pw, sem="sat")
        tat = ("sat", c.cnt["sat"])
        tva = c.emit("vector", lambda q: q.memset(VAUG, 1.0), waits=pw,
                     inc="dve")
        lastT = {}
        for kc in range(65):
            if kc < 64:
                bank = 6 + (kc % 2)
                w = bk.write_waits(bank) + [tat, ident_t]
                lastT[kc] = c.emit("tensor", lambda q, kc=kc, bank=bank:
                                   q.transpose(
                    bk.ap(bank)[:, 0:64], VF[:, 128 * kc:128 * kc + 128],
                    ident[0:64, 0:64]), waits=w, inc="pe")
            else:
                lastT[kc] = self.pe_fence(bank=6)
            if kc >= 1:
                pb = 6 + ((kc - 1) % 2)
                tev = c.emit("vector", lambda q, kc=kc, pb=pb: q.tensor_copy(
                    out=VAUG[:, 65 * (kc - 1):65 * (kc - 1) + 64],
                    in_=bk.ap(pb)[:, 0:64]),
                    waits=[("pe", lastT[kc]), ("dve", tva)], inc="dve")
                bk.read_done(pb, ("dve", tev))
        t_vtr = lastT[64]
        tvf = self.dve_fence()
        pv_hist = {0: [], 1: []}
        last_tao = []
        for g in range(16):
            nkc = 4 * g + 4
            for hl in range(2):
                ob = 2 + hl
                wo = bk.write_waits(ob)
                exp_t = {}
                for kc in range(nkc + 1):
                    if kc < nkc:
                        sb_ = 4 + (kc % 2)
                        ws = bk.write_waits(sb_)
                        if kc == 0:
                            ws = ws + [tat, ("dve", tvf)]
                        tqk = c.emit("tensor", lambda q, kc=kc, sb_=sb_, g=g,
                                     hl=hl: q.matmul(
                            bk.ap(sb_)[:, :], KF[:, 128 * kc:128 * kc + 128],
                            QF[hl][:, 512 * g:512 * g + 512],
                            start=True, stop=True), waits=ws, inc="pe")
                        par = kc % 2
                        wex = [("pe", tqk),
                               ("dve", self.t_pre_dve)] + pv_hist[par]
                        pv_hist[par] = []
                        te = c.emit("scalar", lambda q, sb_=sb_, par=par:
                                    q.activation(
                            out=PBUF[par][:, :], in_=bk.ap(sb_)[:, :],
                            func=AF.Exp, scale=0.125, bias=self.EXPB[:, 0:1]),
                            waits=wex, inc="act")
                        bk.read_done(sb_, ("act", te))
                        if kc >= 4 * g:
                            off = 384 - 128 * (kc - 4 * g)
                            c.emit("vector", lambda q, par=par, off=off:
                                   q.tensor_tensor(
                                out=PBUF[par][:, :], in0=PBUF[par][:, :],
                                in1=mbig[:, off:off + 512], op=ALU.mult),
                                waits=[("act", te), mbig_w], inc="dve")
                            exp_t[kc] = ("dve", self.dve_fence())
                        else:
                            exp_t[kc] = ("act", te)
                    if kc >= 1:
                        pkc = kc - 1
                        par = pkc % 2
                        gate = exp_t[kc] if kc < nkc else \
                            ("act", self.act_fence())
                        wpv = [exp_t[pkc], gate] + (wo if pkc == 0 else [])
                        tpv = c.emit("tensor", lambda q, pkc=pkc, par=par,
                                     ob=ob, nkc=nkc: q.matmul(
                            bk.ap(ob)[0:65, :],
                            VAUG[:, 65 * pkc:65 * pkc + 65], PBUF[par][:, :],
                            start=(pkc == 0), stop=(pkc == nkc - 1)),
                            waits=wpv, inc="pe")
                        pv_hist[par].append(("pe", tpv))
                tpf = self.pe_fence()
                tdc = c.emit("vector", lambda q, ob=ob: q.tensor_copy(
                    out=self.SCR[0:1, :], in_=bk.ap(ob)[64:65, :]),
                    waits=[("pe", tpf)], inc="dve")
                c.emit("vector", lambda q: q.reciprocal(
                    out=self.SCR[0:1, :], in_=self.SCR[0:1, :]), inc="dve")
                tfd = self.dve_fence()
                bb = hl
                wb = bk.write_waits(bb)
                tbc = c.emit("tensor", lambda q, bb=bb: q.matmul(
                    bk.ap(bb)[0:64, :], self.ONESF[0:1, 0:64],
                    self.SCR[0:1, :], start=True, stop=True),
                    waits=wb + [("dve", tfd), ("dve", self.t_onesf)],
                    inc="pe")
                tnm = c.emit("scalar", lambda q, ob=ob: q.activation(
                    out=NUMB[:, :], in_=bk.ap(ob)[0:64, :], func=AF.Copy),
                    waits=[("pe", tbc)] + last_tao, inc="act")
                bk.read_done(ob, ("act", tnm))
                bk.read_done(ob, ("dve", tdc))
                taf = self.act_fence()
                tao = c.emit("vector", lambda q, g=g, hl=hl, bb=bb:
                             q.tensor_tensor(
                    out=AOUT[hl][:, 512 * g:512 * g + 512], in0=NUMB[:, :],
                    in1=bk.ap(bb)[0:64, :], op=ALU.mult),
                    waits=[("act", taf), ("pe", t_vtr)], inc="dve")
                bk.read_done(bb, ("dve", tao))
                last_tao = [("dve", tao)]
        tfin = self.dve_fence()
        for d in range(8):
            for (bslot, g) in ((0, d), (1, 15 - d)):
                for hl in range(2):
                    c.dma("gpsimd",
                          a2a3_out[128 * d + 64 * hl:128 * d + 64 * hl + 64,
                                   512 * bslot:512 * bslot + 512],
                          AOUT[hl][:, 512 * g:512 * g + 512],
                          waits=[("dve", tfin)], sem="sa3")
        return ("sa3", c.cnt["sa3"])

    def join_x(self):
        w = [(sn, self.c.cnt[sn]) for sn in ("sgp0", "sgp1")
             if self.c.cnt.get(sn, 0) > 0]
        for par in (0, 1):
            self.xt_free[par] = self.xt_free[par] + w

    def halo_exchange(self, qkvd, agin, agout, halod):
        c, bk = self.c, self.bk
        for sn in ("shx", "cc1", "shl", "shl2"):
            c.sem(sn)
        r2, r2_w = self.SB["R2"]
        wq = [("sqo", c.cnt["sqo"])]
        for bi in (0, 1):
            for i in range(4):
                col = 512 * bi + 508 + i
                c.dma("gpsimd", agin[4 * bi + i:4 * bi + i + 1, :],
                      qkvd[:, col:col + 1].rearrange("a b -> b a"),
                      waits=wq, sem="shx",
                      allow_slow_non_contiguous=True)
        grp = getattr(self, "cc_group", list(range(NC)))
        c.emit("gpsimd", lambda q: q.collective_compute(
            "AllGather", ALU.bypass, replica_groups=[grp],
            ins=[agin[:, :]], outs=[agout[0:8 * len(grp), :]]),
            waits=[("shx", c.cnt["shx"])], inc="cc1")
        XT1B = self.XT[1][:, :].bitcast(BF16)
        AGS = XT1B[0:64, 0:1536]
        HSB = XT1B[:, 1536:1632]
        wx = self.xt_free[1] + [("cc1", c.cnt["cc1"])]
        self.xt_free[1] = []
        tl = ("shl", c.dma("sync", AGS, agout[:, :], waits=wx, sem="shl"))
        w2 = bk.write_waits(2)
        tmm = None
        for mt in range(12):
            tmm = c.emit("tensor", lambda q, mt=mt, AGS=AGS: q.matmul(
                bk.ap(2)[:, 8 * mt:8 * mt + 8],
                AGS[:, 128 * mt:128 * mt + 128], r2[:, :],
                start=True, stop=True),
                waits=(w2 + [tl, r2_w]) if mt == 0 else [],
                inc="pe" if mt == 11 else None)
        tpf = self.pe_fence()
        thc = c.emit("vector", lambda q, HSB=HSB: q.tensor_copy(
            out=HSB, in_=bk.ap(2)[:, 0:96]), waits=[("pe", tpf)], inc="dve")
        bk.read_done(2, ("dve", thc))
        thf = self.dve_fence()
        for mt in range(12):
            c.dma("gpsimd", halod[128 * mt:128 * mt + 128, :],
                  HSB[:, 8 * mt:8 * mt + 8], waits=[("dve", thf)],
                  sem="shl2")
        self.xt_free[1] = [("shl2", c.cnt["shl2"])]
        self.qkv_area_free = self.qkv_area_free + [("shl2", c.cnt["shl2"])]

    def kv_allgather(self, kvind, kvgd):
        c = self.c
        c.sem("cc2")
        grp = getattr(self, "cc_group", list(range(NC)))
        c.emit("gpsimd", lambda q: q.collective_compute(
            "AllGather", ALU.bypass, replica_groups=[grp],
            ins=[kvind[:, :]], outs=[kvgd[0:512 * len(grp), :]]),
            waits=[("sa2", c.cnt["sa2"])], inc="cc2")
        return ("cc2", c.cnt["cc2"])

    def attention2(self, qloc, kvg, attd, extra_w=()):
        c, bk = self.c, self.bk
        for sn in ("sat", "sa30", "sa31"):
            c.sem(sn)
        QSB = self.O[0:64, 0:16384]
        KSB = self.XLT[0:64, 0:8192]
        VF = self.XLT[0:64, 8192:16384]
        XT0B = self.XT[0][:, :].bitcast(BF16)
        XT1B = self.XT[1][:, :].bitcast(BF16)
        VAUG = XT0B[:, 0:4160]
        PBUF = [XT0B[:, 4160 + 1024 * i:5184 + 1024 * i] for i in range(3)]
        PBUF.append(XT1B[:, 2304:3328])
        NUMB = XT0B[0:64, 7232:7744]
        AOS = [XT1B[0:64, 3328 + 512 * i:3840 + 512 * i] for i in range(2)]
        ident, ident_t = self.SB["IDENT"]
        pw = [("pe", c.cnt["pe"]), ("dve", c.cnt["dve"]),
              ("act", c.cnt["act"])] + self.qkv_area_free + \
            self.xlt_readers + list(extra_w)
        self.qkv_area_free = []
        self.xlt_readers = []
        c.sem("scat")
        thr = self.XT[1][:, 0:128]
        iota = XT1B[:, 256:768]
        c.dma("sync", thr, self.din("THR"),
              waits=pw + self.xt_free[1], sem="scat")
        self.xt_free[1] = []
        c.dma("sync", iota, self.din("IOTA"), waits=pw, sem="scat")
        thr_w = ("scat", c.cnt["scat"])
        iota_w = thr_w
        for ccq in range(8):
            for hh in range(2):
                c.dma("sync",
                      QSB[:, 1024 * (2 * ccq + hh):1024 * (2 * ccq + hh) +
                          1024],
                      qloc[128 * ccq + 64 * hh:128 * ccq + 64 * hh + 64, :],
                      waits=pw, sem="sat")
        tq = ("sat", c.cnt["sat"])
        tva = c.emit("vector", lambda q: q.memset(VAUG, 1.0), waits=pw,
                     inc="dve")
        pv_hist = {0: [], 1: [], 2: [], 3: []}
        last_tao = []
        hist_sa3 = {0: [], 1: []}
        gi = 0
        base_scope = c.scope
        for kvh in range(4):
            c.scope = f"{base_scope}_kv{kvh}"
            kw = [("pe", c.cnt["pe"]), ("dve", c.cnt["dve"]),
                  ("act", c.cnt["act"])]
            for g in range(16):
                so, sbi = (g, 0) if g < 8 else (15 - g, 1)
                c.dma("sync", KSB[:, 512 * g:512 * g + 512],
                      kvg[512 * so + 64 * kvh:512 * so + 64 * kvh + 64,
                          512 * sbi:512 * sbi + 512], waits=kw, sem="sat")
                c.dma("sync", VF[:, 512 * g:512 * g + 512],
                      kvg[512 * so + 256 + 64 * kvh:
                          512 * so + 256 + 64 * kvh + 64,
                          512 * sbi:512 * sbi + 512], waits=kw, sem="sat")
            tat = ("sat", c.cnt["sat"])
            lastT = {}
            for kc in range(65):
                if kc < 64:
                    bank = 6 + (kc % 2)
                    w = bk.write_waits(bank) + [tat, ident_t]
                    lastT[kc] = c.emit("tensor", lambda q, kc=kc, bank=bank:
                                       q.transpose(
                        bk.ap(bank)[:, 0:64], VF[:, 128 * kc:128 * kc + 128],
                        ident[0:64, 0:64]), waits=w, inc="pe")
                else:
                    lastT[kc] = self.pe_fence(bank=6)
                if kc >= 1:
                    pb = 6 + ((kc - 1) % 2)
                    tev = c.emit("vector", lambda q, kc=kc, pb=pb:
                                 q.tensor_copy(
                        out=VAUG[:, 65 * (kc - 1):65 * (kc - 1) + 64],
                        in_=bk.ap(pb)[:, 0:64]),
                        waits=[("pe", lastT[kc]), ("dve", tva)], inc="dve")
                    bk.read_done(pb, ("dve", tev))
            tvf = self.dve_fence()
            c.scope = f"{base_scope}_mn{kvh}"
            for hq in range(4):
                h = 4 * kvh + hq
                for bslot in range(2):
                    nkc = 32 if bslot == 0 else 64
                    npair = nkc // 2
                    ob = 2 + (gi % 2)
                    wo = bk.write_waits(ob)
                    exp_t = {}
                    for pi in range(npair + 2):
                        if pi < npair:
                            lo = 4 if pi % 2 == 0 else 0
                            tqk = None
                            for half in (0, 1):
                                kc = 2 * pi + half
                                bank = lo + half
                                ws = bk.write_waits(bank)
                                if pi == 0 and half == 0:
                                    ws = ws + [tq, tat, ("dve", tvf)]
                                tqk = c.emit("tensor", lambda q, kc=kc,
                                             bank=bank, h=h, bslot=bslot:
                                             q.matmul(
                                    bk.ap(bank)[:, :],
                                    KSB[:, 128 * kc:128 * kc + 128],
                                    QSB[:, 1024 * h + 512 * bslot:
                                        1024 * h + 512 * bslot + 512],
                                    start=True, stop=True), waits=ws,
                                    inc="pe")
                            par = pi % 4
                            wex = [("pe", tqk),
                                   ("dve", self.t_pre_dve)] + pv_hist[par]
                            pv_hist[par] = []
                            te = c.emit("scalar", lambda q, lo=lo, par=par:
                                        q.activation(
                                out=PBUF[par][:, :], in_=bk.pair_ap(lo),
                                func=AF.Exp, scale=0.125,
                                bias=self.EXPB[:, 0:1]),
                                waits=wex, inc="act")
                            bk.read_done(lo, ("act", te))
                            bk.read_done(lo + 1, ("act", te))
                            if bslot == 1 and pi < 16:
                                exp_t[pi] = ("exp", te)
                            else:
                                for half in (0, 1):
                                    kc = 2 * pi + half
                                    tcol = 64 * bslot + kc
                                    c.emit("vector", lambda q, par=par,
                                           tcol=tcol, half=half:
                                           q.scalar_tensor_tensor(
                                        out=PBUF[par][:, 512 * half:
                                                      512 * half + 512],
                                        in0=iota[:, :],
                                        scalar=thr[:, tcol:tcol + 1],
                                        in1=PBUF[par][:, 512 * half:
                                                      512 * half + 512],
                                        op0=ALU.is_ge, op1=ALU.mult),
                                        waits=[("act", te), iota_w, thr_w],
                                        inc="dve")
                                exp_t[pi] = ("dve", self.dve_fence())
                        if pi >= 2:
                            pp = pi - 2
                            par = pp % 4
                            ge = exp_t[pp]
                            if ge[0] == "exp":
                                nx = exp_t[pp + 1]
                                ge = ("act", nx[1]) if nx[0] == "exp" \
                                    else nx
                            for half in (0, 1):
                                pkc = 2 * pp + half
                                wpv = [ge] + \
                                    (wo if pkc == 0 else [])
                                tpv = c.emit("tensor", lambda q, pkc=pkc,
                                             par=par, ob=ob, half=half,
                                             nkc=nkc: q.matmul(
                                    bk.ap(ob)[0:65, :],
                                    VAUG[:, 65 * pkc:65 * pkc + 65],
                                    PBUF[par][:, 512 * half:
                                              512 * half + 512],
                                    start=(pkc == 0),
                                    stop=(pkc == nkc - 1)),
                                    waits=wpv, inc="pe")
                                pv_hist[par].append(("pe", tpv))
                    tpf = self.pe_fence()
                    tdc = c.emit("vector", lambda q, ob=ob: q.tensor_copy(
                        out=self.SCR[0:1, :], in_=bk.ap(ob)[64:65, :]),
                        waits=[("pe", tpf)], inc="dve")
                    c.emit("vector", lambda q: q.reciprocal(
                        out=self.SCR[0:1, :], in_=self.SCR[0:1, :]),
                        inc="dve")
                    tfd = self.dve_fence()
                    wb = bk.write_waits(4)
                    tbc = c.emit("tensor", lambda q: q.matmul(
                        bk.ap(4)[0:64, :], self.ONESF[0:1, 0:64],
                        self.SCR[0:1, :], start=True, stop=True),
                        waits=wb + [("dve", tfd), ("dve", self.t_onesf)],
                        inc="pe")
                    tnm = c.emit("scalar", lambda q, ob=ob: q.activation(
                        out=NUMB[:, :], in_=bk.ap(ob)[0:64, :],
                        func=AF.Copy),
                        waits=[("pe", tbc)] + last_tao, inc="act")
                    bk.read_done(ob, ("act", tnm))
                    bk.read_done(ob, ("dve", tdc))
                    taf = self.act_fence()
                    par2 = gi % 2
                    tao = c.emit("vector", lambda q, par2=par2:
                                 q.tensor_tensor(
                        out=AOS[par2][:, :], in0=NUMB[:, :],
                        in1=bk.ap(4)[0:64, :], op=ALU.mult),
                        waits=[("act", taf)] + hist_sa3[par2], inc="dve")
                    bk.read_done(4, ("dve", tao))
                    last_tao = [("dve", tao)]
                    tfo = self.dve_fence()
                    ts3 = c.dma("scalar",
                                attd[64 * h:64 * h + 64,
                                     512 * bslot:512 * bslot + 512],
                                AOS[par2][:, :], waits=[("dve", tfo)],
                                sem=f"sa3{par2}")
                    hist_sa3[par2] = [(f"sa3{par2}", ts3)]
                    gi += 1
        c.scope = base_scope
        self.o_readers = []
        self.xlt_readers = [("pe", c.cnt["pe"])]
        self.xt_free[0] += [("sa30", c.cnt["sa30"]), ("sa31", c.cnt["sa31"])]
        self.xt_free[1].append(("dve", c.cnt["dve"]))
        return [("sa30", c.cnt["sa30"]), ("sa31", c.cnt["sa31"])]

    def arm_wo(self, xsrc, xdst, final=False, att_dram=None, extra_w=()):
        c, bk = self.c, self.bk
        if att_dram is None:
            att_dram = self.din("ATTD", [1024, 1024], BF16)
        c.sem("sai")
        ATT = self.XLT[:, 0:8192]
        pw = [("pe", c.cnt["pe"]), ("dve", c.cnt["dve"]),
              ("act", c.cnt["act"])] + self.xlt_readers + list(extra_w)
        self.xlt_readers = []
        for cc in range(8):
            c.dma("sync", ATT[:, 1024 * cc:1024 * cc + 1024],
                  att_dram[128 * cc:128 * cc + 128, :], waits=pw, sem="sai")
        ti = ("sai", c.cnt["sai"])
        wo2t = self.din("WO2T")
        for b in (0, 1):
            def src_fn(cc, j, b=b):
                return ATT[:, 1024 * cc + 512 * b + 128 * j:
                           1024 * cc + 512 * b + 128 * j + 128]
            evac = self.proj_tm(None, 8, wo2t, [[ti]] * NJ, src_fn=src_fn)
            self.epilogue_block(b, xsrc, xdst, evac, final=final)

    def arm_l1_ffn(self, xsrc, xdst, final=False):
        c, bk = self.c, self.bk
        kert = self.din("KERT")
        gates = self.din("GATES")
        l1a = self.din("L1A")
        l1bt = self.din("L1BT")
        l1astp = self.din("L1ASTP")
        l1bst = self.din("L1BST")
        rowp, rowp_t = self.SB["ROWP"]
        i32s, i32s_w = self.SB["I32S"]
        e4128, e4128_w = self.SB["E4128"]
        for sn in ("sga", "sls", "sbs"):
            c.sem(sn)
        stg_free = [("dve", self.t_pre_dve)]

        def rw_manual(src2d, extra=()):
            bi = self.rwi % 2
            self.rwi += 1
            t = c.dma("sync", self.RW[bi][:, :], src2d,
                      waits=self.rw_cons[bi] + list(extra), sem=f"srw{bi}")
            self.rw_cons[bi] = []
            self.rw_load[bi] = (f"srw{bi}", t)
            return bi

        def mw_manual(src2d, extra=()):
            bi = self.mwi % 2
            self.mwi += 1
            t = c.dma("sync", self.MWB[bi][:, :], src2d,
                      waits=self.mw_cons[bi] + list(extra), sem=f"smw{bi}")
            self.mw_cons[bi] = []
            self.mw_load[bi] = (f"smw{bi}", t)
            return bi

        def load4(dram):
            bs = [rw_manual(dram[0:128, :]), rw_manual(dram[128:256, :]),
                  mw_manual(dram[256:384, :]), mw_manual(dram[384:512, :])]
            bufs = [self.RW[bs[0]], self.RW[bs[1]],
                    self.MWB[bs[2]], self.MWB[bs[3]]]
            lds = [self.rw_load[bs[0]], self.rw_load[bs[1]],
                   self.mw_load[bs[2]], self.mw_load[bs[3]]]

            def done(t):
                self.rw_cons[bs[0]].append(t)
                self.rw_cons[bs[1]].append(t)
                self.mw_cons[bs[2]].append(t)
                self.mw_cons[bs[3]].append(t)
            return bufs, lds, done

        for b in (0, 1):
            ready = self.preln_block(xsrc, b)
            l1bst_sb = self.XLN[0:32, :]
            l1bst_w = ("sbs", c.dma("sync", l1bst_sb, l1bst,
                                    waits=[("pe", c.cnt["pe"])] +
                                    self.xln_readers, sem="sbs"))
            XT0B = self.XT[0][:, :].bitcast(BF16)
            XT1B = self.XT[1][:, :].bitcast(BF16)
            B1R = XT1B[0:1, 1024:5248]
            G512 = XT0B[:, 0:128]
            AET = XT0B[:, 128:1152]
            AXT = XT0B[0:32, 1152:1664]
            AXST = XT0B[0:32, 1664:2176]
            BET = XT0B[0:32, 2176:6272]
            ASTS = XT1B[:, 0:1024]
            xtw0 = self.xt_free[0]
            self.xt_free[0] = []
            xtw1 = self.xt_free[1]
            self.xt_free[1] = []
            for kc in range(4):
                c.dma("sync", self.STG[0:4, kc:kc + 1],
                      gates[b:b + 1, 4 * kc:4 * kc + 4]
                      .rearrange("a b -> b a"), waits=stg_free, sem="sga")
            tgw = ("sga", c.cnt["sga"])
            GB = self.STG[0:4, 8:10].bitcast(BF16)[:, 0:4]
            c.emit("vector", lambda q, GB=GB: q.tensor_copy(
                out=GB, in_=self.STG[0:4, 0:4]),
                waits=[tgw] + stg_free, inc="dve")
            tcvf = self.dve_fence()
            w2 = bk.write_waits(2)
            tex = c.emit("tensor", lambda q, GB=GB: q.matmul(
                bk.ap(2)[:, 0:4], e4128[:, :], GB, start=True, stop=True),
                waits=w2 + [("dve", tcvf), e4128_w], inc="pe")
            stg_free = [("pe", tex), ("dve", tcvf)]
            tpf = self.pe_fence()
            tge = c.emit("vector", lambda q: q.tensor_copy(
                out=self.SCR[:, 0:4], in_=bk.ap(2)[:, 0:4]),
                waits=[("pe", tpf)], inc="dve")
            bk.read_done(2, ("dve", tge))
            tgc = self.dve_fence()
            for kc in range(4):
                c.emit("vector", lambda q, kc=kc, G512=G512: q.tensor_scalar(
                    out=G512[:, 32 * kc:32 * kc + 32], in0=i32s[:, :],
                    scalar1=self.SCR[:, kc:kc + 1], scalar2=0.0,
                    op0=ALU.mult, op1=ALU.add),
                    waits=([("dve", tgc), i32s_w] + xtw0) if kc == 0 else [],
                    inc="dve")
            tg5 = self.dve_fence()
            tas = ("sls", c.dma("sync", ASTS[:, :], l1astp, waits=xtw1,
                                sem="sls"))
            tb1 = ("sls", c.dma("sync", B1R, self.din("BIAS1R"),
                                waits=[tas], sem="sls"))
            bufs, lds, done4 = load4(l1a)
            tlast = None
            for ic in range(32):
                bank = 4 + ic // 16
                for kc in range(4):
                    w = []
                    if kc == 0 and (ic % 16) == 0:
                        w += bk.write_waits(bank) + [("dve", tg5)]
                    if ic == 0:
                        w = w + [lds[kc]]
                    last = (ic == 31 and kc == 3)
                    t = c.emit("tensor", lambda q, ic=ic, kc=kc,
                               bank=bank, buf=bufs[kc]: q.matmul(
                        bk.ap(bank)[:, 32 * (ic % 16):32 * (ic % 16) + 32],
                        buf[:, 128 * ic:128 * ic + 128],
                        G512[:, 32 * kc:32 * kc + 32],
                        start=(kc == 0), stop=(kc == 3)),
                        waits=w, inc="pe" if last else None)
                    if last:
                        tlast = t
            done4(("pe", tlast))
            tpfA = self.pe_fence()
            tae0 = c.emit("vector", lambda q, AET=AET: q.tensor_copy(
                out=AET[:, 0:512], in_=bk.ap(4)[:, :]),
                waits=[("pe", tpfA)] + xtw0, inc="dve")
            tae1 = c.emit("vector", lambda q, AET=AET: q.tensor_copy(
                out=AET[:, 512:1024], in_=bk.ap(5)[:, :]), inc="dve")
            bk.read_done(4, ("dve", tae0))
            bk.read_done(5, ("dve", tae1))
            taet = self.dve_fence()
            w2 = bk.write_waits(2)
            w3 = bk.write_waits(3)
            t3_close = None
            for j in range(NJ):
                for cc in range(NT):
                    w = []
                    if cc == 0:
                        w = list(ready[j]) + (
                            (w2 + [("dve", taet)]) if j == 0 else [])
                    c.emit("tensor", lambda q, j=j, cc=cc, AET=AET: q.matmul(
                        bk.ap(2)[0:32, 128 * j:128 * j + 128],
                        AET[:, 32 * cc:32 * cc + 32],
                        self.XLT[:, 512 * cc + 128 * j:
                                 512 * cc + 128 * j + 128],
                        start=(cc == 0), stop=(cc == NT - 1)),
                        waits=w, inc=None)
                for cc in range(NT):
                    w = []
                    if cc == 0:
                        w = (w3 + [tas]) if j == 0 else []
                    last = (j == NJ - 1 and cc == NT - 1)
                    t = c.emit("tensor", lambda q, j=j, cc=cc, ASTS=ASTS:
                               q.matmul(
                        bk.ap(3)[0:32, 128 * j:128 * j + 128],
                        ASTS[:, 32 * cc:32 * cc + 32],
                        self.XLT[:, 512 * cc + 128 * j:
                                 512 * cc + 128 * j + 128],
                        start=(cc == 0), stop=(cc == NT - 1)),
                        waits=w, inc="pe" if last else None)
                    if last:
                        t3_close = t
            tpx = self.pe_fence()
            tx1 = c.emit("vector", lambda q, AXT=AXT: q.tensor_copy(
                out=AXT, in_=bk.ap(2)[0:32, 0:512]),
                waits=[("pe", tpx)], inc="dve")
            tx2 = c.emit("vector", lambda q, AXST=AXST: q.tensor_copy(
                out=AXST, in_=bk.ap(3)[0:32, 0:512]), inc="dve")
            bk.read_done(2, ("dve", tx1))
            bk.read_done(3, ("dve", tx2))
            tax = self.dve_fence()
            bufs, lds, done4 = load4(l1bt)
            close = {}
            tevs = []
            for oc in range(9):
                if oc < 8:
                    bank = 4 + (oc % 2)
                    for kc in range(4):
                        w = []
                        if kc == 0:
                            w += bk.write_waits(bank)
                            if oc == 0:
                                w.append(("dve", tg5))
                        if oc == 0:
                            w.append(lds[kc])
                        t = c.emit("tensor", lambda q, oc=oc, kc=kc,
                                   bank=bank, buf=bufs[kc], G512=G512:
                                   q.matmul(
                            bk.ap(bank)[0:32, :],
                            G512[:, 32 * kc:32 * kc + 32],
                            buf[:, 512 * oc:512 * oc + 512],
                            start=(kc == 0), stop=(kc == 3)),
                            waits=w, inc="pe" if kc == 3 else None)
                    close[oc] = t
                if oc >= 1:
                    po = oc - 1
                    bank = 4 + (po % 2)
                    guard = close[oc] if oc < 8 else self.pe_fence()
                    tev = c.emit("vector", lambda q, po=po, bank=bank,
                                 BET=BET: q.tensor_copy(
                        out=BET[:, 512 * po:512 * po + 512],
                        in_=bk.ap(bank)[0:32, :]),
                        waits=[("pe", guard)] + (xtw0 if po == 0 else []),
                        inc="dve")
                    bk.read_done(bank, ("dve", tev))
            done4(("pe", close[7]))
            tbet = self.dve_fence()

            def extras(og, j, tax=tax, tbet=tbet, AXT=AXT, AXST=AXST,
                       BET=BET):
                return [
                    (lambda q, ps, stop, og=og, j=j: q.matmul(
                        ps, AXT[:, 128 * j:128 * j + 128],
                        BET[:, 512 * og:512 * og + 512],
                        start=False, stop=False),
                     [("dve", tax), ("dve", tbet)]),
                    (lambda q, ps, stop, og=og, j=j: q.matmul(
                        ps, AXST[:, 128 * j:128 * j + 128],
                        l1bst_sb[0:32, 512 * og:512 * og + 512],
                        start=False, stop=stop),
                     [l1bst_w]),
                ]
            evac = self.proj_tm(self.XLT, NT, kert, ready, extras=extras,
                                bias_row=(B1R[:, 0:128], B1R[:, 128:],
                                          tb1))
            self.xt_free[0] = [("pe", c.cnt["pe"]), ("dve", c.cnt["dve"])]
            self.xt_free[1] = [("pe", c.cnt["pe"])]
            self.xln_readers.append(("pe", c.cnt["pe"]))
            self.epilogue_block(b, xsrc, xdst, evac, final=final)



ARM_KEYS = {
    "l0_attn": ["IDENT", "CBIAS"],
    "l0_ffn": ["IDENT", "IDENT32", "ROWP", "E432"],
    "qkv": ["IDENT"],
    "conv": ["IDENT", "CBIAS"],
    "attn": ["IDENT"],
    "wo": ["IDENT"],
    "l1_ffn": ["IDENT", "ROWP", "I32S", "E4128"],
}


def _build_arm(shared, percore0, arm):
    nc = bass.Bass("TRN2", target_bir_lowering=False, debug=False)
    K = Kern(nc, shared, percore0, fused=False)
    xsrc = K.din("XIN", [TPC, D], F32)
    xout = K.dout("XOUT", [TPC, D], F32)
    K.preamble(keys=ARM_KEYS.get(arm))
    getattr(K, "arm_" + arm)(xsrc, xout)
    K.c.finalize()
    K.c.es.close()
    return nc, K


def _run_arm(shared, percore, arm, xin_list):
    nc, K = _build_arm(shared, percore[0], arm)
    in_maps = []
    for ci in range(NC):
        m = {}
        for key in K.in_keys:
            if key == "XIN":
                m["i_xin"] = np.ascontiguousarray(xin_list[ci])
            elif key in percore[ci]:
                m["i_" + key.lower()] = percore[ci][key]
            else:
                m["i_" + key.lower()] = shared[key]
        in_maps.append(m)
    res = run_bass_kernel_spmd(nc, in_maps, core_ids=list(range(NC)))
    return [r["o_xout"] for r in res.results]



def _run_phase(shared, percore, build_fn, extra_fn, out_keys):
    nc = bass.Bass("TRN2", target_bir_lowering=False, debug=False)
    K = Kern(nc, shared, percore[0], fused=False)
    build_fn(K)
    K.c.finalize()
    K.c.es.close()
    in_maps = []
    for ci in range(NC):
        m = {}
        extra = extra_fn(ci)
        for key in K.in_keys:
            if key in extra:
                m["i_" + key.lower()] = np.ascontiguousarray(extra[key])
            elif key in percore[ci]:
                m["i_" + key.lower()] = percore[ci][key]
            else:
                m["i_" + key.lower()] = shared[key]
        in_maps.append(m)
    res = run_bass_kernel_spmd(nc, in_maps, core_ids=list(range(NC)))
    return [[np.asarray(r[k]) for k in out_keys] for r in res.results]


def _owner(g):
    return (g, 0) if g < 8 else (15 - g, 1)


def kernel_unfused(**inputs):
    inp = {k: np.asarray(v) for k, v in inputs.items()}
    shared, percore = _prep(inp)
    x = [percore[c]["X0"] for c in range(NC)]

    x = _run_arm(shared, percore, "l0_attn", x)
    x = _run_arm(shared, percore, "l0_ffn", x)

    def b_qkv(K):
        xsrc = K.din("XIN", [TPC, D], F32)
        qkvd = K.dout("QKV", [1536, 1024], BF16)
        K.preamble(keys=ARM_KEYS["qkv"])
        for b in (0, 1):
            K.qkv_block(xsrc, b, qkvd)
    qkv = _run_phase(shared, percore, b_qkv,
                     lambda ci: {"XIN": x[ci]}, ["o_qkv"])
    qkv = [r[0] for r in qkv]

    halos = []
    for ci in range(NC):
        ga, gb = _blocks_of_core(ci)
        h = np.zeros((1536, 8), qkv[0].dtype)
        for mslot, g in ((0, ga), (4, gb)):
            if g == 0:
                continue
            c2, bi2 = _owner(g - 1)
            h[:, mslot:mslot + 4] = qkv[c2][:, 512 * bi2 + 508:512 * bi2 + 512]
        halos.append(h)

    def b_conv(K):
        qkvd = K.din("QKVI", [1536, 1024], BF16)
        halod = K.din("HALO", [1536, 8], BF16)
        qlocd = K.dout("QLOC", [1024, 1024], BF16)
        kvind = K.dout("KVIN", [512, 1024], BF16)
        K.preamble(keys=ARM_KEYS["conv"])
        K.load_conv_consts()
        for b in (0, 1):
            K.conv_block(b, qkvd, halod, None, qloc=qlocd, kvin=kvind)
    cv = _run_phase(shared, percore, b_conv,
                    lambda ci: {"QKVI": qkv[ci], "HALO": halos[ci]},
                    ["o_qloc", "o_kvin"])
    qloc = [r[0] for r in cv]
    kvg = np.concatenate([r[1] for r in cv])

    def b_attn(K):
        qlocd = K.din("QLOC", [1024, 1024], BF16)
        kvgd = K.din("KVG", [4096, 1024], BF16)
        attd = K.dout("ATTD", [1024, 1024], BF16)
        K.preamble(keys=ARM_KEYS["attn"])
        K.attention2(qlocd, kvgd, attd)
    at = _run_phase(shared, percore, b_attn,
                    lambda ci: {"QLOC": qloc[ci], "KVG": kvg}, ["o_attd"])

    for ci in range(NC):
        percore[ci]["ATTD"] = at[ci][0]

    x = _run_arm(shared, percore, "wo", x)
    x = _run_arm(shared, percore, "l1_ffn", x)

    out = np.zeros((1, S, D), np.float32)
    for ci in range(NC):
        ga, gb = _blocks_of_core(ci)
        out[0, 512 * ga:512 * ga + 512] = x[ci][0:512]
        out[0, 512 * gb:512 * gb + 512] = x[ci][512:1024]
    return out


def _build_fused(shared, percore0, ngroup=NC):
    nc = bass.Bass("TRN2", target_bir_lowering=False, debug=False)
    K = Kern(nc, shared, percore0, fused=True)
    K.cc_group = list(range(ngroup))
    xin = K.din("XIN", [TPC, D], F32)
    xout = K.dout("XOUT", [TPC, D], F32)
    x1 = K.dint("X1", [TPC, D], F32)
    x2 = K.dint("X2", [TPC, D], F32)
    x3 = K.dint("X3", [TPC, D], F32)
    qkvd = K.dint("QKVB", [1536, 1024], BF16)
    agin = K.dint("AGIN", [8, 1536], BF16)
    asp = "Shared" if ngroup == NC else "Local"
    agout = K.dint("AGOUT", [64, 1536], BF16, addr_space=asp)
    halod = K.dint("HALOB", [1536, 8], BF16)
    qlocd = K.dint("QLOCB", [1024, 1024], BF16)
    kvind = K.dint("KVINB", [512, 1024], BF16)
    kvgd = K.dint("KVGB", [4096, 1024], BF16, addr_space=asp)
    attdd = K.dint("ATTDB", [1024, 1024], BF16)
    K.c.scope = "pre"
    K.preamble()
    K.c.scope = "l0_attn"
    K.arm_l0_attn(xin, x1)
    K.join_x()
    K.c.scope = "l0_ffn"
    K.arm_l0_ffn(x1, x2)
    K.join_x()
    K.c.scope = "qkv"
    for b in (0, 1):
        K.qkv_block(x2, b, qkvd)
    K.c.scope = "halo"
    K.halo_exchange(qkvd, agin, agout, halod)
    K.load_conv_consts()
    K.c.scope = "conv"
    for b in (0, 1):
        K.conv_block(b, qkvd, halod, None, qloc=qlocd, kvin=kvind)
    K.c.scope = "attn"
    kvw = K.kv_allgather(kvind, kvgd)
    sa3w = K.attention2(qlocd, kvgd, attdd, extra_w=[kvw])
    K.join_x()
    K.c.scope = "wo"
    K.arm_wo(x2, x3, att_dram=attdd, extra_w=sa3w)
    K.join_x()
    K.c.scope = "l1_ffn"
    K.arm_l1_ffn(x3, xout, final=True)
    K.c.finalize()
    K.c.es.close()
    return nc, K


def kernel(**inputs):
    inp = {k: np.asarray(v) for k, v in inputs.items()}
    shared, percore = _prep(inp)
    nc, K = _build_fused(shared, percore[0])
    in_maps = []
    for ci in range(NC):
        m = {}
        for key in K.in_keys:
            if key == "XIN":
                m["i_xin"] = np.ascontiguousarray(percore[ci]["X0"])
            elif key in percore[ci]:
                m["i_" + key.lower()] = percore[ci][key]
            else:
                m["i_" + key.lower()] = shared[key]
        in_maps.append(m)
    res = run_bass_kernel_spmd(nc, in_maps, core_ids=list(range(NC)))
    out = np.zeros((1, S, D), np.float32)
    for ci in range(NC):
        ga, gb = _blocks_of_core(ci)
        x = res.results[ci]["o_xout"]
        out[0, 512 * ga:512 * ga + 512] = x[0:512]
        out[0, 512 * gb:512 * gb + 512] = x[512:1024]
    return out


# version 09
# speedup vs baseline: 1.0300x; 1.0300x over previous
import contextlib
import numpy as np
import ml_dtypes
import concourse.bass as bass
import concourse.mybir as mybir
from concourse.bass_utils import run_bass_kernel_spmd

F32 = mybir.dt.float32
BF16 = mybir.dt.bfloat16
AF = mybir.ActivationFunctionType
ALU = mybir.AluOpType
AX = mybir.AxisListType.X
BF = ml_dtypes.bfloat16

NC = 8
S = 8192
D = 4096
NBLK = 16
BLK = 512
TPC = 2 * BLK
NT = D // 128
NJ = BLK // 128

LN_EPS = 1e-5
DEN_EPS = 1e-6
PRIOR_EPS = 1e-12

DEINT = list(range(0, 64, 2)) + list(range(1, 64, 2))


def _bf(a):
    return np.ascontiguousarray(np.asarray(a, dtype=np.float32)).astype(BF)


def _f32(a):
    return np.ascontiguousarray(np.asarray(a, dtype=np.float32))


def _ogmajor(wt, og_w=512):
    K, N = wt.shape
    nog = N // og_w
    return np.ascontiguousarray(
        wt.reshape(K, nog, og_w).transpose(1, 0, 2).reshape(nog * K, og_w))


def _blocks_of_core(c):
    return (c, 15 - c)



def _prep(inp):
    shared = {}
    percore = [dict() for _ in range(NC)]

    x0 = _f32(inp["L0_attn_hidden_states"][0])

    sel_heads = list(range(0, 32, 4))
    Wq = _f32(inp["L0_attn_Wq"])
    rows = [r for h in sel_heads for r in range(128 * h, 128 * h + 128)]
    wqsel_t = Wq[rows].T
    shared["WQSELT"] = _bf(
        wqsel_t.reshape(D, 8, 128).transpose(1, 0, 2).reshape(8 * D, 128))
    bq_sel = _f32(inp["L0_attn_bq"])[rows]

    WoT = _f32(inp["L0_attn_Wo"]).T
    shared["WOT"] = _bf(_ogmajor(WoT))
    shared["M0"] = _f32(inp["L0_attn_M"])
    shared["Z0"] = _f32(inp["L0_attn_z"])

    act = np.asarray(inp["L0_ffn_active"]).astype(np.int64)
    act = np.where(act < 0, act + 16, act)
    act = np.clip(act, 0, 15)
    A = _f32(inp["L0_ffn_lora_A"])[act]
    Bm = _f32(inp["L0_ffn_lora_B"])[act]
    Rw = _f32(inp["L0_ffn_router_W"])[act]
    w_moe = np.concatenate([A.reshape(32, D), Rw], 0)
    shared["WMT"] = _bf(w_moe.T)
    shared["BMT"] = _bf(Bm.transpose(0, 2, 1).reshape(32, D))
    shared["RBSEL"] = _f32(inp["L0_ffn_router_b"])[act].reshape(1, 4)
    shared["PRSEL"] = _f32(inp["L0_ffn_domain_prior"])[act].reshape(1, 4)
    shared["WBT"] = _bf(_ogmajor(_f32(inp["L0_ffn_W_base"]).T))

    qperm = [64 * h + d for h in range(16) for d in DEINT]
    kperm = [1024 + 64 * h + d for h in range(4) for d in DEINT]
    vrows = list(range(1280, 1536))
    w_qkv_p = _f32(inp["L1_attn_w_qkv"])[qperm + kperm + vrows]
    shared["WQKVT"] = _bf(
        w_qkv_p.T.reshape(D, 12, 128).transpose(1, 0, 2).reshape(12 * D, 128))

    qc1 = _f32(inp["L1_attn_qc1_w"])
    qc1p = qc1[[64 * h + d for h in range(16) for d in DEINT]][:, DEINT, :]
    shared["QC1T"] = _bf(np.ascontiguousarray(
        qc1p.reshape(16, 64, 64, 3).transpose(3, 0, 2, 1)).reshape(3 * 16 * 64, 64))
    qc2 = _f32(inp["L1_attn_qc2_w"])
    qperm2 = [64 * h + d for h in range(16) for d in DEINT]
    qc2p = qc2[qperm2][:, qperm2, :]
    shared["QC2T"] = _bf(np.ascontiguousarray(
        qc2p.transpose(2, 1, 0).reshape(3, 1024, 8, 128).transpose(2, 0, 1, 3)
    ).reshape(8 * 3 * 1024, 128))
    kc1 = _f32(inp["L1_attn_kc1_w"])
    kperm1 = [64 * h + d for h in range(4) for d in DEINT]
    kc1p = kc1[kperm1][:, DEINT, :]
    shared["KC1T"] = _bf(np.ascontiguousarray(
        kc1p.reshape(4, 64, 64, 3).transpose(3, 0, 2, 1)).reshape(3 * 4 * 64, 64))
    kc2 = _f32(inp["L1_attn_kc2_w"])
    kc2p = kc2[kperm1][:, kperm1, :]
    shared["KC2T"] = _bf(np.ascontiguousarray(
        kc2p.transpose(2, 1, 0).reshape(3, 256, 2, 128).transpose(2, 0, 1, 3)
    ).reshape(2 * 3 * 256, 128))

    bias_vecs = [bq_sel,
                 _f32(inp["L1_attn_qc1_b"])[qperm2],
                 _f32(inp["L1_attn_qc2_b"])[qperm2],
                 _f32(inp["L1_attn_kc1_b"])[kperm1],
                 _f32(inp["L1_attn_kc2_b"])[kperm1]]
    cb = np.concatenate([v.reshape(-1, 128).T for v in bias_vecs], axis=1)
    shared["CBIAS"] = _f32(cb)
    shared["KEYT"] = _f32(inp["L1_attn_key_temp"]).reshape(1, 1)

    shared["WO2T"] = _bf(_ogmajor(_f32(inp["L1_attn_w_o"]).T))

    shared["KERT"] = _bf(_ogmajor(_f32(inp["L1_ffn_kernel"])))
    rowp = np.zeros((1, 128 + D), np.float32)
    rowp[0, 0:128] = 1.0
    rowp[0, 128:] = _f32(inp["L0_ffn_b_base"])
    shared["ROWP"] = _bf(rowp)
    b1r = np.zeros((1, 128 + D), np.float32)
    b1r[0, 0:128] = 1.0
    b1r[0, 128:] = _f32(inp["L1_ffn_bias"])
    shared["BIAS1R"] = _bf(b1r)
    shared["L1A"] = _bf(_f32(inp["L1_ffn_lora_A"]).reshape(512, D))
    shared["L1BT"] = _bf(
        _f32(inp["L1_ffn_lora_B"]).transpose(0, 2, 1).reshape(512, D))
    ast = _f32(inp["L1_ffn_lora_A_shared"]).T
    shared["L1ASTP"] = _bf(np.ascontiguousarray(
        ast.reshape(32, 128, 32).transpose(1, 0, 2)).reshape(128, 1024))
    shared["L1BST"] = _bf(_f32(inp["L1_ffn_lora_B_shared"]).T)
    gate = _f32(inp["L1_ffn_gate"])

    shared["IDENT"] = _bf(np.eye(128))
    shared["IDENT32"] = _f32(np.eye(128))
    mb = np.zeros((128, 896), np.float32)
    for p in range(128):
        mb[p, p + 384:] = 1.0
    shared["MBIG"] = _bf(mb)
    qms = np.zeros((128, 8 * 64), np.float32)
    kms = np.zeros((128, 2 * 64), np.float32)
    for p in range(128):
        for c in range(8):
            qms[p, 64 * c + (p % 64)] = 1.0 / 16.0
        for c in range(2):
            kms[p, 64 * c + (p % 64)] = 0.25
    shared["QMS"] = _bf(qms)
    shared["KMS"] = _bf(kms)
    hs = np.zeros((128, 8 * 16), np.float32)
    khs = np.zeros((128, 2 * 4), np.float32)
    for p in range(128):
        for c in range(8):
            hs[p, 16 * c + 2 * c + p // 64] = 1.0
        for c in range(2):
            khs[p, 4 * c + 2 * c + p // 64] = 1.0
    shared["HSQ"] = _bf(hs)
    shared["HSK"] = _bf(khs)
    exq = np.zeros((16, 8 * 128), np.float32)
    exk = np.zeros((4, 2 * 128), np.float32)
    for c in range(8):
        for p in range(128):
            exq[2 * c + p // 64, 128 * c + p] = 1.0
    for c in range(2):
        for p in range(128):
            exk[2 * c + p // 64, 128 * c + p] = 1.0
    shared["EXQ"] = _f32(exq)
    shared["EXK"] = _f32(exk)
    perm = np.zeros((128, 128), np.float32)
    for p in range(128):
        perm[64 * (p // 64) + (p % 64 + 32) % 64, p] = 1.0
    shared["PERMSW"] = _bf(perm)
    i32 = np.zeros((128, 32), np.float32)
    for p in range(128):
        i32[p, p % 32] = 1.0
    shared["I32S"] = _bf(i32)
    e4 = np.zeros((4, 128), np.float32)
    for p in range(128):
        e4[p // 32, p] = 1.0
    shared["E4128"] = _bf(e4)
    e432 = np.zeros((4, 32), np.float32)
    for m in range(32):
        e432[m // 8, m] = 1.0
    shared["E432"] = _f32(e432)
    shared["IOTA"] = _bf(np.broadcast_to(
        np.arange(512, dtype=np.float32), (128, 512)).copy())

    inv = (1.0 / 10000.0 ** (np.arange(0, 64, 2, dtype=np.float32) / 64.0))

    for c in range(NC):
        ga, gb = _blocks_of_core(c)
        toks = np.concatenate([np.arange(BLK * ga, BLK * ga + BLK),
                               np.arange(BLK * gb, BLK * gb + BLK)])
        percore[c]["X0"] = np.ascontiguousarray(x0[toks])
        ang = toks[None, :].astype(np.float32) * inv[:, None]
        ct = np.cos(ang)
        st = np.sin(ang)
        CT = np.concatenate([ct, ct, ct, ct], axis=0)
        ST = np.concatenate([-st, st, -st, st], axis=0)
        percore[c]["CT"] = _f32(CT)
        percore[c]["ST"] = _f32(ST)
        percore[c]["GATES"] = _f32(gate[[ga // 2, gb // 2]])
        r2 = np.zeros((64, 8), np.float32)
        for m, g in ((0, ga), (4, gb)):
            if g == 0:
                continue
            gp = g - 1
            o = gp if gp <= 7 else 15 - gp
            hf = 0 if gp <= 7 else 1
            for i in range(4):
                r2[8 * o + 4 * hf + i, m + i] = 1.0
        percore[c]["R2"] = _bf(r2)
        percore[c]["HMASK"] = _f32(np.array(
            [[0.0 if ga == 0 else 1.0, 0.0 if gb == 0 else 1.0]]))
        thr = np.zeros((128, 128), np.float32)
        for bslot, g in ((0, ga), (1, gb)):
            for kc in range(64):
                thr[:, 64 * bslot + kc] = (128 * kc +
                                           np.arange(128) - 512 * g)
        percore[c]["THR"] = thr

    return shared, percore



PARANOID = True


class Ctx:
    ENGS = ("sync", "tensor", "vector", "scalar", "gpsimd")

    def __init__(self, nc):
        self.nc = nc
        self.selfser = {"vector": "dve", "scalar": "act", "gpsimd": "pool"}
        self.ops = {e: [] for e in self.ENGS}
        self.es = contextlib.ExitStack()
        self.semh = {}
        self.cnt = {}
        for name in ("sy", "gp", "pe", "act", "dve", "pool"):
            self._addsem(name)
        self.hw = {e: {} for e in self.ENGS}
        self.dram = {}
        self.sb = {}
        self.scope = "init"
        self.scope_by_name = {}

    def _addsem(self, name):
        self.semh[name] = self.es.enter_context(self.nc.semaphore(name))
        self.cnt[name] = 0

    def sem(self, name):
        if name not in self.semh:
            self._addsem(name)
        return name

    def sbuf(self, name, shape, dtype):
        t = self.es.enter_context(self.nc.sbuf_tensor(name, shape, dtype))
        self.sb[name] = t
        return t

    def psum(self, name, shape, dtype):
        return self.es.enter_context(self.nc.psum_tensor(name, shape, dtype))

    def emit(self, eng, fn, waits=(), inc=None, inc_by=1):
        hw = self.hw[eng]
        if PARANOID and eng in self.selfser and inc is None:
            inc = self.selfser[eng]
        if PARANOID and eng in self.selfser:
            sname = self.selfser[eng]
            waits = list(waits) + [(sname, self.cnt[sname])]
        wl = []
        for s, v in waits:
            if v > hw.get(s, 0):
                wl.append((s, v))
                hw[s] = v
        ticket = None
        if inc is not None:
            self.cnt[inc] += inc_by
            ticket = self.cnt[inc]
        semh = self.semh

        nc = self.nc
        scope = self.scope

        sbn = self.scope_by_name

        def replay(q, wl=tuple(wl), fn=fn, inc=inc, inc_by=inc_by,
                   scope=scope):
            with nc.named_scope(scope):
                for s, v in wl:
                    q.wait_ge(semh[s], v)
                ins = fn(q)
                try:
                    sbn[ins.ins.name] = scope
                except Exception:
                    pass
                if inc is not None:
                    ins.then_inc(semh[inc], inc_by)
        self.ops[eng].append(replay)
        return ticket

    def raw(self, eng, fn):
        self.ops[eng].append(fn)

    def dma(self, eng, out, in_, waits=(), sem=None, **kw):
        semn = sem or {"sync": "sy", "gpsimd": "gp", "scalar": "act_dma"}[eng]
        self.sem(semn)
        return self.emit(
            eng, lambda q: q.dma_start(out=out, in_=in_, **kw),
            waits=waits, inc=semn, inc_by=16)

    def finalize(self, final_waits=None):
        nc = self.nc
        if final_waits is None:
            final_waits = [(s, v) for s, v in self.cnt.items() if v > 0]
        with nc.Block() as block:
            for eng in self.ENGS:
                ops = self.ops[eng]
                if not ops and eng != "sync":
                    continue
                deco = getattr(block, eng)
                if eng == "sync":
                    fw = list(final_waits)
                    semh = self.semh

                    def body_sync(q, ops=ops, fw=fw):
                        for f in ops:
                            f(q)
                        for s, v in fw:
                            q.wait_ge(semh[s], v)
                    deco(body_sync)
                else:
                    def body(q, ops=ops):
                        for f in ops:
                            f(q)
                    deco(body)


class Banks:

    def __init__(self, ctx):
        self.ctx = ctx
        self.p01 = ctx.psum("ps01", [128, 1024], F32)
        self.p23 = ctx.psum("ps23", [128, 1024], F32)
        self.p45 = ctx.psum("ps45", [128, 1024], F32)
        self.p67 = ctx.psum("pst67", [128, 2048], BF16)
        self.rd = {i: [] for i in range(8)}

    def ap(self, i):
        if i >= 6:
            return self.p67[:, 1024 * (i - 6):1024 * (i - 6) + 1024]
        pair = (self.p01, self.p23, self.p45)[i // 2]
        return pair[:, 512 * (i % 2):512 * (i % 2) + 512]

    def pair_ap(self, lo):
        return (self.p01, self.p23, self.p45)[lo // 2][:, :]

    def write_waits(self, i):
        w = self.rd[i]
        self.rd[i] = []
        return w

    def read_done(self, i, ticket):
        self.rd[i].append(ticket)


class Stream:

    def __init__(self, ctx, name, shape, dtype, nbuf=2):
        self.ctx = ctx
        self.bufs = [ctx.sbuf(f"{name}{i}", shape, dtype) for i in range(nbuf)]
        self.nbuf = nbuf
        self.i = 0
        self.cons = [[] for _ in range(nbuf)]
        self.load_t = [None] * nbuf

    def load(self, src_ap):
        b = self.i % self.nbuf
        self.i += 1
        t = self.ctx.dma("sync", self.bufs[b][:, :], src_ap,
                         waits=self.cons[b])
        self.cons[b] = []
        self.load_t[b] = t
        return b

    def buf(self, b):
        return self.bufs[b]

    def wait(self, b):
        return ("sy", self.load_t[b])

    def done(self, b, ticket):
        self.cons[b].append(ticket)





CONST_KEYS = [
    "IDENT", "IDENT32", "I32S", "E4128", "E432", "CBIAS", "ROWP",
]
PERCORE_SB = ["R2"]


class Kern:
    def __init__(self, nc, shared, percore0, fused=True):
        self.nc = nc
        self.fused = fused
        self.c = Ctx(nc)
        self.bk = Banks(self.c)
        self.shapes = {}
        for k, v in shared.items():
            self.shapes[k] = v
        for k, v in percore0.items():
            self.shapes[k] = v
        self.in_keys = set()
        self.dram_in = {}
        c = self.c
        self.XT = [c.sbuf("xt0", [128, D], F32), c.sbuf("xt1", [128, D], F32)]
        self.XLN = c.sbuf("xlnb", [128, D], BF16)
        self.XLT = c.sbuf("xltb", [128, NT * BLK], BF16)
        self.O = c.sbuf("obuf", [128, NJ * D], BF16)
        self.SCR = c.sbuf("scrA", [128, 512], F32)
        self.SCR2 = c.sbuf("scrB", [128, 512], F32)
        self.SS = c.sbuf("sscol", [128, 160], F32)
        self.EPSC = c.sbuf("epsc", [128, 4], F32)
        self.ONESW = c.sbuf("onesw", [1, 512], BF16)
        self.ONESF = c.sbuf("onesf", [1, 128], F32)
        self.RW = [c.sbuf("rwA", [128, 8 * 512], BF16),
                   c.sbuf("rwB", [128, 8 * 512], BF16)]
        self.MWB = [c.sbuf("mwA", [128, NT * 128], BF16),
                    c.sbuf("mwB", [128, NT * 128], BF16)]
        self.rwi, self.rw_load, self.rw_cons = 0, [None, None], [[], []]
        self.mwi, self.mw_load, self.mw_cons = 0, [None, None], [[], []]
        self.SB = {}
        self.FCOL = c.sbuf("fcol", [1, 4], F32)
        c.emit("vector", lambda q: q.memset(self.FCOL[:, :], 0.0), inc="dve")
        self.xt_free = {0: [], 1: []}
        self.xo_free = []
        self.xln_readers = []
        self.qkv_area_free = []
        self.xlt_readers = []
        self.o_readers = []
        self.ep_parity = 0

    def act_fence(self):
        return self.c.emit("scalar", lambda q: q.activation(
            out=self.FCOL[0:1, 0:1], in_=self.FCOL[0:1, 3:4], func=AF.Copy),
            inc="act")

    def dve_fence(self):
        return self.c.emit("vector", lambda q: q.tensor_copy(
            out=self.FCOL[0:1, 1:2], in_=self.FCOL[0:1, 2:3]), inc="dve")

    def pe_fence(self, bank=7):
        ident, ident_t = self.SB["IDENT"]
        fw = self.bk.write_waits(bank)
        t = self.c.emit("tensor", lambda q, ident=ident, bank=bank:
                        q.transpose(
            self.bk.ap(bank)[:, 0:128], ident[:, :], ident[:, :]),
            waits=fw + [ident_t], inc="pe")
        self.bk.read_done(bank, ("pe", t))
        return t

    def din(self, key, shape=None, dtype=None):
        if key in self.dram_in:
            return self.dram_in[key]
        if shape is None:
            arr = self.shapes[key]
            shape = list(arr.shape)
            dtype = BF16 if arr.dtype == BF else F32
        t = self.nc.dram_tensor("i_" + key.lower(), shape, dtype,
                                kind="ExternalInput")
        self.in_keys.add(key)
        self.dram_in[key] = t.ap()
        return self.dram_in[key]

    def dout(self, key, shape, dtype):
        return self.nc.dram_tensor("o_" + key.lower(), shape, dtype,
                                   kind="ExternalOutput").ap()

    def dint(self, key, shape, dtype, **kw):
        return self.nc.dram_tensor("t_" + key.lower(), shape, dtype, **kw).ap()

    def dbg(self, name, sb_ap):
        c = self.c
        d = self.dout("dbg_" + name, list(sb_ap.shape), sb_ap.dtype)
        w = [(k, v) for k, v in c.cnt.items() if v > 0 and k in
             ("pe", "act", "dve", "pool")]
        c.dma("sync", d, sb_ap, waits=w, sem="sdbg")

    def rw_next(self, src_ap, nch):
        b = self.rwi % 2
        self.rwi += 1
        half = nch // 2
        src3 = src_ap.rearrange("(c p) n -> p c n", p=128)
        w = self.rw_cons[b]
        self.c.dma("sync", self.RW[b][:, :half * 512], src3[:, :half, :],
                   waits=w, sem=f"srw{b}")
        t = self.c.dma("scalar", self.RW[b][:, half * 512:nch * 512],
                       src3[:, half:nch, :], waits=w, sem=f"srw{b}")
        self.rw_cons[b] = []
        self.rw_load[b] = (f"srw{b}", t)
        return b

    def mw_next(self, src_ap, nch):
        b = self.mwi % 2
        self.mwi += 1
        t = self.c.dma("sync", self.MWB[b][:, :nch * 128],
                       src_ap.rearrange("(c p) n -> p c n", p=128),
                       waits=self.mw_cons[b], sem=f"smw{b}")
        self.mw_cons[b] = []
        self.mw_load[b] = (f"smw{b}", t)
        return b

    def preamble(self, keys=None):
        c = self.c
        if keys is None:
            keys = CONST_KEYS + PERCORE_SB
        for k in keys:
            arr = self.shapes[k]
            dt = BF16 if arr.dtype == BF else F32
            sb = c.sbuf("k_" + k.lower(), list(arr.shape), dt)
            c.dma("sync", sb[:, :], self.din(k), sem="cst")
            self.SB[k] = sb
        cst_total = c.cnt["cst"]
        cw = ("cst", cst_total)
        self.SB = {k: (v, cw) for k, v in self.SB.items()}
        self.cst_w = cw

        c.emit("vector", lambda q: q.memset(self.EPSC[:, 0:1], LN_EPS), inc="dve")
        c.emit("vector", lambda q: q.memset(self.EPSC[:, 1:2], DEN_EPS), inc="dve")
        c.emit("vector", lambda q: q.memset(self.EPSC[:, 2:3], PRIOR_EPS), inc="dve")
        c.emit("vector", lambda q: q.memset(self.ONESW[:, :], 1.0), inc="dve")
        tof = c.emit("vector", lambda q: q.memset(self.ONESF[:, :], 1.0), inc="dve")
        self.t_onesf = tof

        stg = c.sbuf("stg", [128, 128], F32)
        self.STG = stg
        self.MAUG = c.sbuf("maug", [128, 8 * 128], BF16)
        self.ZCOLS = c.sbuf("zcols", [128, 8], BF16)
        m0 = self.din("M0")
        z0 = self.din("Z0")
        prev = []
        for hk in range(8):
            tl = c.dma("sync", stg[:, 0:128], m0[hk, :, :], waits=prev,
                       sem="stg")
            t = c.emit("vector", lambda q, hk=hk: q.tensor_copy(
                out=self.MAUG[:, 128 * hk:128 * hk + 128], in_=stg[:, 0:128]),
                waits=[("stg", tl)], inc="dve")
            prev = [("dve", t)]
        for hk in range(8):
            tl = c.dma("sync", stg[:, 0:1],
                       z0[hk:hk + 1, :].rearrange("a b -> b a"), waits=prev,
                       sem="stg")
            t = c.emit("vector", lambda q, hk=hk: q.tensor_copy(
                out=self.ZCOLS[:, hk:hk + 1], in_=stg[:, 0:1]),
                waits=[("stg", tl)], inc="dve")
            prev = [("dve", t)]

        self.RB36 = c.sbuf("rb36", [1, 36], BF16)
        c.emit("vector", lambda q: q.memset(self.RB36[:, :], 0.0), inc="dve")
        trb = c.dma("sync", stg[0:1, 0:4], self.din("RBSEL"), waits=prev,
                    sem="stg")
        c.emit("vector", lambda q: q.tensor_copy(
            out=self.RB36[0:1, 32:36], in_=stg[0:1, 0:4]),
            waits=[("stg", trb)], inc="dve")
        self.t_rb36 = self.dve_fence()

        self.PRB = c.sbuf("prb", [128, 4], F32)
        tpr = c.dma("sync", stg[0:1, 4:8], self.din("PRSEL"),
                    waits=[("dve", self.t_rb36)], sem="stg")
        c.emit("scalar", lambda q: q.activation(
            out=stg[0:1, 4:8], in_=stg[0:1, 4:8], func=AF.Ln,
            bias=self.EPSC[0:1, 2:3]), waits=[("stg", tpr), ("dve", 3)],
            inc="act")
        tln = self.act_fence()
        bw = self.bk.write_waits(5)
        tmm = c.emit("tensor", lambda q: q.matmul(
            self.bk.ap(5)[:, 0:4], self.ONESF[0:1, :], stg[0:1, 4:8],
            start=True, stop=True),
            waits=[("act", tln), ("dve", tof)] + bw, inc="pe")
        tprb = c.emit("vector", lambda q: q.tensor_copy(
            out=self.PRB[:, :], in_=self.bk.ap(5)[:, 0:4]),
            waits=[("pe", tmm)], inc="dve")
        self.bk.read_done(5, ("dve", tprb))

        self.KTC = c.sbuf("ktc", [128, 1], F32)
        self.EXPB = c.sbuf("expb", [128, 1], F32)
        tk = c.dma("sync", stg[0:1, 8:9], self.din("KEYT"),
                   waits=[("dve", tprb)], sem="stg")
        c.emit("scalar", lambda q: q.activation(
            out=stg[0:1, 9:10], in_=stg[0:1, 8:9], func=AF.Abs),
            waits=[("stg", tk)], inc="act")
        ta1 = self.act_fence()
        tab = c.emit("vector", lambda q: q.tensor_scalar_mul(
            out=stg[0:1, 9:10], in0=stg[0:1, 9:10], scalar1=-0.125),
            waits=[("act", ta1)], inc="dve")
        bw = self.bk.write_waits(5)
        tmm2 = c.emit("tensor", lambda q: q.matmul(
            self.bk.ap(5)[:, 4:6], self.ONESF[0:1, :], stg[0:1, 8:10],
            start=True, stop=True),
            waits=[("dve", tab), ("dve", tof)] + bw, inc="pe")
        tc1 = c.emit("vector", lambda q: q.tensor_copy(
            out=self.KTC[:, :], in_=self.bk.ap(5)[:, 4:5]),
            waits=[("pe", tmm2)], inc="dve")
        tc2 = c.emit("vector", lambda q: q.tensor_copy(
            out=self.EXPB[:, :], in_=self.bk.ap(5)[:, 5:6]), inc="dve")
        self.bk.read_done(5, ("dve", tc2))

        self.HMB = c.sbuf("hmb", [128, 2], F32)
        th = c.dma("sync", stg[0:1, 10:12], self.din("HMASK"),
                   waits=[("dve", tc2)], sem="stg")
        bw = self.bk.write_waits(5)
        tmm = c.emit("tensor", lambda q: q.matmul(
            self.bk.ap(5)[:, 6:8], self.ONESF[0:1, :], stg[0:1, 10:12],
            start=True, stop=True), waits=[("stg", th), ("dve", tof)] + bw,
            inc="pe")
        thm = c.emit("vector", lambda q: q.tensor_copy(
            out=self.HMB[:, :], in_=self.bk.ap(5)[:, 6:8]),
            waits=[("pe", tmm)], inc="dve")
        self.bk.read_done(5, ("dve", thm))
        self.t_pre_dve = self.dve_fence()

    def build_woeff(self):
        c = self.c
        wot = self.din("WOT")
        park = self.dint("woeffp", [8 * 1024, 512], BF16)
        src = [self.RW[0][:, 0:2048], self.RW[1][:, 0:2048]]
        OF32 = self.O[:, :].bitcast(F32)
        acc = [OF32[:, 0:512], OF32[:, 512:1024]]
        out = [self.O[:, 2048:2560], self.O[:, 2560:3072]]
        st_t = [None, None]
        pool_t = [None, None]
        i = 0
        for og in range(8):
            for hk in range(8):
                bi = i % 2
                w = list(self.rw_cons[bi])
                self.rw_cons[bi] = []
                if pool_t[bi] is not None:
                    w.append(("pool", pool_t[bi]))
                tl = c.dma("sync", src[bi][:, :],
                           wot[4096 * og + 512 * hk:4096 * og + 512 * hk + 512, :]
                           .rearrange("(j p) n -> p j n", p=128), waits=w,
                           sem=f"swo{bi}")
                sl = src[bi]
                w2 = [(f"swo{bi}", tl)]
                wst = [(f"spk{bi}", st_t[bi])] if st_t[bi] else []
                c.emit("gpsimd", lambda q, sl=sl, bi=bi: q.tensor_tensor(
                    out=acc[bi][:, :], in0=sl[:, 0:512], in1=sl[:, 512:1024],
                    op=ALU.add), waits=w2, inc="pool")
                c.emit("gpsimd", lambda q, sl=sl, bi=bi: q.tensor_tensor(
                    out=out[bi][:, :], in0=sl[:, 1024:1536], in1=sl[:, 1536:2048],
                    op=ALU.add), waits=wst, inc="pool")
                pool_t[bi] = c.emit("gpsimd", lambda q, bi=bi: q.tensor_tensor(
                    out=out[bi][:, :], in0=acc[bi][:, :], in1=out[bi][:, :],
                    op=ALU.add), inc="pool")
                st_t[bi] = c.dma(
                    "scalar",
                    park[1024 * og + 128 * hk:1024 * og + 128 * hk + 128, :],
                    out[bi][:, :], waits=[("pool", pool_t[bi])],
                    sem=f"spk{bi}")
                i += 1
        self.woeff_park = park
        self.woeff_done = [("spk0", st_t[0]), ("spk1", st_t[1])]
        for bi in (0, 1):
            self.rw_cons[bi].append(("pool", pool_t[bi]))

    def preln_block(self, xsrc, b):
        c, bk = self.c, self.bk
        c.scope = c.scope.split("/")[0] + "/preln"
        ident, ident_t = self.SB["IDENT"]
        ready = []
        for j in range(NJ):
            xt = self.XT[j % 2]
            tld = ("sx%d" % (j % 2), c.dma(
                "sync", xt[:, :],
                xsrc[BLK * b + 128 * j:BLK * b + 128 * j + 128, :],
                waits=self.xt_free[j % 2], sem="sx%d" % (j % 2)))
            self.xt_free[j % 2] = []
            sc = 16 + 4 * (j % 2)
            xw = list(self.xln_readers)
            c.emit("vector", lambda q, xt=xt, sc=sc: q.tensor_scalar(
                out=self.XLN[:, :], in0=xt[:, :],
                scalar1=1.0, scalar2=0.0, op0=ALU.mult, op1=ALU.add,
                accum_out=self.SS[:, sc:sc + 1]),
                waits=[tld] + xw, inc="dve")
            c.emit("vector", lambda q, xt=xt, sc=sc:
                   q.scalar_tensor_tensor(
                out=self.XLN[:, :], in0=xt[:, :],
                scalar=1.0, in1=xt[:, :],
                op0=ALU.mult, op1=ALU.mult,
                accum_out=self.SS[:, sc + 1:sc + 2]), inc="dve")
            c.emit("vector", lambda q, sc=sc: q.tensor_scalar_mul(
                out=self.SS[:, sc:sc + 1], in0=self.SS[:, sc:sc + 1],
                scalar1=1.0 / D), inc="dve")
            c.emit("vector", lambda q, sc=sc: q.tensor_scalar_mul(
                out=self.SS[:, sc + 1:sc + 2], in0=self.SS[:, sc + 1:sc + 2],
                scalar1=1.0 / D), inc="dve")
            c.emit("vector", lambda q, sc=sc: q.tensor_tensor(
                out=self.SS[:, sc + 2:sc + 3], in0=self.SS[:, sc:sc + 1],
                in1=self.SS[:, sc:sc + 1], op=ALU.mult), inc="dve")
            tv = c.emit("vector", lambda q, sc=sc: q.tensor_tensor(
                out=self.SS[:, sc + 1:sc + 2], in0=self.SS[:, sc + 1:sc + 2],
                in1=self.SS[:, sc + 2:sc + 3], op=ALU.subtract), inc="dve")
            c.emit("scalar", lambda q, sc=sc: q.activation(
                out=self.SS[:, sc + 3:sc + 4], in_=self.SS[:, sc + 1:sc + 2],
                func=AF.Ln, bias=self.EPSC[:, 0:1]),
                waits=[("dve", tv)], inc="act")
            c.emit("scalar", lambda q, sc=sc: q.activation(
                out=self.SS[:, sc + 3:sc + 4], in_=self.SS[:, sc + 3:sc + 4],
                func=AF.Exp, scale=-0.5), inc="act")
            tr = self.act_fence()
            tn = c.emit("vector", lambda q, xt=xt, sc=sc: q.tensor_scalar(
                out=self.XLN[:, :], in0=xt[:, :], scalar1=self.SS[:, sc:sc + 1],
                scalar2=self.SS[:, sc + 3:sc + 4], op0=ALU.subtract,
                op1=ALU.mult), waits=[("act", tr)] + self.xln_readers,
                inc="dve")
            self.xln_readers = []
            self.xt_free[j % 2].append(("dve", tn))
            tn = self.dve_fence()
            evs = []
            lasts = []
            pend = []
            for hb in range(9):
                if hb < 8:
                    bank = 6 + (hb % 2)
                    w = [("dve", tn), ident_t] + bk.write_waits(bank)
                    last = None
                    for k in range(4):
                        cc = 4 * hb + k
                        last = c.emit("tensor", lambda q, cc=cc, bank=bank,
                                      k=k: q.transpose(
                                          self.bk.ap(bank)[:, 128 * k:
                                                           128 * k + 128],
                                          self.XLN[:, 128 * cc:128 * cc + 128],
                                          ident[:, :]),
                                      waits=w if k == 0 else [],
                                      inc="pe" if k == 3 else None)
                    lasts.append(last)
                else:
                    bank = 6 + (hb % 2)
                    fw = bk.write_waits(bank)
                    last = c.emit("tensor", lambda q, bank=bank: q.transpose(
                        self.bk.ap(bank)[:, 0:128],
                        self.XLN[:, 0:128], ident[:, :]), waits=fw, inc="pe")
                    lasts.append(last)
                    bk.read_done(bank, ("pe", last))
                if hb >= 1:
                    phb = hb - 1
                    pbank = 6 + (phb % 2)
                    ew = list(self.xlt_readers)
                    self.xlt_readers = []
                    xv = self.XLT[:, :].rearrange("p (c t) -> p c t", t=BLK)
                    tev = c.emit("vector", lambda q, phb=phb, j=j, pbank=pbank,
                                 xv=xv: q.tensor_copy(
                                     out=xv[:, 4 * phb:4 * phb + 4,
                                            128 * j:128 * j + 128],
                                     in_=self.bk.ap(pbank)[:, 0:512]
                                     .rearrange("p (c t) -> p c t", t=128)),
                                 waits=[("pe", lasts[hb])] + ew, inc="dve")
                    bk.read_done(pbank, ("dve", tev))
                    evs.append(tev)
            self.xln_readers = [("pe", lasts[-1])]
            tfd = self.dve_fence()
            evs.append(tfd)
            ready.append([("dve", evs[-2]), ("dve", evs[-1])])
        return ready

    def proj_tm(self, src_sb, nk, w_dram, ready, extras=None, bias_row=None,
                src_fn=None):
        c, bk = self.c, self.bk
        c.scope = c.scope.split("/")[0] + "/proj"
        if src_fn is None:
            def src_fn(cc, j):
                return src_sb[:, 512 * cc + 128 * j:512 * cc + 128 * j + 128]
        K = nk * 128
        cpl = min(nk, 8)
        nloads = nk // cpl
        evac = {}
        for og in range(8):
            grp_w = {j: None for j in range(NJ)}
            close_t = {}
            for lg in range(nloads):
                src = w_dram[K * og + cpl * 128 * lg:
                             K * og + cpl * 128 * (lg + 1), :]
                bi = self.rw_next(src, cpl)
                t_lg = None
                for j in range(NJ):
                    bank = (4 * og + j) % 6
                    if grp_w[j] is None:
                        grp_w[j] = bk.write_waits(bank) + list(ready[j])
                    for k in range(cpl):
                        cc = cpl * lg + k
                        w = []
                        if lg == 0 and k == 0:
                            w += grp_w[j]
                        if k == 0:
                            w.append(self.rw_load[bi])
                        st = (lg == 0 and k == 0)
                        fin = (extras is None and bias_row is None
                               and lg == nloads - 1 and k == cpl - 1)
                        is_last = (j == NJ - 1 and k == cpl - 1)
                        t = c.emit("tensor", lambda q, bank=bank, j=j,
                                   bi=bi, st=st, fin=fin, k=k,
                                   lhs=src_fn(cc, j): q.matmul(
                                       bk.ap(bank)[:, :], lhs,
                                       self.RW[bi][:, 512 * k:512 * k + 512],
                                       start=st, stop=fin), waits=w,
                                   inc="pe" if (is_last or fin) else None)
                        if is_last:
                            t_lg = t
                        if fin:
                            close_t[j] = t
                self.rw_cons[bi].append(("pe", t_lg + 1))
            reader_emits = []
            for j in range(NJ):
                bank = (4 * og + j) % 6
                exl = []
                if bias_row is not None:
                    ones_ap, row_ap, row_t = bias_row
                    exl.append((lambda q, ps, stop, og=og, row_ap=row_ap,
                                ones_ap=ones_ap: q.matmul(
                                    ps, ones_ap,
                                    row_ap[:, 512 * og:512 * og + 512],
                                    start=False, stop=stop),
                                [row_t]))
                if extras is not None:
                    exl += extras(og, j)
                for idx, (efn, ew) in enumerate(exl):
                    fin = (idx == len(exl) - 1)
                    t = c.emit("tensor", lambda q, efn=efn, bank=bank,
                               fin=fin: efn(q, bk.ap(bank)[:, :], fin),
                               waits=ew, inc="pe" if fin else None)
                    if fin:
                        close_t[j] = t
                orw = list(self.o_readers)
                self.o_readers = []
                reader_emits.append((j, bank, orw))
            ident, ident_t = self.SB["IDENT"]
            fw = bk.write_waits(6)
            c.emit("tensor", lambda q: q.transpose(
                bk.ap(6)[:, 0:128], ident[:, :], ident[:, :]),
                waits=fw + [ident_t], inc="pe")
            tfence = c.emit("tensor", lambda q: q.transpose(
                bk.ap(6)[:, 128:256], ident[:, :], ident[:, :]),
                inc="pe")
            for bi_ in set(range(2)):
                self.rw_cons[bi_].append(("pe", tfence))
            for (j, bank, orw) in reader_emits:
                tpe = close_t[j + 1] if (j + 1) in close_t else tfence
                tev0 = c.emit("vector", lambda q, bank=bank, og=og, j=j:
                              q.tensor_copy(
                                  out=self.O[:, D * j + 512 * og:
                                             D * j + 512 * og + 512],
                                  in_=bk.ap(bank)[:, :]),
                              waits=orw + [("pe", max(tpe, close_t[j]))],
                              inc="dve")
                bk.read_done(bank, ("dve", tev0))
                bk.read_done(6, ("dve", tev0))
                evac[(og, j)] = ("dve", tev0)
        self.xlt_readers.append(("pe", c.cnt["pe"]))
        return evac

    def epilogue_block(self, b, xsrc, xdst, evac, final=False):
        c = self.c
        c.scope = c.scope.split("/")[0] + "/epi"
        XO = [self.MWB[0][:, :].bitcast(F32), self.MWB[1][:, :].bitcast(F32)]
        out_t = []
        for j in range(NJ):
            xt = self.XT[j % 2]
            tld = ("sx%d" % (j % 2), c.dma(
                "sync", xt[:, :],
                xsrc[BLK * b + 128 * j:BLK * b + 128 * j + 128, :],
                waits=self.xt_free[j % 2], sem="sx%d" % (j % 2)))
            self.xt_free[j % 2] = []
            sc = 16 + 4 * (j % 2)
            wj = [evac[(og, j)] for og in range(8)]
            xw = list(self.xln_readers) + [("pe", c.cnt["pe"])]
            c.emit("vector", lambda q, sc=sc, j=j: q.tensor_scalar(
                out=self.XLN[:, :], in0=self.O[:, D * j:D * j + D],
                scalar1=1.0, scalar2=0.0, op0=ALU.mult, op1=ALU.add,
                accum_out=self.SS[:, sc:sc + 1]),
                waits=wj + xw, inc="dve")
            self.xln_readers = []
            c.emit("vector", lambda q, sc=sc, j=j: q.scalar_tensor_tensor(
                out=self.XLN[:, :], in0=self.O[:, D * j:D * j + D],
                scalar=1.0, in1=self.O[:, D * j:D * j + D],
                op0=ALU.mult, op1=ALU.mult,
                accum_out=self.SS[:, sc + 1:sc + 2]), inc="dve")
            c.emit("vector", lambda q, sc=sc: q.tensor_scalar_mul(
                out=self.SS[:, sc:sc + 1], in0=self.SS[:, sc:sc + 1],
                scalar1=1.0 / D), inc="dve")
            c.emit("vector", lambda q, sc=sc: q.tensor_scalar_mul(
                out=self.SS[:, sc + 1:sc + 2], in0=self.SS[:, sc + 1:sc + 2],
                scalar1=1.0 / D), inc="dve")
            c.emit("vector", lambda q, sc=sc: q.tensor_tensor(
                out=self.SS[:, sc + 2:sc + 3], in0=self.SS[:, sc:sc + 1],
                in1=self.SS[:, sc:sc + 1], op=ALU.mult), inc="dve")
            tv = c.emit("vector", lambda q, sc=sc: q.tensor_tensor(
                out=self.SS[:, sc + 1:sc + 2], in0=self.SS[:, sc + 1:sc + 2],
                in1=self.SS[:, sc + 2:sc + 3], op=ALU.subtract), inc="dve")
            c.emit("scalar", lambda q, sc=sc: q.activation(
                out=self.SS[:, sc + 3:sc + 4], in_=self.SS[:, sc + 1:sc + 2],
                func=AF.Ln, bias=self.EPSC[:, 0:1]),
                waits=[("dve", tv)], inc="act")
            c.emit("scalar", lambda q, sc=sc: q.activation(
                out=self.SS[:, sc + 3:sc + 4], in_=self.SS[:, sc + 3:sc + 4],
                func=AF.Exp, scale=-0.5), inc="act")
            tr = self.act_fence()
            for h in range(2):
                xo = XO[h]
                tn = c.emit("vector", lambda q, j=j, sc=sc, h=h, xo=xo:
                            q.tensor_scalar(
                    out=xo[:, :], in0=self.O[:, D * j + 2048 * h:
                                             D * j + 2048 * h + 2048],
                    scalar1=self.SS[:, sc:sc + 1],
                    scalar2=self.SS[:, sc + 3:sc + 4],
                    op0=ALU.subtract, op1=ALU.mult),
                    waits=[("act", tr)] + self.xo_free + self.mw_cons_pending(h),
                    inc="dve")
                ta = c.emit("vector", lambda q, xt=xt, h=h, xo=xo:
                            q.tensor_tensor(
                    out=xo[:, :], in0=xo[:, :],
                    in1=xt[:, 2048 * h:2048 * h + 2048], op=ALU.add),
                    waits=[tld], inc="dve")
                if h == 1:
                    self.xt_free[j % 2].append(("dve", ta))
                    self.o_readers = [("dve", ta)]
                tf = self.dve_fence()
                ts = c.dma("scalar",
                           xdst[BLK * b + 128 * j:BLK * b + 128 * j + 128,
                                2048 * h:2048 * h + 2048],
                           xo[:, :], waits=[("dve", tf)], sem="sgp%d" % h)
                self.xo_free = [("sgp%d" % h, ts)]
                self.mw_cons[h].append(("sgp%d" % h, ts))
                out_t.append(("sgp%d" % h, ts))
        return out_t

    def mw_cons_pending(self, h):
        if self.mw_load[h] is None:
            return []
        return [self.mw_load[h]]

    def moe_small(self, ready):
        c, bk = self.c, self.bk
        if not hasattr(self, "ZL"):
            self.WMTS = c.sbuf("wmts", [128, 32 * 36], BF16)
            self.t_wmts = ("swm", c.dma(
                "sync", self.WMTS[:, :],
                self.din("WMT").rearrange("(c p) n -> p c n", p=128),
                sem="swm"))
            self.ZL = c.sbuf("zl", [36, 512], F32)
            self.ALW = c.sbuf("alw", [128, 16], F32)
            self.ALT = c.sbuf("alt", [4, 512], F32)
            self.ZW = c.sbuf("zw", [32, 512], BF16)
        rall = [w for r in ready for w in r]
        pebar = [("pe", c.cnt["pe"])]
        ident32, t_id32 = self.SB["IDENT32"]
        w0 = bk.write_waits(4) + rall + [self.t_wmts]
        for cc in range(NT):
            c.emit("tensor", lambda q, cc=cc: q.matmul(
                bk.ap(4)[0:36, :], self.WMTS[:, 36 * cc:36 * cc + 36],
                self.XLT[:, 512 * cc:512 * cc + 512],
                start=(cc == 0), stop=False), waits=w0 if cc == 0 else [])
        tz = c.emit("tensor", lambda q: q.matmul(
            bk.ap(4)[0:36, :], self.RB36[0:1, :], self.ONESW[0:1, :],
            start=False, stop=True),
            waits=[("dve", self.t_rb36)], inc="pe")
        tzl = c.emit("vector", lambda q: q.tensor_copy(
            out=self.ZL[:, :], in_=bk.ap(4)[0:36, :]),
            waits=[("pe", tz)], inc="dve")
        bk.read_done(4, ("dve", tzl))
        tcp = c.emit("vector", lambda q: q.tensor_copy(
            out=self.SCR[0:4, :], in_=self.ZL[32:36, :]),
            waits=[("dve", tzl)] + pebar, inc="dve")
        tcp_f = self.dve_fence()
        w5 = bk.write_waits(5)
        tt = None
        for j in range(NJ):
            tt = c.emit("tensor", lambda q, j=j: q.transpose(
                bk.ap(5)[:, 4 * j:4 * j + 4],
                self.SCR[0:4, 128 * j:128 * j + 128], ident32[0:4, 0:4]),
                waits=(w5 + [("dve", tcp_f), t_id32]) if j == 0 else [],
                inc="pe" if j == 3 else None)
        tlg = c.emit("vector", lambda q: q.tensor_copy(
            out=self.ALW[:, :], in_=bk.ap(5)[:, 0:16]),
            waits=[("pe", tt)] + pebar, inc="dve")
        bk.read_done(5, ("dve", tlg))
        for j in range(NJ):
            c.emit("vector", lambda q, j=j: q.tensor_tensor(
                out=self.ALW[:, 4 * j:4 * j + 4],
                in0=self.ALW[:, 4 * j:4 * j + 4], in1=self.PRB[:, :],
                op=ALU.add), inc="dve")
            c.emit("vector", lambda q, j=j: q.reduce_max(
                out=self.SS[:, 24:25], in_=self.ALW[:, 4 * j:4 * j + 4],
                axis=AX), inc="dve")
            tmx = c.emit("vector", lambda q: q.tensor_scalar_mul(
                out=self.SS[:, 25:26], in0=self.SS[:, 24:25], scalar1=-1.0),
                inc="dve")
            c.emit("scalar", lambda q, j=j: q.activation(
                out=self.ALW[:, 4 * j:4 * j + 4],
                in_=self.ALW[:, 4 * j:4 * j + 4], func=AF.Exp,
                bias=self.SS[:, 25:26]), waits=[("dve", tmx)], inc="act")
            te = self.act_fence()
            c.emit("vector", lambda q, j=j: q.reduce_sum(
                out=self.SS[:, 26:27], in_=self.ALW[:, 4 * j:4 * j + 4],
                axis=AX), waits=[("act", te)], inc="dve")
            c.emit("vector", lambda q: q.reciprocal(
                out=self.SS[:, 27:28], in_=self.SS[:, 26:27]), inc="dve")
            c.emit("vector", lambda q, j=j: q.tensor_scalar(
                out=self.ALW[:, 4 * j:4 * j + 4],
                in0=self.ALW[:, 4 * j:4 * j + 4],
                scalar1=self.SS[:, 27:28], scalar2=2.0,
                op0=ALU.mult, op1=ALU.mult), inc="dve")
        tal_f = self.dve_fence()
        w5 = bk.write_waits(5)
        tt = None
        for j in range(NJ):
            tt = c.emit("tensor", lambda q, j=j: q.transpose(
                bk.ap(5)[0:4, 128 * j:128 * j + 128],
                self.ALW[:, 4 * j:4 * j + 4], ident32[:, :]),
                waits=(w5 + [("dve", tal_f)]) if j == 0 else [],
                inc="pe" if j == 3 else None)
        talt = c.emit("vector", lambda q: q.tensor_copy(
            out=self.ALT[:, :], in_=bk.ap(5)[0:4, 0:512]),
            waits=[("pe", tt)] + pebar, inc="dve")
        bk.read_done(5, ("dve", talt))
        talt_f = self.dve_fence()
        e432, t_e432 = self.SB["E432"]
        w4 = bk.write_waits(4)
        tex = c.emit("tensor", lambda q: q.matmul(
            bk.ap(4)[0:32, :], e432[:, :], self.ALT[:, :],
            start=True, stop=True),
            waits=w4 + [("dve", talt_f), t_e432], inc="pe")
        tzw = c.emit("vector", lambda q: q.tensor_tensor(
            out=self.ZW[:, :], in0=self.ZL[0:32, :], in1=bk.ap(4)[0:32, :],
            op=ALU.mult), waits=[("pe", tex)] + pebar, inc="dve")
        bk.read_done(4, ("dve", tzw))
        tzw_f = self.dve_fence()
        self.xlt_readers.append(("pe", tz))
        return tzw_f

    def arm_l0_ffn(self, xsrc, xdst, final=False):
        c = self.c
        wbt = self.din("WBT")
        bmt = self.din("BMT")
        c.sem("sbm")
        rowp, rowp_t = self.SB["ROWP"]
        for b in (0, 1):
            ready = self.preln_block(xsrc, b)
            bmt_sb = self.XLN[0:32, :]
            tbm = ("sbm", c.dma("sync", bmt_sb, bmt,
                                waits=[("pe", c.cnt["pe"])] +
                                self.xln_readers, sem="sbm"))
            tzw = self.moe_small(ready)

            def extras(og, j, tzw=tzw, bmt_sb=bmt_sb, tbm=tbm):
                return [(lambda q, ps, stop, og=og, j=j: q.matmul(
                    ps, self.ZW[:, 128 * j:128 * j + 128],
                    bmt_sb[:, 512 * og:512 * og + 512],
                    start=False, stop=stop),
                    [("dve", tzw), tbm])]
            evac = self.proj_tm(self.XLT, NT, wbt, ready, extras=extras,
                                bias_row=(rowp[0:1, 0:128], rowp[0:1, 128:],
                                          rowp_t))
            self.xln_readers.append(("pe", c.cnt["pe"]))
            self.epilogue_block(b, xsrc, xdst, evac, final=final)



    def qkv_block(self, xsrc, b, qkv_out):
        c, bk = self.c, self.bk
        ready = self.preln_block(xsrc, b)
        rall = [w for r in ready for w in r]
        wqkvt = self.din("WQKVT")
        QKVE = self.O[:, :].bitcast(BF16)
        close = {}
        st = []
        for mt in range(13):
            if mt < 12:
                bank = 4 + (mt % 2)
                bi = self.mw_next(wqkvt[4096 * mt:4096 * (mt + 1), :], NT)
                w0 = bk.write_waits(bank) + rall + [self.mw_load[bi]]
                for cc in range(NT):
                    t = c.emit("tensor", lambda q, bank=bank, cc=cc, bi=bi:
                               q.matmul(
                                   bk.ap(bank)[:, :],
                                   self.MWB[bi][:, 128 * cc:128 * cc + 128],
                                   self.XLT[:, 512 * cc:512 * cc + 512],
                                   start=(cc == 0), stop=(cc == NT - 1)),
                               waits=w0 if cc == 0 else [],
                               inc="pe" if cc == NT - 1 else None)
                close[mt] = t
                self.mw_cons[bi].append(("pe", t + 1))
            else:
                ident, ident_t = self.SB["IDENT"]
                fw = bk.write_waits(6)
                c.emit("tensor", lambda q: q.transpose(
                    bk.ap(6)[:, 0:128], self.XLN[:, 0:128], ident[:, :]),
                    waits=fw + [ident_t], inc="pe")
                close[mt] = c.emit("tensor", lambda q: q.transpose(
                    bk.ap(6)[:, 128:256], self.XLN[:, 0:128], ident[:, :]),
                    inc="pe")
                bk.read_done(6, ("pe", close[mt]))
            if mt >= 1:
                pm = mt - 1
                bank = 4 + (pm % 2)
                tev = c.emit("vector", lambda q, bank=bank, pm=pm, QKVE=QKVE:
                             q.tensor_copy(
                                 out=QKVE[:, 516 * pm + 4:516 * pm + 516],
                                 in_=bk.ap(bank)[:, :]),
                             waits=[("pe", close[pm + 1])] +
                             self.qkv_area_free, inc="dve")
                bk.read_done(bank, ("dve", tev))
                taf = self.dve_fence()
                ts_ = c.dma("sync",
                            qkv_out[128 * pm:128 * pm + 128,
                                    512 * b:512 * b + 512],
                            QKVE[:, 516 * pm + 4:516 * pm + 516],
                            waits=[("dve", taf)], sem="sqo")
                st.append(ts_)
        self.qkv_area_free = [("sqo", c.cnt["sqo"])]
        return st

    def load_conv_consts(self):
        c = self.c
        c.sem("sccv")
        XT0 = self.XT[0]
        XT0B = XT0[:, :].bitcast(BF16)
        w = [("pe", c.cnt["pe"]), ("dve", c.cnt["dve"]),
             ("act", c.cnt["act"])] + self.xt_free[0]
        self.xt_free[0] = []
        plan = [("CT", XT0[:, 0:1024]), ("ST", XT0[:, 1024:2048]),
                ("EXQ", XT0[0:16, 2048:3072]), ("EXK", XT0[0:4, 3072:3328]),
                ("QMS", XT0B[:, 6656:7168]), ("KMS", XT0B[:, 7168:7296]),
                ("PERMSW", XT0B[:, 7296:7424]), ("HSQ", XT0B[:, 7424:7552]),
                ("HSK", XT0B[:, 7552:7560])]
        self.CCV = {}
        for k, ap in plan:
            c.dma("sync", ap, self.din(k), waits=w, sem="sccv")
            self.CCV[k] = ap
        self.ccv_w = ("sccv", c.cnt["sccv"])

    def conv_block(self, b, qkv_in, halo_in, a2a_out, qloc=None, kvin=None):
        c, bk = self.c, self.bk
        for sn in ("sa2", "sqi", "scv", "sck"):
            c.sem(sn)
        QKVE = self.O[:, :].bitcast(BF16)
        C1E = self.XLT[:, 0:5160]
        SQ = [self.XLT[:, 5160 + 512 * i:5672 + 512 * i] for i in range(2)]
        KR = self.XLT[:, 6184:6696]
        CV1W = self.XLT[:, 6696:9768]
        KV1W = self.XLT[:, 9768:10536]
        QMN = self.SCR[:, :]
        KMN = self.XLT[:, 10536:11560].bitcast(F32)
        R16 = self.SCR[0:16, :]
        cbias, cbias_w = self.SB["CBIAS"]
        cw = self.ccv_w
        ct, ct_w = self.CCV["CT"], cw
        st_, st_w = self.CCV["ST"], cw
        permsw, permsw_w = self.CCV["PERMSW"], cw
        hsq, hsq_w = self.CCV["HSQ"], cw
        hsk, hsk_w = self.CCV["HSK"], cw
        exq, exq_w = self.CCV["EXQ"], cw
        exk, exk_w = self.CCV["EXK"], cw
        qms, qms_w = self.CCV["QMS"], cw
        kms, kms_w = self.CCV["KMS"], cw
        if qkv_in is not None:
            for mt in range(12):
                c.dma("sync", QKVE[:, 516 * mt + 4:516 * mt + 516],
                      qkv_in[128 * mt:128 * mt + 128, 512 * b:512 * b + 512],
                      waits=self.qkv_area_free, sem="sqi")
        if halo_in is not None:
            for mt in range(12):
                c.dma("sync", QKVE[:, 516 * mt:516 * mt + 4],
                      halo_in[128 * mt:128 * mt + 128, 4 * b:4 * b + 4],
                      waits=self.qkv_area_free, sem="sqi")
        self.qkv_area_free = []
        qw = [("sqi", c.cnt["sqi"])]
        for hb2 in range(2):
            tc1 = c.dma("sync",
                        CV1W[64 * hb2:64 * hb2 + 64, :]
                        .rearrange("p (a o) -> p a o", o=64),
                        self.din("QC1T").rearrange("(a i) o -> i a o", i=64),
                        sem="scv")
            tk1 = c.dma("sync",
                        KV1W[64 * hb2:64 * hb2 + 64, :]
                        .rearrange("p (a o) -> p a o", o=64),
                        self.din("KC1T").rearrange("(a i) o -> i a o", i=64),
                        sem="sck")
        c1ev = {}
        wb2 = bk.write_waits(2)
        for gi in range(21):
            if gi < 20:
                isk = gi >= 16
                g = gi - 16 if isk else gi
                bank = 4 + (gi % 2)
                ccd = (8 + g // 2) if isk else (g // 2)
                half = (g % 2) * 64
                W = (KV1W if isk else CV1W)[half:half + 64, :]
                ng = 4 if isk else 16
                ww = (bk.write_waits(bank) + qw +
                      [("sck" if isk else "scv", tk1 if isk else tc1)])
                last = None
                for tau in range(3):
                    lh = W[:, (tau * ng + g) * 64:(tau * ng + g) * 64 + 64]
                    last = c.emit("tensor", lambda q, bank=bank, half=half,
                                  lh=lh, ccd=ccd, tau=tau, QKVE=QKVE:
                                  q.matmul(
                                      bk.ap(bank)[half:half + 64, :],
                                      lh,
                                      QKVE[half:half + 64,
                                           516 * ccd + 2 + tau:
                                           516 * ccd + 2 + tau + 512],
                                      start=(tau == 0), stop=(tau == 2)),
                                  waits=ww if tau == 0 else [],
                                  inc="pe" if tau == 2 else None)
                for tau in range(3):
                    lh = W[:, (tau * ng + g) * 64:(tau * ng + g) * 64 + 64]
                    c.emit("tensor", lambda q, half=half, lh=lh, ccd=ccd,
                           tau=tau, gi=gi, QKVE=QKVE: q.matmul(
                               bk.ap(2)[half:half + 64, 2 * gi:2 * gi + 2],
                               lh,
                               QKVE[half:half + 64,
                                    516 * ccd + tau:516 * ccd + tau + 2],
                               start=(tau == 0), stop=(tau == 2)),
                           waits=wb2 if (gi == 0 and tau == 0) else [])
                wb2 = []
                c1ev[gi] = (last, bank, half, ccd, isk, g)
            if gi >= 1:
                (tcl, bank, half, ccd, isk, g) = c1ev[gi - 1]
                guard = ("pe", c.cnt["pe"])
                bcol = (24 + g // 2) if isk else (8 + g // 2)
                t1 = c.emit("scalar", lambda q, bank=bank, half=half,
                            ccd=ccd, bcol=bcol, C1E=C1E: q.activation(
                    out=C1E[half:half + 64, 516 * ccd + 4:516 * ccd + 516],
                    in_=bk.ap(bank)[half:half + 64, :], func=AF.Identity,
                    bias=cbias[half:half + 64, bcol:bcol + 1]),
                    waits=[guard, cbias_w], inc="act")
                bk.read_done(bank, ("act", t1))
        tfin = self.pe_fence()
        for gi in range(20):
            isk = gi >= 16
            g = gi - 16 if isk else gi
            ccd = (8 + g // 2) if isk else (g // 2)
            half = (g % 2) * 64
            bcol = (24 + g // 2) if isk else (8 + g // 2)
            th = c.emit("scalar", lambda q, half=half, ccd=ccd, gi=gi,
                        bcol=bcol, C1E=C1E: q.activation(
                out=C1E[half:half + 64, 516 * ccd + 2:516 * ccd + 4],
                in_=bk.ap(2)[half:half + 64, 2 * gi:2 * gi + 2], func=AF.Identity,
                bias=cbias[half:half + 64, bcol:bcol + 1]),
                waits=[("pe", tfin)], inc="act")
        bk.read_done(2, ("act", c.cnt["act"]))
        taf = self.act_fence()
        for ccd in range(10):
            c.emit("vector", lambda q, ccd=ccd, b=b, C1E=C1E:
                   q.tensor_scalar_mul(
                out=C1E[:, 516 * ccd + 2:516 * ccd + 4],
                in0=C1E[:, 516 * ccd + 2:516 * ccd + 4],
                scalar1=self.HMB[:, b:b + 1]),
                waits=[("act", taf), ("dve", self.t_pre_dve)], inc="dve")
        thm = self.dve_fence()
        c1w = [("dve", thm)]
        wb = bk.write_waits(2)
        for cc in range(8):
            tqm = c.emit("tensor", lambda q, cc=cc, QKVE=QKVE: q.matmul(
                bk.ap(2)[0:64, :], qms[:, 64 * cc:64 * cc + 64],
                QKVE[:, 516 * cc + 4:516 * cc + 516],
                start=(cc == 0), stop=(cc == 7)),
                waits=(wb + qw + [qms_w]) if cc == 0 else [],
                inc="pe" if cc == 7 else None)
        wb = bk.write_waits(3)
        for cc in range(2):
            tkm = c.emit("tensor", lambda q, cc=cc, QKVE=QKVE: q.matmul(
                bk.ap(3)[0:64, :], kms[:, 64 * cc:64 * cc + 64],
                QKVE[:, 516 * (8 + cc) + 4:516 * (8 + cc) + 516],
                start=(cc == 0), stop=(cc == 1)),
                waits=(wb + [kms_w]) if cc == 0 else [],
                inc="pe" if cc == 1 else None)
        for hb2 in range(2):
            tqm2 = c.emit("vector", lambda q, hb2=hb2: q.tensor_copy(
                out=QMN[64 * hb2:64 * hb2 + 64, :], in_=bk.ap(2)[0:64, :]),
                waits=[("pe", tkm)], inc="dve")
            tkm2 = c.emit("vector", lambda q, hb2=hb2: q.tensor_copy(
                out=KMN[64 * hb2:64 * hb2 + 64, :], in_=bk.ap(3)[0:64, :]),
                inc="dve")
        bk.read_done(2, ("dve", tqm2))
        bk.read_done(3, ("dve", tkm2))
        tmn = self.dve_fence()
        qc2t = self.din("QC2T")
        kc2t = self.din("KC2T")
        cv2 = {}
        qq_t = {}
        prev_qq = None
        for mi in range(11):
            if mi < 10:
                isk = mi >= 8
                mt = mi - 8 if isk else mi
                bank = 4 + (mi % 2)
                nch = 6 if isk else 24
                src = (kc2t[768 * mt:768 * (mt + 1), :] if isk
                       else qc2t[3072 * mt:3072 * (mt + 1), :])
                bi = self.mw_next(src, nch)
                w0 = bk.write_waits(bank) + c1w + [self.mw_load[bi]]
                ncc = 2 if isk else 8
                first = True
                for tau in range(3):
                    for ccl in range(ncc):
                        scc = (8 + ccl) if isk else ccl
                        t = c.emit("tensor", lambda q, bank=bank, bi=bi,
                                   tau=tau, ccl=ccl, scc=scc, ncc=ncc,
                                   first=first, C1E=C1E: q.matmul(
                                       bk.ap(bank)[:, :],
                                       self.MWB[bi][:, 128 * (tau * ncc + ccl):
                                                    128 * (tau * ncc + ccl) + 128],
                                       C1E[:, 516 * scc + 2 + tau:
                                           516 * scc + 2 + tau + 512],
                                       start=first,
                                       stop=(tau == 2 and ccl == ncc - 1)),
                                   waits=w0 if first else [],
                                   inc="pe" if (tau == 2 and ccl == ncc - 1)
                                   else None)
                        first = False
                cv2[mi] = (t, bank, isk, mt)
                self.mw_cons[bi].append(("pe", t + 1))
            if mi >= 1:
                (tcl, bank, isk, mt) = cv2[mi - 1]
                guard = ("pe", cv2[mi][0] if mi < 10 else c.cnt["pe"])
                bcol = (26 + mt) if isk else (16 + mt)
                tb = c.emit("scalar", lambda q, bank=bank, bcol=bcol:
                            q.activation(
                    out=self.SCR2[:, :], in_=bk.ap(bank)[:, :], func=AF.Identity,
                    bias=cbias[:, bcol:bcol + 1]),
                    waits=[guard, ("dve", tmn)] +
                    ([("dve", prev_qq)] if prev_qq else []), inc="act")
                bk.read_done(bank, ("act", tb))
                taf2 = self.act_fence()
                dst = (8 + mt) if isk else mt
                mean = QMN if isk else KMN
                for hh in range(2):
                    c.emit("vector", lambda q, dst=dst, hh=hh, mean=mean,
                           QKVE=QKVE: q.scalar_tensor_tensor(
                        out=QKVE[64 * hh:64 * hh + 64,
                                 516 * dst + 4:516 * dst + 516],
                        in0=mean[64 * hh:64 * hh + 64, :], scalar=0.5,
                        in1=self.SCR2[64 * hh:64 * hh + 64, :],
                        op0=ALU.mult, op1=ALU.add),
                        waits=[("act", taf2)], inc="dve")
                qq_t[dst] = self.dve_fence()
                prev_qq = qq_t[dst]
        for isk in (False, True):
            nh = 4 if isk else 16
            ncc = 2 if isk else 8
            hs, hs_w = (hsk, hsk_w) if isk else (hsq, hsq_w)
            wb = bk.write_waits(3)
            tsqp = None
            for ccl in range(ncc):
                scc = (8 + ccl) if isk else ccl
                par = ccl % 2
                tsq = c.emit("scalar", lambda q, scc=scc, par=par, QKVE=QKVE,
                             SQ=SQ: q.activation(
                    out=SQ[par][:, :],
                    in_=QKVE[:, 516 * scc + 4:516 * scc + 516],
                    func=AF.Square),
                    waits=[("dve", qq_t[scc])] +
                          ([("pe", tsqp)] if tsqp else []), inc="act")
                tafq = self.act_fence()
                tsqp = c.emit("tensor", lambda q, ccl=ccl, par=par, nh=nh,
                              hs=hs, ncc=ncc, SQ=SQ: q.matmul(
                    bk.ap(3)[0:nh, :], hs[:, nh * ccl:nh * ccl + nh],
                    SQ[par][:, :], start=(ccl == 0), stop=(ccl == ncc - 1)),
                    waits=(wb + [hs_w, ("act", tafq)]) if ccl == 0
                    else [("act", tafq)], inc="pe")
            tsqf = self.pe_fence()
            trs = c.emit("vector", lambda q, nh=nh: q.tensor_copy(
                out=self.SCR[0:nh, :],
                in_=bk.ap(3)[0:nh, :]), waits=[("pe", tsqf)], inc="dve")
            bk.read_done(3, ("dve", trs))
            R = self.SCR[0:nh, :]
            tlnf = self.dve_fence()
            c.emit("scalar", lambda q, R=R: q.activation(
                out=R, in_=R, func=AF.Ln), waits=[("dve", tlnf)],
                inc="act")
            c.emit("scalar", lambda q, R=R: q.activation(
                out=R, in_=R, func=AF.Exp, scale=-0.5), inc="act")
            tafr = self.act_fence()
            tcl = c.emit("vector", lambda q, R=R: q.tensor_scalar_min(
                out=R, in0=R, scalar1=1e12), waits=[("act", tafr)], inc="dve")
            if isk:
                tcl = c.emit("vector", lambda q, R=R, nh=nh: q.tensor_scalar(
                    out=R, in0=R, scalar1=self.KTC[0:nh, 0:1], scalar2=0.0,
                    op0=ALU.mult, op1=ALU.add), inc="dve")
            tclf = self.dve_fence()
            ex, ex_w = (exk, exk_w) if isk else (exq, exq_w)
            for ccl in range(ncc):
                scc = (8 + ccl) if isk else ccl
                bb = ccl % 2
                wbx = bk.write_waits(bb)
                tex = c.emit("tensor", lambda q, ccl=ccl, bb=bb, ex=ex, R=R,
                             nh=nh: q.matmul(
                    bk.ap(bb)[:, :], ex[:, 128 * ccl:128 * ccl + 128], R,
                    start=True, stop=True),
                    waits=wbx + [("dve", tclf), ex_w], inc="pe")
                tqn = c.emit("vector", lambda q, scc=scc, bb=bb, QKVE=QKVE:
                             q.tensor_tensor(
                    out=QKVE[:, 516 * scc + 4:516 * scc + 516],
                    in0=QKVE[:, 516 * scc + 4:516 * scc + 516],
                    in1=bk.ap(bb)[:, :], op=ALU.mult),
                    waits=[("pe", tex)], inc="dve")
                bk.read_done(bb, ("dve", tqn))
                qq_t[scc] = self.dve_fence()
        na2 = 0
        for cc in range(10):
            isk = cc >= 8
            ccl = cc - 8 if isk else cc
            bank = 4 + (cc % 2)
            wbx = bk.write_waits(bank)
            tpm = c.emit("tensor", lambda q, cc=cc, bank=bank, QKVE=QKVE:
                         q.matmul(
                bk.ap(bank)[:, :], permsw[:, :],
                QKVE[:, 516 * cc + 4:516 * cc + 516],
                start=True, stop=True),
                waits=wbx + [("dve", qq_t[cc]), permsw_w], inc="pe")
            tpmf = self.pe_fence()
            t1 = c.emit("vector", lambda q, bank=bank, b=b, st_=st_:
                        q.tensor_tensor(
                out=self.SCR2[:, :], in0=bk.ap(bank)[:, :],
                in1=st_[:, 512 * b:512 * b + 512], op=ALU.mult),
                waits=[("pe", tpmf), st_w], inc="dve")
            bk.read_done(bank, ("dve", t1))
            t2 = c.emit("vector", lambda q, cc=cc, b=b, ct=ct, QKVE=QKVE:
                        q.scalar_tensor_tensor(
                out=self.SCR[:, :], in0=QKVE[:, 516 * cc + 4:516 * cc + 516],
                scalar=1.0, in1=ct[:, 512 * b:512 * b + 512],
                op0=ALU.mult, op1=ALU.mult), waits=[ct_w], inc="dve")
            t3 = c.emit("vector", lambda q, KR=KR: q.tensor_tensor(
                out=KR[:, :], in0=self.SCR[:, :], in1=self.SCR2[:, :],
                op=ALU.add), waits=[("sa2", c.cnt["sa2"])], inc="dve")
            tf3 = self.dve_fence()
            if not isk:
                na2 += 1
                if qloc is not None:
                    c.dma("scalar", qloc[128 * cc:128 * cc + 128,
                                         512 * b:512 * b + 512],
                          KR[:, :], waits=[("dve", tf3)], sem="sa2")
                else:
                    c.dma("gpsimd", a2a_out[512 * cc + 256 * b:
                                            512 * cc + 256 * b + 128, :],
                          KR[:, :], waits=[("dve", tf3)], sem="sa2")
            else:
                for hh in range(2):
                    h = 2 * ccl + hh
                    if kvin is not None:
                        na2 += 1
                        c.dma("scalar",
                              kvin[64 * h:64 * h + 64,
                                   512 * b:512 * b + 512],
                              KR[64 * hh:64 * hh + 64, :],
                              waits=[("dve", tf3)], sem="sa2")
                        continue
                    for dd in (2 * h, 2 * h + 1):
                        na2 += 1
                        c.dma("gpsimd",
                              a2a_out[512 * dd + 256 * b + 128:
                                      512 * dd + 256 * b + 192, :],
                              KR[64 * hh:64 * hh + 64, :],
                              waits=[("dve", tf3)], sem="sa2")
        tvf = ("sqi", c.cnt["sqi"])
        for h in range(4):
            src_cc = 10 if h < 2 else 11
            roff = 64 * (h % 2)
            c0 = 516 * src_cc + (4 if h < 2 else 3)
            if kvin is not None:
                na2 += 1
                c.dma("scalar",
                      kvin[256 + 64 * h:256 + 64 * h + 64,
                           512 * b:512 * b + 512],
                      QKVE[roff:roff + 64, c0:c0 + 512],
                      waits=[tvf, ("dve", c.cnt["dve"])], sem="sa2")
                continue
            for dd in (2 * h, 2 * h + 1):
                na2 += 1
                c.dma("gpsimd",
                      a2a_out[512 * dd + 256 * b + 192:
                              512 * dd + 256 * b + 256, :],
                      QKVE[roff:roff + 64, c0:c0 + 512],
                      waits=[tvf, ("dve", c.cnt["dve"])], sem="sa2")
        self.qkv_area_free = [("sa2", c.cnt["sa2"]), ("pe", c.cnt["pe"]),
                              ("act", c.cnt["act"]), ("dve", c.cnt["dve"])]
        self.xt_free[0].append(("pe", c.cnt["pe"]))
        return ("sa2", c.cnt["sa2"])

    def arm_l0_attn(self, xsrc, xdst, final=False):
        c, bk = self.c, self.bk
        wqselt = self.din("WQSELT")
        cbias, cbias_w = self.SB["CBIAS"]
        for b in (0, 1):
            ready = self.preln_block(xsrc, b)
            if b == 0:
                self.build_woeff()
            woeff_done = self.woeff_done
            rall = [w for r in ready for w in r]
            SIGQ = self.XLN[:, :]
            RETR = self.XT[0][:, :].bitcast(BF16)[:, 0:4096]
            NUMB = self.XT[1][:, :].bitcast(BF16)[:, 0:512]
            close = {}
            sig_t = {}
            for mt in range(9):
                if mt < 8:
                    bank = 4 + (mt % 2)
                    bi = self.mw_next(wqselt[4096 * mt:4096 * (mt + 1), :], NT)
                    w0 = bk.write_waits(bank) + rall + [self.mw_load[bi]]
                    for cc in range(NT):
                        t = c.emit("tensor", lambda q, bank=bank, cc=cc, bi=bi:
                                   q.matmul(
                                       bk.ap(bank)[:, :],
                                       self.MWB[bi][:, 128 * cc:128 * cc + 128],
                                       self.XLT[:, 512 * cc:512 * cc + 512],
                                       start=(cc == 0), stop=(cc == NT - 1)),
                                   waits=w0 if cc == 0 else [],
                                   inc="pe" if cc == NT - 1 else None)
                    close[mt] = t
                    self.mw_cons[bi].append(("pe", t + 1))
                else:
                    ident, ident_t = self.SB["IDENT"]
                    fw = bk.write_waits(6)
                    c.emit("tensor", lambda q: q.transpose(
                        bk.ap(6)[:, 0:128], self.XLN[:, 0:128], ident[:, :]),
                        waits=fw + [ident_t], inc="pe")
                    close[mt] = c.emit("tensor", lambda q: q.transpose(
                        bk.ap(6)[:, 128:256], self.XLN[:, 0:128], ident[:, :]),
                        inc="pe")
                    bk.read_done(6, ("pe", close[mt]))
                if mt >= 1:
                    pm = mt - 1
                    bank = 4 + (pm % 2)
                    guard = [("pe", close[mt])]
                    t1 = c.emit("vector", lambda q, bank=bank, pm=pm:
                                q.tensor_scalar(
                        out=self.SCR[:, :], in0=bk.ap(bank)[:, :],
                        scalar1=cbias[:, pm:pm + 1], scalar2=0.0,
                        op0=ALU.add, op1=ALU.min),
                        waits=guard + [cbias_w], inc="dve")
                    t2 = c.emit("vector", lambda q, bank=bank, pm=pm:
                                q.tensor_scalar(
                        out=self.SCR2[:, :], in0=bk.ap(bank)[:, :],
                        scalar1=cbias[:, pm:pm + 1], scalar2=0.0,
                        op0=ALU.add, op1=ALU.max), inc="dve")
                    bk.read_done(bank, ("dve", t2))
                    tf = self.dve_fence()
                    c.emit("scalar", lambda q: q.activation(
                        out=self.SCR[:, :], in_=self.SCR[:, :], func=AF.Exp),
                        waits=[("dve", tf)], inc="act")
                    ta = self.act_fence()
                    ts_ = c.emit("vector", lambda q, pm=pm, SIGQ=SIGQ:
                                 q.tensor_tensor(
                        out=SIGQ[:, 512 * pm:512 * pm + 512],
                        in0=self.SCR[:, :], in1=self.SCR2[:, :], op=ALU.add),
                        waits=[("act", ta)], inc="dve")
                    sig_t[pm] = self.dve_fence()
            retr_t = {}
            denb = {}
            for hk in range(9):
                if hk < 8:
                    nb = 4 + (hk % 2)
                    db = 2 + (hk % 2)
                    w0 = bk.write_waits(nb) + [("dve", sig_t[hk]),
                                               ("dve", self.t_pre_dve)]
                    c.emit("tensor", lambda q, nb=nb, hk=hk, SIGQ=SIGQ:
                           q.matmul(
                               bk.ap(nb)[:, :],
                               self.MAUG[:, 128 * hk:128 * hk + 128],
                               SIGQ[:, 512 * hk:512 * hk + 512],
                               start=True, stop=True), waits=w0, inc="pe")
                    wd = bk.write_waits(db)
                    t = c.emit("tensor", lambda q, db=db, hk=hk, SIGQ=SIGQ:
                               q.matmul(
                                   bk.ap(db)[0:1, :],
                                   self.ZCOLS[:, hk:hk + 1],
                                   SIGQ[:, 512 * hk:512 * hk + 512],
                                   start=True, stop=True), waits=wd, inc="pe")
                    denb[hk] = t
                if hk >= 1:
                    ph = hk - 1
                    nb = 4 + (ph % 2)
                    db = 2 + (ph % 2)
                    guard = [("pe", denb[hk] if hk < 8 else c.cnt["pe"])]
                    c.emit("vector", lambda q, db=db: q.tensor_scalar_add(
                        out=self.SCR[0:1, :], in0=bk.ap(db)[0:1, :],
                        scalar1=DEN_EPS), waits=guard, inc="dve")
                    trc = c.emit("vector", lambda q: q.reciprocal(
                        out=self.SCR[0:1, :], in_=self.SCR[0:1, :]), inc="dve")
                    bk.read_done(db, ("dve", trc))
                    tfd = self.dve_fence()
                    bb = 0 if (ph % 2) == 0 else 1
                    wb = bk.write_waits(bb)
                    tbc = c.emit("tensor", lambda q, bb=bb: q.matmul(
                        bk.ap(bb)[:, :], self.ONESF[0:1, :], self.SCR[0:1, :],
                        start=True, stop=True),
                        waits=wb + [("dve", tfd)], inc="pe")
                    tnm = c.emit("scalar", lambda q, nb=nb, NUMB=NUMB:
                                 q.activation(out=NUMB[:, :],
                                              in_=bk.ap(nb)[:, :],
                                              func=AF.Copy),
                                 waits=[("pe", tbc)], inc="act")
                    bk.read_done(nb, ("act", tnm))
                    taf = self.act_fence()
                    trr = c.emit("vector", lambda q, ph=ph, bb=bb, NUMB=NUMB,
                                 RETR=RETR: q.tensor_tensor(
                        out=RETR[:, 512 * ph:512 * ph + 512],
                        in0=NUMB[:, :], in1=bk.ap(bb)[:, :], op=ALU.mult),
                        waits=[("act", taf)], inc="dve")
                    bk.read_done(bb, ("dve", trr))
                    retr_t[ph] = trr
            tretr = self.dve_fence()
            self.rw_cons[0] += woeff_done
            self.rw_cons[1] += woeff_done
            rd = [[("dve", tretr)]] * NJ
            evac = self.proj_tm(RETR, 8, self.woeff_park, rd)
            pw = ("pe", c.cnt["pe"])
            self.xt_free[0].append(pw)
            self.xt_free[1].append(pw)
            self.epilogue_block(b, xsrc, xdst, evac, final=final)



    def attention(self, att_in, a2a3_out):
        c, bk = self.c, self.bk
        for sn in ("sat", "sa3"):
            c.sem(sn)
        QF = [self.XLT[0:64, 0:8192], self.XLT[0:64, 8192:16384]]
        KF = self.O[0:64, 0:8192]
        VF = self.O[0:64, 8192:16384]
        XT0B = self.XT[0][:, :].bitcast(BF16)
        XT1B = self.XT[1][:, :].bitcast(BF16)
        VAUG = XT0B[:, 0:4160]
        PBUF = [XT0B[:, 4160 + 512 * i:4672 + 512 * i] for i in range(2)]
        NUMB = XT0B[0:64, 5184:5696]
        AOUT = [XT1B[0:64, 0:8192], self.O[0:64, 8192:16384]]
        mbig, mbig_w = self.SB["MBIG"]
        ident, ident_t = self.SB["IDENT"]
        pw = [("pe", c.cnt["pe"]), ("dve", c.cnt["dve"]),
              ("act", c.cnt["act"])]
        for g in range(16):
            s_ = g if g < 8 else 15 - g
            hf = 0 if g < 8 else 1
            base = 512 * s_ + 256 * hf
            for hl in range(2):
                c.dma("sync", QF[hl][:, 512 * g:512 * g + 512],
                      att_in[base + 64 * hl:base + 64 * hl + 64, :],
                      waits=pw, sem="sat")
            c.dma("sync", KF[:, 512 * g:512 * g + 512],
                  att_in[base + 128:base + 192, :], waits=pw, sem="sat")
            c.dma("sync", VF[:, 512 * g:512 * g + 512],
                  att_in[base + 192:base + 256, :], waits=pw, sem="sat")
        tat = ("sat", c.cnt["sat"])
        tva = c.emit("vector", lambda q: q.memset(VAUG, 1.0), waits=pw,
                     inc="dve")
        lastT = {}
        for kc in range(65):
            if kc < 64:
                bank = 6 + (kc % 2)
                w = bk.write_waits(bank) + [tat, ident_t]
                lastT[kc] = c.emit("tensor", lambda q, kc=kc, bank=bank:
                                   q.transpose(
                    bk.ap(bank)[:, 0:64], VF[:, 128 * kc:128 * kc + 128],
                    ident[0:64, 0:64]), waits=w, inc="pe")
            else:
                lastT[kc] = self.pe_fence(bank=6)
            if kc >= 1:
                pb = 6 + ((kc - 1) % 2)
                tev = c.emit("vector", lambda q, kc=kc, pb=pb: q.tensor_copy(
                    out=VAUG[:, 65 * (kc - 1):65 * (kc - 1) + 64],
                    in_=bk.ap(pb)[:, 0:64]),
                    waits=[("pe", lastT[kc]), ("dve", tva)], inc="dve")
                bk.read_done(pb, ("dve", tev))
        t_vtr = lastT[64]
        tvf = self.dve_fence()
        pv_hist = {0: [], 1: []}
        last_tao = []
        for g in range(16):
            nkc = 4 * g + 4
            for hl in range(2):
                ob = 2 + hl
                wo = bk.write_waits(ob)
                exp_t = {}
                for kc in range(nkc + 1):
                    if kc < nkc:
                        sb_ = 4 + (kc % 2)
                        ws = bk.write_waits(sb_)
                        if kc == 0:
                            ws = ws + [tat, ("dve", tvf)]
                        tqk = c.emit("tensor", lambda q, kc=kc, sb_=sb_, g=g,
                                     hl=hl: q.matmul(
                            bk.ap(sb_)[:, :], KF[:, 128 * kc:128 * kc + 128],
                            QF[hl][:, 512 * g:512 * g + 512],
                            start=True, stop=True), waits=ws, inc="pe")
                        par = kc % 2
                        wex = [("pe", tqk),
                               ("dve", self.t_pre_dve)] + pv_hist[par]
                        pv_hist[par] = []
                        te = c.emit("scalar", lambda q, sb_=sb_, par=par:
                                    q.activation(
                            out=PBUF[par][:, :], in_=bk.ap(sb_)[:, :],
                            func=AF.Exp, scale=0.125, bias=self.EXPB[:, 0:1]),
                            waits=wex, inc="act")
                        bk.read_done(sb_, ("act", te))
                        if kc >= 4 * g:
                            off = 384 - 128 * (kc - 4 * g)
                            c.emit("vector", lambda q, par=par, off=off:
                                   q.tensor_tensor(
                                out=PBUF[par][:, :], in0=PBUF[par][:, :],
                                in1=mbig[:, off:off + 512], op=ALU.mult),
                                waits=[("act", te), mbig_w], inc="dve")
                            exp_t[kc] = ("dve", self.dve_fence())
                        else:
                            exp_t[kc] = ("act", te)
                    if kc >= 1:
                        pkc = kc - 1
                        par = pkc % 2
                        gate = exp_t[kc] if kc < nkc else \
                            ("act", self.act_fence())
                        wpv = [exp_t[pkc], gate] + (wo if pkc == 0 else [])
                        tpv = c.emit("tensor", lambda q, pkc=pkc, par=par,
                                     ob=ob, nkc=nkc: q.matmul(
                            bk.ap(ob)[0:65, :],
                            VAUG[:, 65 * pkc:65 * pkc + 65], PBUF[par][:, :],
                            start=(pkc == 0), stop=(pkc == nkc - 1)),
                            waits=wpv, inc="pe")
                        pv_hist[par].append(("pe", tpv))
                tpf = self.pe_fence()
                tdc = c.emit("vector", lambda q, ob=ob: q.tensor_copy(
                    out=self.SCR[0:1, :], in_=bk.ap(ob)[64:65, :]),
                    waits=[("pe", tpf)], inc="dve")
                c.emit("vector", lambda q: q.reciprocal(
                    out=self.SCR[0:1, :], in_=self.SCR[0:1, :]), inc="dve")
                tfd = self.dve_fence()
                bb = hl
                wb = bk.write_waits(bb)
                tbc = c.emit("tensor", lambda q, bb=bb: q.matmul(
                    bk.ap(bb)[0:64, :], self.ONESF[0:1, 0:64],
                    self.SCR[0:1, :], start=True, stop=True),
                    waits=wb + [("dve", tfd), ("dve", self.t_onesf)],
                    inc="pe")
                tnm = c.emit("scalar", lambda q, ob=ob: q.activation(
                    out=NUMB[:, :], in_=bk.ap(ob)[0:64, :], func=AF.Copy),
                    waits=[("pe", tbc)] + last_tao, inc="act")
                bk.read_done(ob, ("act", tnm))
                bk.read_done(ob, ("dve", tdc))
                taf = self.act_fence()
                tao = c.emit("vector", lambda q, g=g, hl=hl, bb=bb:
                             q.tensor_tensor(
                    out=AOUT[hl][:, 512 * g:512 * g + 512], in0=NUMB[:, :],
                    in1=bk.ap(bb)[0:64, :], op=ALU.mult),
                    waits=[("act", taf), ("pe", t_vtr)], inc="dve")
                bk.read_done(bb, ("dve", tao))
                last_tao = [("dve", tao)]
        tfin = self.dve_fence()
        for d in range(8):
            for (bslot, g) in ((0, d), (1, 15 - d)):
                for hl in range(2):
                    c.dma("gpsimd",
                          a2a3_out[128 * d + 64 * hl:128 * d + 64 * hl + 64,
                                   512 * bslot:512 * bslot + 512],
                          AOUT[hl][:, 512 * g:512 * g + 512],
                          waits=[("dve", tfin)], sem="sa3")
        return ("sa3", c.cnt["sa3"])

    def join_x(self):
        w = [(sn, self.c.cnt[sn]) for sn in ("sgp0", "sgp1")
             if self.c.cnt.get(sn, 0) > 0]
        for par in (0, 1):
            self.xt_free[par] = self.xt_free[par] + w

    def halo_exchange(self, qkvd, agin, agout, halod):
        c, bk = self.c, self.bk
        for sn in ("shx", "cc1", "shl", "shl2"):
            c.sem(sn)
        r2, r2_w = self.SB["R2"]
        wq = [("sqo", c.cnt["sqo"])]
        for bi in (0, 1):
            for i in range(4):
                col = 512 * bi + 508 + i
                c.dma("gpsimd", agin[4 * bi + i:4 * bi + i + 1, :],
                      qkvd[:, col:col + 1].rearrange("a b -> b a"),
                      waits=wq, sem="shx",
                      allow_slow_non_contiguous=True)
        grp = getattr(self, "cc_group", list(range(NC)))
        c.emit("gpsimd", lambda q: q.collective_compute(
            "AllGather", ALU.bypass, replica_groups=[grp],
            ins=[agin[:, :]], outs=[agout[0:8 * len(grp), :]]),
            waits=[("shx", c.cnt["shx"])], inc="cc1")
        XT1B = self.XT[1][:, :].bitcast(BF16)
        AGS = XT1B[0:64, 0:1536]
        HSB = XT1B[:, 1536:1632]
        wx = self.xt_free[1] + [("cc1", c.cnt["cc1"])]
        self.xt_free[1] = []
        tl = ("shl", c.dma("sync", AGS, agout[:, :], waits=wx, sem="shl"))
        w2 = bk.write_waits(2)
        tmm = None
        for mt in range(12):
            tmm = c.emit("tensor", lambda q, mt=mt, AGS=AGS: q.matmul(
                bk.ap(2)[:, 8 * mt:8 * mt + 8],
                AGS[:, 128 * mt:128 * mt + 128], r2[:, :],
                start=True, stop=True),
                waits=(w2 + [tl, r2_w]) if mt == 0 else [],
                inc="pe" if mt == 11 else None)
        tpf = self.pe_fence()
        thc = c.emit("vector", lambda q, HSB=HSB: q.tensor_copy(
            out=HSB, in_=bk.ap(2)[:, 0:96]), waits=[("pe", tpf)], inc="dve")
        bk.read_done(2, ("dve", thc))
        thf = self.dve_fence()
        for mt in range(12):
            c.dma("gpsimd", halod[128 * mt:128 * mt + 128, :],
                  HSB[:, 8 * mt:8 * mt + 8], waits=[("dve", thf)],
                  sem="shl2")
        self.xt_free[1] = [("shl2", c.cnt["shl2"])]
        self.qkv_area_free = self.qkv_area_free + [("shl2", c.cnt["shl2"])]

    def kv_allgather(self, kvind, kvgd):
        c = self.c
        c.sem("cc2")
        grp = getattr(self, "cc_group", list(range(NC)))
        c.emit("gpsimd", lambda q: q.collective_compute(
            "AllGather", ALU.bypass, replica_groups=[grp],
            ins=[kvind[:, :]], outs=[kvgd[0:512 * len(grp), :]]),
            waits=[("sa2", c.cnt["sa2"])], inc="cc2")
        return ("cc2", c.cnt["cc2"])

    def attention2(self, qloc, kvg, attd, extra_w=()):
        c, bk = self.c, self.bk
        for sn in ("sat", "sa30", "sa31"):
            c.sem(sn)
        QSB = self.O[0:64, 0:16384]
        KSB = self.XLT[0:64, 0:8192]
        VF = self.XLT[0:64, 8192:16384]
        XT0B = self.XT[0][:, :].bitcast(BF16)
        XT1B = self.XT[1][:, :].bitcast(BF16)
        VAUG = XT0B[:, 0:4160]
        PBUF = [XT0B[:, 4160 + 1024 * i:5184 + 1024 * i] for i in range(3)]
        PBUF.append(XT1B[:, 2304:3328])
        NUMB = XT0B[0:64, 7232:7744]
        AOS = [XT1B[0:64, 3328 + 512 * i:3840 + 512 * i] for i in range(2)]
        ident, ident_t = self.SB["IDENT"]
        pw = [("pe", c.cnt["pe"]), ("dve", c.cnt["dve"]),
              ("act", c.cnt["act"])] + self.qkv_area_free + \
            self.xlt_readers + list(extra_w)
        self.qkv_area_free = []
        self.xlt_readers = []
        c.sem("scat")
        thr = self.XT[1][:, 0:128]
        iota = XT1B[:, 256:768]
        c.dma("sync", thr, self.din("THR"),
              waits=pw + self.xt_free[1], sem="scat")
        self.xt_free[1] = []
        c.dma("sync", iota, self.din("IOTA"), waits=pw, sem="scat")
        thr_w = ("scat", c.cnt["scat"])
        iota_w = thr_w
        for ccq in range(8):
            for hh in range(2):
                c.dma("sync",
                      QSB[:, 1024 * (2 * ccq + hh):1024 * (2 * ccq + hh) +
                          1024],
                      qloc[128 * ccq + 64 * hh:128 * ccq + 64 * hh + 64, :],
                      waits=pw, sem="sat")
        tq = ("sat", c.cnt["sat"])
        tva = c.emit("vector", lambda q: q.memset(VAUG, 1.0), waits=pw,
                     inc="dve")
        pv_hist = {0: [], 1: [], 2: [], 3: []}
        last_tao = []
        hist_sa3 = {0: [], 1: []}
        gi = 0
        base_scope = c.scope
        for kvh in range(4):
            c.scope = f"{base_scope}_kv{kvh}"
            kw = [("pe", c.cnt["pe"]), ("dve", c.cnt["dve"]),
                  ("act", c.cnt["act"])]
            for g in range(16):
                so, sbi = (g, 0) if g < 8 else (15 - g, 1)
                c.dma("sync", KSB[:, 512 * g:512 * g + 512],
                      kvg[512 * so + 64 * kvh:512 * so + 64 * kvh + 64,
                          512 * sbi:512 * sbi + 512], waits=kw, sem="sat")
                c.dma("sync", VF[:, 512 * g:512 * g + 512],
                      kvg[512 * so + 256 + 64 * kvh:
                          512 * so + 256 + 64 * kvh + 64,
                          512 * sbi:512 * sbi + 512], waits=kw, sem="sat")
            tat = ("sat", c.cnt["sat"])
            lastT = {}
            for kc in range(65):
                if kc < 64:
                    bank = 6 + (kc % 2)
                    w = bk.write_waits(bank) + [tat, ident_t]
                    lastT[kc] = c.emit("tensor", lambda q, kc=kc, bank=bank:
                                       q.transpose(
                        bk.ap(bank)[:, 0:64], VF[:, 128 * kc:128 * kc + 128],
                        ident[0:64, 0:64]), waits=w, inc="pe")
                else:
                    lastT[kc] = self.pe_fence(bank=6)
                if kc >= 1:
                    pb = 6 + ((kc - 1) % 2)
                    tev = c.emit("vector", lambda q, kc=kc, pb=pb:
                                 q.tensor_copy(
                        out=VAUG[:, 65 * (kc - 1):65 * (kc - 1) + 64],
                        in_=bk.ap(pb)[:, 0:64]),
                        waits=[("pe", lastT[kc]), ("dve", tva)], inc="dve")
                    bk.read_done(pb, ("dve", tev))
            tvf = self.dve_fence()
            c.scope = f"{base_scope}_mn{kvh}"
            for hq in range(4):
                h = 4 * kvh + hq
                for bslot in range(2):
                    nkc = 32 if bslot == 0 else 64
                    npair = nkc // 2
                    ob = 2 + (gi % 2)
                    wo = bk.write_waits(ob)
                    exp_t = {}
                    for pi in range(npair + 2):
                        if pi < npair:
                            lo = 4 if pi % 2 == 0 else 0
                            tqk = None
                            for half in (0, 1):
                                kc = 2 * pi + half
                                bank = lo + half
                                ws = bk.write_waits(bank)
                                if pi == 0 and half == 0:
                                    ws = ws + [tq, tat, ("dve", tvf)]
                                tqk = c.emit("tensor", lambda q, kc=kc,
                                             bank=bank, h=h, bslot=bslot:
                                             q.matmul(
                                    bk.ap(bank)[:, :],
                                    KSB[:, 128 * kc:128 * kc + 128],
                                    QSB[:, 1024 * h + 512 * bslot:
                                        1024 * h + 512 * bslot + 512],
                                    start=True, stop=True), waits=ws,
                                    inc="pe")
                            par = pi % 4
                            wex = [("pe", tqk),
                                   ("dve", self.t_pre_dve)] + pv_hist[par]
                            pv_hist[par] = []
                            te = c.emit("scalar", lambda q, lo=lo, par=par:
                                        q.activation(
                                out=PBUF[par][:, :], in_=bk.pair_ap(lo),
                                func=AF.Exp, scale=0.125,
                                bias=self.EXPB[:, 0:1]),
                                waits=wex, inc="act")
                            bk.read_done(lo, ("act", te))
                            bk.read_done(lo + 1, ("act", te))
                            if bslot == 1 and pi < 16:
                                exp_t[pi] = ("exp", te)
                            else:
                                meng = "vector"
                                for half in (0, 1):
                                    kc = 2 * pi + half
                                    tcol = 64 * bslot + kc
                                    c.emit(meng, lambda q, par=par,
                                           tcol=tcol, half=half:
                                           q.scalar_tensor_tensor(
                                        out=PBUF[par][:, 512 * half:
                                                      512 * half + 512],
                                        in0=iota[:, :],
                                        scalar=thr[:, tcol:tcol + 1],
                                        in1=PBUF[par][:, 512 * half:
                                                      512 * half + 512],
                                        op0=ALU.is_ge, op1=ALU.mult),
                                        waits=[("act", te), iota_w, thr_w],
                                        inc=None)
                                if meng == "vector":
                                    exp_t[pi] = ("dve", self.dve_fence())
                                else:
                                    exp_t[pi] = ("pool", c.emit(
                                        "gpsimd", lambda q: q.tensor_copy(
                                            out=self.EPSC[0:1, 3:4],
                                            in_=self.EPSC[0:1, 2:3]),
                                        inc="pool"))
                        if pi >= 2:
                            pp = pi - 2
                            par = pp % 4
                            ge = exp_t[pp]
                            if ge[0] == "exp":
                                nx = exp_t[pp + 1]
                                ge = ("act", nx[1]) if nx[0] == "exp" \
                                    else nx
                            for half in (0, 1):
                                pkc = 2 * pp + half
                                wpv = [ge] + \
                                    (wo if pkc == 0 else [])
                                tpv = c.emit("tensor", lambda q, pkc=pkc,
                                             par=par, ob=ob, half=half,
                                             nkc=nkc: q.matmul(
                                    bk.ap(ob)[0:65, :],
                                    VAUG[:, 65 * pkc:65 * pkc + 65],
                                    PBUF[par][:, 512 * half:
                                              512 * half + 512],
                                    start=(pkc == 0),
                                    stop=(pkc == nkc - 1)),
                                    waits=wpv, inc="pe")
                                pv_hist[par].append(("pe", tpv))
                    tpf = self.pe_fence()
                    tdc = c.emit("vector", lambda q, ob=ob: q.tensor_copy(
                        out=self.SCR[0:1, :], in_=bk.ap(ob)[64:65, :]),
                        waits=[("pe", tpf)], inc="dve")
                    c.emit("vector", lambda q: q.reciprocal(
                        out=self.SCR[0:1, :], in_=self.SCR[0:1, :]),
                        inc="dve")
                    tfd = self.dve_fence()
                    wb = bk.write_waits(4)
                    tbc = c.emit("tensor", lambda q: q.matmul(
                        bk.ap(4)[0:64, :], self.ONESF[0:1, 0:64],
                        self.SCR[0:1, :], start=True, stop=True),
                        waits=wb + [("dve", tfd), ("dve", self.t_onesf)],
                        inc="pe")
                    tnm = c.emit("vector", lambda q, ob=ob: q.tensor_copy(
                        out=NUMB[:, :], in_=bk.ap(ob)[0:64, :]),
                        waits=[("pe", tbc)] + last_tao, inc="dve")
                    bk.read_done(ob, ("dve", tnm))
                    bk.read_done(ob, ("dve", tdc))
                    tpf2 = self.pe_fence()
                    par2 = gi % 2
                    tao = c.emit("vector", lambda q, par2=par2:
                                 q.tensor_tensor(
                        out=AOS[par2][:, :], in0=NUMB[:, :],
                        in1=bk.ap(4)[0:64, :], op=ALU.mult),
                        waits=[("pe", tpf2)] + hist_sa3[par2], inc="dve")
                    bk.read_done(4, ("dve", tao))
                    last_tao = [("dve", tao)]
                    tfo = self.dve_fence()
                    ts3 = c.dma("scalar",
                                attd[64 * h:64 * h + 64,
                                     512 * bslot:512 * bslot + 512],
                                AOS[par2][:, :], waits=[("dve", tfo)],
                                sem=f"sa3{par2}")
                    hist_sa3[par2] = [(f"sa3{par2}", ts3)]
                    gi += 1
        c.scope = base_scope
        self.o_readers = []
        self.xlt_readers = [("pe", c.cnt["pe"])]
        self.xt_free[0] += [("sa30", c.cnt["sa30"]), ("sa31", c.cnt["sa31"])]
        self.xt_free[1].append(("dve", c.cnt["dve"]))
        return [("sa30", c.cnt["sa30"]), ("sa31", c.cnt["sa31"])]

    def arm_wo(self, xsrc, xdst, final=False, att_dram=None, extra_w=()):
        c, bk = self.c, self.bk
        if att_dram is None:
            att_dram = self.din("ATTD", [1024, 1024], BF16)
        c.sem("sai")
        ATT = self.XLT[:, 0:8192]
        pw = [("pe", c.cnt["pe"]), ("dve", c.cnt["dve"]),
              ("act", c.cnt["act"])] + self.xlt_readers + list(extra_w)
        self.xlt_readers = []
        for cc in range(8):
            c.dma("sync", ATT[:, 1024 * cc:1024 * cc + 1024],
                  att_dram[128 * cc:128 * cc + 128, :], waits=pw, sem="sai")
        ti = ("sai", c.cnt["sai"])
        wo2t = self.din("WO2T")
        for b in (0, 1):
            def src_fn(cc, j, b=b):
                return ATT[:, 1024 * cc + 512 * b + 128 * j:
                           1024 * cc + 512 * b + 128 * j + 128]
            evac = self.proj_tm(None, 8, wo2t, [[ti]] * NJ, src_fn=src_fn)
            self.epilogue_block(b, xsrc, xdst, evac, final=final)

    def arm_l1_ffn(self, xsrc, xdst, final=False):
        c, bk = self.c, self.bk
        kert = self.din("KERT")
        gates = self.din("GATES")
        l1a = self.din("L1A")
        l1bt = self.din("L1BT")
        l1astp = self.din("L1ASTP")
        l1bst = self.din("L1BST")
        rowp, rowp_t = self.SB["ROWP"]
        i32s, i32s_w = self.SB["I32S"]
        e4128, e4128_w = self.SB["E4128"]
        for sn in ("sga", "sls", "sbs"):
            c.sem(sn)
        stg_free = [("dve", self.t_pre_dve)]

        def rw_manual(src2d, extra=()):
            bi = self.rwi % 2
            self.rwi += 1
            t = c.dma("sync", self.RW[bi][:, :], src2d,
                      waits=self.rw_cons[bi] + list(extra), sem=f"srw{bi}")
            self.rw_cons[bi] = []
            self.rw_load[bi] = (f"srw{bi}", t)
            return bi

        def mw_manual(src2d, extra=()):
            bi = self.mwi % 2
            self.mwi += 1
            t = c.dma("sync", self.MWB[bi][:, :], src2d,
                      waits=self.mw_cons[bi] + list(extra), sem=f"smw{bi}")
            self.mw_cons[bi] = []
            self.mw_load[bi] = (f"smw{bi}", t)
            return bi

        def load4(dram):
            bs = [rw_manual(dram[0:128, :]), rw_manual(dram[128:256, :]),
                  mw_manual(dram[256:384, :]), mw_manual(dram[384:512, :])]
            bufs = [self.RW[bs[0]], self.RW[bs[1]],
                    self.MWB[bs[2]], self.MWB[bs[3]]]
            lds = [self.rw_load[bs[0]], self.rw_load[bs[1]],
                   self.mw_load[bs[2]], self.mw_load[bs[3]]]

            def done(t):
                self.rw_cons[bs[0]].append(t)
                self.rw_cons[bs[1]].append(t)
                self.mw_cons[bs[2]].append(t)
                self.mw_cons[bs[3]].append(t)
            return bufs, lds, done

        for b in (0, 1):
            ready = self.preln_block(xsrc, b)
            l1bst_sb = self.XLN[0:32, :]
            l1bst_w = ("sbs", c.dma("sync", l1bst_sb, l1bst,
                                    waits=[("pe", c.cnt["pe"])] +
                                    self.xln_readers, sem="sbs"))
            XT0B = self.XT[0][:, :].bitcast(BF16)
            XT1B = self.XT[1][:, :].bitcast(BF16)
            B1R = XT1B[0:1, 1024:5248]
            G512 = XT0B[:, 0:128]
            AET = XT0B[:, 128:1152]
            AXT = XT0B[0:32, 1152:1664]
            AXST = XT0B[0:32, 1664:2176]
            BET = XT0B[0:32, 2176:6272]
            ASTS = XT1B[:, 0:1024]
            xtw0 = self.xt_free[0]
            self.xt_free[0] = []
            xtw1 = self.xt_free[1]
            self.xt_free[1] = []
            for kc in range(4):
                c.dma("sync", self.STG[0:4, kc:kc + 1],
                      gates[b:b + 1, 4 * kc:4 * kc + 4]
                      .rearrange("a b -> b a"), waits=stg_free, sem="sga")
            tgw = ("sga", c.cnt["sga"])
            GB = self.STG[0:4, 8:10].bitcast(BF16)[:, 0:4]
            c.emit("vector", lambda q, GB=GB: q.tensor_copy(
                out=GB, in_=self.STG[0:4, 0:4]),
                waits=[tgw] + stg_free, inc="dve")
            tcvf = self.dve_fence()
            w2 = bk.write_waits(2)
            tex = c.emit("tensor", lambda q, GB=GB: q.matmul(
                bk.ap(2)[:, 0:4], e4128[:, :], GB, start=True, stop=True),
                waits=w2 + [("dve", tcvf), e4128_w], inc="pe")
            stg_free = [("pe", tex), ("dve", tcvf)]
            tpf = self.pe_fence()
            tge = c.emit("vector", lambda q: q.tensor_copy(
                out=self.SCR[:, 0:4], in_=bk.ap(2)[:, 0:4]),
                waits=[("pe", tpf)], inc="dve")
            bk.read_done(2, ("dve", tge))
            tgc = self.dve_fence()
            for kc in range(4):
                c.emit("vector", lambda q, kc=kc, G512=G512: q.tensor_scalar(
                    out=G512[:, 32 * kc:32 * kc + 32], in0=i32s[:, :],
                    scalar1=self.SCR[:, kc:kc + 1], scalar2=0.0,
                    op0=ALU.mult, op1=ALU.add),
                    waits=([("dve", tgc), i32s_w] + xtw0) if kc == 0 else [],
                    inc="dve")
            tg5 = self.dve_fence()
            tas = ("sls", c.dma("sync", ASTS[:, :], l1astp, waits=xtw1,
                                sem="sls"))
            tb1 = ("sls", c.dma("sync", B1R, self.din("BIAS1R"),
                                waits=[tas], sem="sls"))
            bufs, lds, done4 = load4(l1a)
            tlast = None
            for ic in range(32):
                bank = 4 + ic // 16
                for kc in range(4):
                    w = []
                    if kc == 0 and (ic % 16) == 0:
                        w += bk.write_waits(bank) + [("dve", tg5)]
                    if ic == 0:
                        w = w + [lds[kc]]
                    last = (ic == 31 and kc == 3)
                    t = c.emit("tensor", lambda q, ic=ic, kc=kc,
                               bank=bank, buf=bufs[kc]: q.matmul(
                        bk.ap(bank)[:, 32 * (ic % 16):32 * (ic % 16) + 32],
                        buf[:, 128 * ic:128 * ic + 128],
                        G512[:, 32 * kc:32 * kc + 32],
                        start=(kc == 0), stop=(kc == 3)),
                        waits=w, inc="pe" if last else None)
                    if last:
                        tlast = t
            done4(("pe", tlast))
            tpfA = self.pe_fence()
            tae0 = c.emit("vector", lambda q, AET=AET: q.tensor_copy(
                out=AET[:, 0:512], in_=bk.ap(4)[:, :]),
                waits=[("pe", tpfA)] + xtw0, inc="dve")
            tae1 = c.emit("vector", lambda q, AET=AET: q.tensor_copy(
                out=AET[:, 512:1024], in_=bk.ap(5)[:, :]), inc="dve")
            bk.read_done(4, ("dve", tae0))
            bk.read_done(5, ("dve", tae1))
            taet = self.dve_fence()
            w2 = bk.write_waits(2)
            w3 = bk.write_waits(3)
            t3_close = None
            for j in range(NJ):
                for cc in range(NT):
                    w = []
                    if cc == 0:
                        w = list(ready[j]) + (
                            (w2 + [("dve", taet)]) if j == 0 else [])
                    c.emit("tensor", lambda q, j=j, cc=cc, AET=AET: q.matmul(
                        bk.ap(2)[0:32, 128 * j:128 * j + 128],
                        AET[:, 32 * cc:32 * cc + 32],
                        self.XLT[:, 512 * cc + 128 * j:
                                 512 * cc + 128 * j + 128],
                        start=(cc == 0), stop=(cc == NT - 1)),
                        waits=w, inc=None)
                for cc in range(NT):
                    w = []
                    if cc == 0:
                        w = (w3 + [tas]) if j == 0 else []
                    last = (j == NJ - 1 and cc == NT - 1)
                    t = c.emit("tensor", lambda q, j=j, cc=cc, ASTS=ASTS:
                               q.matmul(
                        bk.ap(3)[0:32, 128 * j:128 * j + 128],
                        ASTS[:, 32 * cc:32 * cc + 32],
                        self.XLT[:, 512 * cc + 128 * j:
                                 512 * cc + 128 * j + 128],
                        start=(cc == 0), stop=(cc == NT - 1)),
                        waits=w, inc="pe" if last else None)
                    if last:
                        t3_close = t
            tpx = self.pe_fence()
            tx1 = c.emit("vector", lambda q, AXT=AXT: q.tensor_copy(
                out=AXT, in_=bk.ap(2)[0:32, 0:512]),
                waits=[("pe", tpx)], inc="dve")
            tx2 = c.emit("vector", lambda q, AXST=AXST: q.tensor_copy(
                out=AXST, in_=bk.ap(3)[0:32, 0:512]), inc="dve")
            bk.read_done(2, ("dve", tx1))
            bk.read_done(3, ("dve", tx2))
            tax = self.dve_fence()
            bufs, lds, done4 = load4(l1bt)
            close = {}
            tevs = []
            for oc in range(9):
                if oc < 8:
                    bank = 4 + (oc % 2)
                    for kc in range(4):
                        w = []
                        if kc == 0:
                            w += bk.write_waits(bank)
                            if oc == 0:
                                w.append(("dve", tg5))
                        if oc == 0:
                            w.append(lds[kc])
                        t = c.emit("tensor", lambda q, oc=oc, kc=kc,
                                   bank=bank, buf=bufs[kc], G512=G512:
                                   q.matmul(
                            bk.ap(bank)[0:32, :],
                            G512[:, 32 * kc:32 * kc + 32],
                            buf[:, 512 * oc:512 * oc + 512],
                            start=(kc == 0), stop=(kc == 3)),
                            waits=w, inc="pe" if kc == 3 else None)
                    close[oc] = t
                if oc >= 1:
                    po = oc - 1
                    bank = 4 + (po % 2)
                    guard = close[oc] if oc < 8 else self.pe_fence()
                    tev = c.emit("vector", lambda q, po=po, bank=bank,
                                 BET=BET: q.tensor_copy(
                        out=BET[:, 512 * po:512 * po + 512],
                        in_=bk.ap(bank)[0:32, :]),
                        waits=[("pe", guard)] + (xtw0 if po == 0 else []),
                        inc="dve")
                    bk.read_done(bank, ("dve", tev))
            done4(("pe", close[7]))
            tbet = self.dve_fence()

            def extras(og, j, tax=tax, tbet=tbet, AXT=AXT, AXST=AXST,
                       BET=BET):
                return [
                    (lambda q, ps, stop, og=og, j=j: q.matmul(
                        ps, AXT[:, 128 * j:128 * j + 128],
                        BET[:, 512 * og:512 * og + 512],
                        start=False, stop=False),
                     [("dve", tax), ("dve", tbet)]),
                    (lambda q, ps, stop, og=og, j=j: q.matmul(
                        ps, AXST[:, 128 * j:128 * j + 128],
                        l1bst_sb[0:32, 512 * og:512 * og + 512],
                        start=False, stop=stop),
                     [l1bst_w]),
                ]
            evac = self.proj_tm(self.XLT, NT, kert, ready, extras=extras,
                                bias_row=(B1R[:, 0:128], B1R[:, 128:],
                                          tb1))
            self.xt_free[0] = [("pe", c.cnt["pe"]), ("dve", c.cnt["dve"])]
            self.xt_free[1] = [("pe", c.cnt["pe"])]
            self.xln_readers.append(("pe", c.cnt["pe"]))
            self.epilogue_block(b, xsrc, xdst, evac, final=final)



ARM_KEYS = {
    "l0_attn": ["IDENT", "CBIAS"],
    "l0_ffn": ["IDENT", "IDENT32", "ROWP", "E432"],
    "qkv": ["IDENT"],
    "conv": ["IDENT", "CBIAS"],
    "attn": ["IDENT"],
    "wo": ["IDENT"],
    "l1_ffn": ["IDENT", "ROWP", "I32S", "E4128"],
}


def _build_arm(shared, percore0, arm):
    nc = bass.Bass("TRN2", target_bir_lowering=False, debug=False)
    K = Kern(nc, shared, percore0, fused=False)
    xsrc = K.din("XIN", [TPC, D], F32)
    xout = K.dout("XOUT", [TPC, D], F32)
    K.preamble(keys=ARM_KEYS.get(arm))
    getattr(K, "arm_" + arm)(xsrc, xout)
    K.c.finalize()
    K.c.es.close()
    return nc, K


def _run_arm(shared, percore, arm, xin_list):
    nc, K = _build_arm(shared, percore[0], arm)
    in_maps = []
    for ci in range(NC):
        m = {}
        for key in K.in_keys:
            if key == "XIN":
                m["i_xin"] = np.ascontiguousarray(xin_list[ci])
            elif key in percore[ci]:
                m["i_" + key.lower()] = percore[ci][key]
            else:
                m["i_" + key.lower()] = shared[key]
        in_maps.append(m)
    res = run_bass_kernel_spmd(nc, in_maps, core_ids=list(range(NC)))
    return [r["o_xout"] for r in res.results]



def _run_phase(shared, percore, build_fn, extra_fn, out_keys):
    nc = bass.Bass("TRN2", target_bir_lowering=False, debug=False)
    K = Kern(nc, shared, percore[0], fused=False)
    build_fn(K)
    K.c.finalize()
    K.c.es.close()
    in_maps = []
    for ci in range(NC):
        m = {}
        extra = extra_fn(ci)
        for key in K.in_keys:
            if key in extra:
                m["i_" + key.lower()] = np.ascontiguousarray(extra[key])
            elif key in percore[ci]:
                m["i_" + key.lower()] = percore[ci][key]
            else:
                m["i_" + key.lower()] = shared[key]
        in_maps.append(m)
    res = run_bass_kernel_spmd(nc, in_maps, core_ids=list(range(NC)))
    return [[np.asarray(r[k]) for k in out_keys] for r in res.results]


def _owner(g):
    return (g, 0) if g < 8 else (15 - g, 1)


def kernel_unfused(**inputs):
    inp = {k: np.asarray(v) for k, v in inputs.items()}
    shared, percore = _prep(inp)
    x = [percore[c]["X0"] for c in range(NC)]

    x = _run_arm(shared, percore, "l0_attn", x)
    x = _run_arm(shared, percore, "l0_ffn", x)

    def b_qkv(K):
        xsrc = K.din("XIN", [TPC, D], F32)
        qkvd = K.dout("QKV", [1536, 1024], BF16)
        K.preamble(keys=ARM_KEYS["qkv"])
        for b in (0, 1):
            K.qkv_block(xsrc, b, qkvd)
    qkv = _run_phase(shared, percore, b_qkv,
                     lambda ci: {"XIN": x[ci]}, ["o_qkv"])
    qkv = [r[0] for r in qkv]

    halos = []
    for ci in range(NC):
        ga, gb = _blocks_of_core(ci)
        h = np.zeros((1536, 8), qkv[0].dtype)
        for mslot, g in ((0, ga), (4, gb)):
            if g == 0:
                continue
            c2, bi2 = _owner(g - 1)
            h[:, mslot:mslot + 4] = qkv[c2][:, 512 * bi2 + 508:512 * bi2 + 512]
        halos.append(h)

    def b_conv(K):
        qkvd = K.din("QKVI", [1536, 1024], BF16)
        halod = K.din("HALO", [1536, 8], BF16)
        qlocd = K.dout("QLOC", [1024, 1024], BF16)
        kvind = K.dout("KVIN", [512, 1024], BF16)
        K.preamble(keys=ARM_KEYS["conv"])
        K.load_conv_consts()
        for b in (0, 1):
            K.conv_block(b, qkvd, halod, None, qloc=qlocd, kvin=kvind)
    cv = _run_phase(shared, percore, b_conv,
                    lambda ci: {"QKVI": qkv[ci], "HALO": halos[ci]},
                    ["o_qloc", "o_kvin"])
    qloc = [r[0] for r in cv]
    kvg = np.concatenate([r[1] for r in cv])

    def b_attn(K):
        qlocd = K.din("QLOC", [1024, 1024], BF16)
        kvgd = K.din("KVG", [4096, 1024], BF16)
        attd = K.dout("ATTD", [1024, 1024], BF16)
        K.preamble(keys=ARM_KEYS["attn"])
        K.attention2(qlocd, kvgd, attd)
    at = _run_phase(shared, percore, b_attn,
                    lambda ci: {"QLOC": qloc[ci], "KVG": kvg}, ["o_attd"])

    for ci in range(NC):
        percore[ci]["ATTD"] = at[ci][0]

    x = _run_arm(shared, percore, "wo", x)
    x = _run_arm(shared, percore, "l1_ffn", x)

    out = np.zeros((1, S, D), np.float32)
    for ci in range(NC):
        ga, gb = _blocks_of_core(ci)
        out[0, 512 * ga:512 * ga + 512] = x[ci][0:512]
        out[0, 512 * gb:512 * gb + 512] = x[ci][512:1024]
    return out


def _build_fused(shared, percore0, ngroup=NC):
    nc = bass.Bass("TRN2", target_bir_lowering=False, debug=False)
    K = Kern(nc, shared, percore0, fused=True)
    K.cc_group = list(range(ngroup))
    xin = K.din("XIN", [TPC, D], F32)
    xout = K.dout("XOUT", [TPC, D], F32)
    x1 = K.dint("X1", [TPC, D], F32)
    x2 = K.dint("X2", [TPC, D], F32)
    x3 = K.dint("X3", [TPC, D], F32)
    qkvd = K.dint("QKVB", [1536, 1024], BF16)
    agin = K.dint("AGIN", [8, 1536], BF16)
    asp = "Shared" if ngroup == NC else "Local"
    agout = K.dint("AGOUT", [64, 1536], BF16, addr_space=asp)
    halod = K.dint("HALOB", [1536, 8], BF16)
    qlocd = K.dint("QLOCB", [1024, 1024], BF16)
    kvind = K.dint("KVINB", [512, 1024], BF16)
    kvgd = K.dint("KVGB", [4096, 1024], BF16, addr_space=asp)
    attdd = K.dint("ATTDB", [1024, 1024], BF16)
    K.c.scope = "pre"
    K.preamble()
    K.c.scope = "l0_attn"
    K.arm_l0_attn(xin, x1)
    K.join_x()
    K.c.scope = "l0_ffn"
    K.arm_l0_ffn(x1, x2)
    K.join_x()
    K.c.scope = "qkv"
    for b in (0, 1):
        K.qkv_block(x2, b, qkvd)
    K.c.scope = "halo"
    K.halo_exchange(qkvd, agin, agout, halod)
    K.load_conv_consts()
    K.c.scope = "conv"
    for b in (0, 1):
        K.conv_block(b, qkvd, halod, None, qloc=qlocd, kvin=kvind)
    K.c.scope = "attn"
    kvw = K.kv_allgather(kvind, kvgd)
    sa3w = K.attention2(qlocd, kvgd, attdd, extra_w=[kvw])
    K.join_x()
    K.c.scope = "wo"
    K.arm_wo(x2, x3, att_dram=attdd, extra_w=sa3w)
    K.join_x()
    K.c.scope = "l1_ffn"
    K.arm_l1_ffn(x3, xout, final=True)
    K.c.finalize()
    K.c.es.close()
    return nc, K


def kernel(**inputs):
    inp = {k: np.asarray(v) for k, v in inputs.items()}
    shared, percore = _prep(inp)
    nc, K = _build_fused(shared, percore[0])
    in_maps = []
    for ci in range(NC):
        m = {}
        for key in K.in_keys:
            if key == "XIN":
                m["i_xin"] = np.ascontiguousarray(percore[ci]["X0"])
            elif key in percore[ci]:
                m["i_" + key.lower()] = percore[ci][key]
            else:
                m["i_" + key.lower()] = shared[key]
        in_maps.append(m)
    res = run_bass_kernel_spmd(nc, in_maps, core_ids=list(range(NC)))
    out = np.zeros((1, S, D), np.float32)
    for ci in range(NC):
        ga, gb = _blocks_of_core(ci)
        x = res.results[ci]["o_xout"]
        out[0, 512 * ga:512 * ga + 512] = x[0:512]
        out[0, 512 * gb:512 * gb + 512] = x[512:1024]
    return out
